# version 03
# baseline (speedup 1.0000x reference)
import numpy as np
from contextlib import ExitStack

import concourse.bass as bass
import concourse.mybir as mybir
from concourse.bass_utils import run_bass_kernel_spmd

F32 = mybir.dt.float32
BF16 = mybir.dt.bfloat16
AF = mybir.ActivationFunctionType
ALU = mybir.AluOpType
AX = mybir.AxisListType

MAGIC = 12582912.0
SEM_LIM = 24000

S = 2048
NT = S // 128
D = 1024
DCH = D // 128

COMPUTE = ("scalar", "vector", "tensor", "gpsimd")


class Prog:

    def __init__(self, name="prog"):
        self.nc = bass.Bass("TRN2", target_bir_lowering=False)
        self._stack = ExitStack()
        self._stack.__enter__()
        self.cnt = 0
        self.items = []
        self._last = None
        self._uid = 0
        self._consts = {}
        self._const_tile = self.sbuf("constpool", [128, 32], F32)

    def uid(self):
        self._uid += 1
        return self._uid

    def sbuf(self, name, shape, dtype, scope=None):
        st = scope if scope is not None else self._stack
        return st.enter_context(self.nc.sbuf_tensor(f"{name}_{self.uid()}", shape, dtype))

    def psum(self, name, shape, dtype=F32, scope=None):
        st = scope if scope is not None else self._stack
        return st.enter_context(self.nc.psum_tensor(f"{name}_{self.uid()}", shape, dtype))

    def scope(self):
        return ExitStack()

    def semaphore(self, name):
        return self._stack.enter_context(self.nc.semaphore(f"{name}_{self.uid()}"))

    def dram(self, name, shape, dtype, kind="Internal"):
        return self.nc.dram_tensor(name, shape, dtype, kind=kind)

    def constv(self, v):
        v = float(v)
        if v in (0.0, 1.0):
            return v
        if v not in self._consts:
            j = len(self._consts)
            assert j < 32
            self._consts[v] = j
            col = self._const_tile[:, j : j + 1]
            self.memset(col, v)
            self.nc.const_aps.register(F32, v, col)
        return v

    def op(self, eng, emit, inc=1, is_async=False, barrier=False):
        chain = (
            not barrier
            and self._last is not None
            and self._last[0] == eng
            and not self._last[1]
            and eng == "tensor"
        )
        wait = None if chain else self.cnt
        self.items.append((eng, wait, emit, inc))
        self.cnt += inc
        self._last = (eng, is_async)
        return self.cnt

    def dma(self, out, in_, eng="sync", **kw):
        return self.op(eng, lambda e, o=out, i=in_, k=kw: e.dma_start(o, i, **k), 16, is_async=True)

    def dmat(self, out, in_, eng="scalar"):
        return self.op(eng, lambda e, o=out, i=in_: e.dma_start_transpose(o, i), 16, is_async=True)

    def mm(self, out, lhsT, rhs, start=True, stop=True):
        return self.op(
            "tensor",
            lambda e, o=out, l=lhsT, r=rhs, s0=start, s1=stop: e.matmul(o, l, r, start=s0, stop=s1),
        )

    def transpose(self, out, in_, ident):
        return self.op("tensor", lambda e, o=out, i=in_, d=ident: e.transpose(o, i, d))

    def act(self, out, in_, func, bias=0.0, scale=1.0, accum_out=None, eng="scalar"):
        return self.op(
            eng,
            lambda e, o=out, i=in_, f=func, b=bias, s=scale, a=accum_out: e.activation(
                o, i, f, bias=b, scale=s, accum_out=a
            ),
        )

    def tt(self, out, a, b, alu, eng="vector"):
        return self.op(eng, lambda e, o=out, x=a, y=b, p=alu: e.tensor_tensor(o, x, y, p))

    def ts(self, out, in_, s1, s2, op0, op1=ALU.bypass, accum_out=None, eng="vector"):
        return self.op(
            eng,
            lambda e, o=out, i=in_, a=s1, b=s2, p0=op0, p1=op1, ac=accum_out: e.tensor_scalar(
                o, i, a, b, p0, p1, accum_out=ac
            ),
        )

    def tss(self, out, in_, s, alu, eng="vector"):
        return self.op(eng, lambda e, o=out, i=in_, a=s, p=alu: e.tensor_single_scalar(o, i, a, p))

    def stt(self, out, in0, scalar, in1, op0, op1, eng="vector"):
        return self.op(
            eng,
            lambda e, o=out, i0=in0, sc=scalar, i1=in1, p0=op0, p1=op1: e.scalar_tensor_tensor(
                o, i0, sc, i1, p0, p1
            ),
        )

    def red(self, out, in_, alu, absval=False, eng="vector"):
        return self.op(
            eng,
            lambda e, o=out, i=in_, p=alu, a=absval: e.tensor_reduce(
                o, i, AX.X, p, apply_absolute_value=a
            ),
        )

    def scan(self, out, d0, d1, init, op0, op1, eng="vector"):
        return self.op(
            eng,
            lambda e, o=out, a=d0, b=d1, ini=init, p0=op0, p1=op1: e.tensor_tensor_scan(
                o, a, b, ini, p0, p1
            ),
        )

    def copy(self, out, in_, eng="vector"):
        return self.op(eng, lambda e, o=out, i=in_: e.tensor_copy(o, i))

    def memset(self, ap, v, eng="vector"):
        return self.op(eng, lambda e, o=ap, x=v: e.memset(o, x))

    def recip(self, out, in_):
        return self.op("vector", lambda e, o=out, i=in_: e.reciprocal(o, i))

    def iota(self, out, pattern, base=0, channel_multiplier=0):
        return self.op(
            "gpsimd",
            lambda e, o=out, p=pattern, b=base, c=channel_multiplier: e.iota(
                o, p, base=b, channel_multiplier=c, allow_small_or_imprecise_dtypes=True
            ),
        )

    def affsel(self, out, in_, pattern, compare_op, fill, base=0, channel_multiplier=1):
        return self.op(
            "gpsimd",
            lambda e, o=out, i=in_, p=pattern, c=compare_op, f=fill, b=base, m=channel_multiplier: e.affine_select(
                out=o, in_=i, pattern=p, compare_op=c, fill=f, base=b, channel_multiplier=m
            ),
        )

    def collective(self, kind, alu, groups, ins, outs):
        return self.op(
            "gpsimd",
            lambda e, k=kind, o=alu, g=groups, i=ins, u=outs: e.collective_compute(
                k, o, replica_groups=g, ins=i, outs=u
            ),
            1,
            is_async=True,
        )

    def finish(self):
        run = 0
        assign = []
        tot = []
        after_map = {0: (0, 0)}
        for eng, wait, emit, inc in self.items:
            after = run + inc
            k = (after - 1) // SEM_LIM
            while len(tot) <= k:
                tot.append(0)
            tot[k] += inc
            assign.append((eng, wait, emit, inc, k, tot[k]))
            after_map[after] = (k, tot[k])
            run = after

        sems = [self.semaphore("gs") for _ in range(len(tot))]
        by_eng = {}
        for eng, wait, emit, inc, k, valk in assign:
            by_eng.setdefault(eng, []).append((wait, emit, inc, k, valk))

        block = self._stack.enter_context(self.nc.Block())
        amap = after_map
        totals = tot

        def emit_engine(engobj, lst):
            for wait, emit, inc, k, valk in lst:
                if wait is not None and wait > 0:
                    wk, wv = amap[wait]
                    for j in range(wk):
                        engobj.wait_ge(sems[j], totals[j])
                    engobj.wait_ge(sems[wk], wv)
                inst = emit(engobj)
                inst.then_inc(sems[k], inc)

        def final_wait(engobj):
            for j in range(len(totals)):
                engobj.wait_ge(sems[j], totals[j])

        lists = {e: by_eng.get(e, []) for e in ("sync", "scalar", "vector", "tensor", "gpsimd")}

        @block.sync
        def _(e):
            emit_engine(e, lists["sync"])
            final_wait(e)

        @block.scalar
        def _(e):
            emit_engine(e, lists["scalar"])

        @block.vector
        def _(e):
            emit_engine(e, lists["vector"])

        @block.tensor
        def _(e):
            emit_engine(e, lists["tensor"])

        @block.gpsimd
        def _(e):
            emit_engine(e, lists["gpsimd"])

        self._stack.close()
        return self.nc




def emit_plain_ln(p, x_tile, out_tile, scratch, eps=1e-5, out_is_bf16=False):
    st = scratch["st"]
    mu, ms, var, rstd = st[:, 0:1], st[:, 1:2], st[:, 2:3], st[:, 3:4]
    n = x_tile.shape[1]
    p.memset(st[:, 0:4], 0.0)
    p.act(scratch["sq"][:, 0:n], x_tile, AF.Square, accum_out=ms)
    p.red(mu, x_tile, ALU.add)
    p.ts(mu, mu, 1.0 / n, None, ALU.mult)
    p.ts(ms, ms, 1.0 / n, None, ALU.mult)
    p.act(var, mu, AF.Square)
    p.tt(var, ms, var, ALU.subtract)
    p.act(var, var, AF.Sqrt, bias=p.constv(eps))
    p.recip(rstd, var)
    p.ts(out_tile, x_tile, mu, rstd, ALU.subtract, ALU.mult)


def emit_fp4_quant(p, x_tile, out_tile, scratch, ncol):
    nb = ncol // 16
    a, b2 = scratch["a"][:, 0:ncol], scratch["b"][:, 0:ncol]
    bm, br = scratch["bm"][:, 0:nb], scratch["br"][:, 0:nb]
    x3 = x_tile.rearrange("p (a b) -> p a b", b=16)
    p.red(bm, x3, ALU.max, absval=True)
    p.ts(bm, bm, 1.0 / 12.0, 5e-13, ALU.mult, ALU.max)
    p.recip(br, bm)
    p.act(a, x_tile, AF.Abs)
    a3 = a.rearrange("p (a b) -> p a b", b=16)
    p.tt(a3, a3, br.unsqueeze(2).broadcast_to([128, nb, 16]), ALU.mult)
    c1, c2 = b2, scratch["c2"][:, 0:ncol]
    p.tss(c1, a, 0.5, ALU.is_gt)
    p.stt(c1, a, 1.5, c1, ALU.is_gt, ALU.add)
    p.stt(c1, a, 2.5, c1, ALU.is_gt, ALU.add)
    p.stt(c1, a, 3.5, c1, ALU.is_gt, ALU.add)
    p.tss(c2, a, 5.0, ALU.is_gt)
    p.stt(c2, a, 7.0, c2, ALU.is_gt, ALU.add)
    p.stt(c2, a, 10.0, c2, ALU.is_gt, ALU.add)
    p.stt(c2, a, 10.0, c2, ALU.is_gt, ALU.add)
    p.stt(c1, c2, 2.0, c1, ALU.mult, ALU.add)
    p.op("scalar", lambda e, o=a, i=x_tile: e.sign(o, i))
    p.tt(c1, c1, a, ALU.mult)
    c13 = c1.rearrange("p (a b) -> p a b", b=16)
    o3 = out_tile.rearrange("p (a b) -> p a b", b=16)
    p.tt(o3, c13, bm.unsqueeze(2).broadcast_to([128, nb, 16]), ALU.mult)



I_FF = 2816
NI = I_FF // 128


def emit_l1_ffn(p, X, wsl_d, wdsl_d, tag="f1"):
    qx_dr = p.dram(f"{tag}_qx", [S, D], BF16)
    qh_dr = p.dram(f"{tag}_qh", [S, I_FF], BF16)
    wq_dr = p.dram(f"{tag}_wq", [8 * 2 * I_FF, 128], BF16)
    wqin_dr = p.dram(f"{tag}_wqi", [2 * I_FF, 128], BF16)
    wdq_dr = p.dram(f"{tag}_wdq", [8 * 128, I_FF], BF16)
    wdqin_dr = p.dram(f"{tag}_wdqi", [128, I_FF], BF16)
    DBG.update(qx=qx_dr, qh=qh_dr, wq=wq_dr, wdq=wdq_dr)

    with p.scope() as outer:
        scr = {
            "sq": p.sbuf("sq", [128, D], F32, scope=outer),
            "st": p.sbuf("st", [128, 8], F32, scope=outer),
            "a": p.sbuf("qa", [128, D], F32, scope=outer),
            "b": p.sbuf("qb", [128, D], F32, scope=outer),
            "c2": p.sbuf("qc", [128, D], F32, scope=outer),
            "bm": p.sbuf("qbm", [128, D // 16], F32, scope=outer),
            "br": p.sbuf("qbr", [128, D // 16], F32, scope=outer),
        }
        emit_l1_ffn_body(p, X, wsl_d, wdsl_d, scr, qx_dr, qh_dr, wq_dr, wqin_dr, wdq_dr, wdqin_dr)


def emit_l1_ffn_body(p, X, wsl_d, wdsl_d, scr, qx_dr, qh_dr, wq_dr, wqin_dr, wdq_dr, wdqin_dr):
    with p.scope() as sc:
        wtile = p.sbuf("wt", [128, 1024], F32, scope=sc)
        wqt = p.sbuf("wqt", [128, 1024], BF16, scope=sc)
        for t in range(2 * NI):
            r0 = 128 * t
            p.dma(wtile[:, 0:128], wsl_d[r0 : r0 + 128, :])
            emit_fp4_quant(p, wtile[:, 0:128], wqt[:, 0:128], scr, 128)
            p.dma(wqin_dr[r0 : r0 + 128, :], wqt[:, 0:128])
        for j in range(4):
            c0 = 704 * j
            p.dma(wtile[:, 0:704], wdsl_d[:, c0 : c0 + 704])
            emit_fp4_quant(p, wtile[:, 0:704], wqt[:, 0:704], scr, 704)
            p.dma(wdqin_dr[:, c0 : c0 + 704], wqt[:, 0:704])
        p.collective("AllGather", ALU.bypass, [list(range(8))], [wqin_dr.ap()], [wq_dr.ap()])
        p.collective("AllGather", ALU.bypass, [list(range(8))], [wdqin_dr.ap()], [wdq_dr.ap()])

        xn = p.sbuf("xn", [128, D], F32, scope=sc)
        qx = p.sbuf("qx", [128, D], BF16, scope=sc)
        for t in range(NT):
            emit_plain_ln(p, X[t][:], xn[:], scr)
            emit_fp4_quant(p, xn[:], qx[:], scr, D)
            p.dma(qx_dr[128 * t : 128 * (t + 1), :], qx[:])

    with p.scope() as sc:
        qxg = p.sbuf("qxg", [128, DCH * 512], BF16, scope=sc)
        wgt = p.sbuf("wgt", [128, 512], BF16, scope=sc)
        wut = p.sbuf("wut", [128, 512], BF16, scope=sc)
        sg = p.sbuf("sg", [128, 512], F32, scope=sc)
        hb = p.sbuf("hb", [128, 512], F32, scope=sc)
        qh = p.sbuf("qh", [128, 512], BF16, scope=sc)
        psg = [p.psum(f"psg{i}", [128, 512], scope=sc) for i in range(4)]
        psu = [p.psum(f"psu{i}", [128, 512], scope=sc) for i in range(4)]
        ICH = [512, 512, 512, 512, 512, 256]
        for g in range(4):
            g0 = 512 * g
            for k in range(DCH):
                p.dmat(qxg[:, 512 * k : 512 * (k + 1)], qx_dr[g0 : g0 + 512, 128 * k : 128 * (k + 1)])
            for ic in range(6):
                i0 = 512 * ic
                iw = ICH[ic]
                for k in range(DCH):
                    base = 2 * I_FF * k
                    p.dmat(wgt[:, 0:iw], wq_dr[base + i0 : base + i0 + iw, :])
                    p.dmat(wut[:, 0:iw], wq_dr[base + I_FF + i0 : base + I_FF + i0 + iw, :])
                    for tt in range(4):
                        lhsT = qxg[:, 512 * k + 128 * tt : 512 * k + 128 * (tt + 1)]
                        p.mm(psg[tt][:, 0:iw], lhsT, wgt[:, 0:iw], start=(k == 0), stop=(k == DCH - 1))
                        p.mm(psu[tt][:, 0:iw], lhsT, wut[:, 0:iw], start=(k == 0), stop=(k == DCH - 1))
                for tt in range(4):
                    p.act(sg[:, 0:iw], psg[tt][:, 0:iw], AF.Silu)
                    p.tt(hb[:, 0:iw], sg[:, 0:iw], psu[tt][:, 0:iw], ALU.mult)
                    emit_fp4_quant(p, hb[:, 0:iw], qh[:, 0:iw], scr, iw)
                    p.dma(qh_dr[g0 + 128 * tt : g0 + 128 * (tt + 1), i0 : i0 + iw], qh[:, 0:iw])

    with p.scope() as sc:
        qhg = p.sbuf("qhg", [128, NI * 512], BF16, scope=sc)
        wdt = p.sbuf("wdt", [128, D], BF16, scope=sc)
        psd = [p.psum(f"psd{i}", [128, 512], scope=sc) for i in range(8)]
        for g in range(4):
            g0 = 512 * g
            for k in range(NI):
                p.dmat(qhg[:, 512 * k : 512 * (k + 1)], qh_dr[g0 : g0 + 512, 128 * k : 128 * (k + 1)])
            for k in range(NI):
                p.dmat(wdt[:, 0:512], wdq_dr[0:512, 128 * k : 128 * (k + 1)])
                p.dmat(wdt[:, 512:1024], wdq_dr[512:1024, 128 * k : 128 * (k + 1)])
                for tt in range(4):
                    lhsT = qhg[:, 512 * k + 128 * tt : 512 * k + 128 * (tt + 1)]
                    for hc in range(2):
                        p.mm(
                            psd[2 * tt + hc][:],
                            lhsT,
                            wdt[:, 512 * hc : 512 * (hc + 1)],
                            start=(k == 0),
                            stop=(k == NI - 1),
                        )
            for tt in range(4):
                t = 4 * g + tt
                for hc in range(2):
                    h0 = 512 * hc
                    p.tt(X[t][:, h0 : h0 + 512], X[t][:, h0 : h0 + 512], psd[2 * tt + hc][:], ALU.add)




def emit_int8_wquant(p, w_tile, out_tile, sc_col, scratch, nrow, ncol):
    a = scratch["a"]
    r = scratch["st"][:, 4:5]
    p.red(sc_col, w_tile, ALU.max, absval=True)
    p.ts(sc_col, sc_col, 1.0 / 127.0, 1e-30, ALU.mult, ALU.max)
    p.recip(r[0:nrow, :], sc_col)
    p.ts(a[0:nrow, 0:ncol], w_tile, r[0:nrow, :], None, ALU.mult)
    p.ts(a[0:nrow, 0:ncol], a[0:nrow, 0:ncol], MAGIC, -MAGIC, ALU.add, ALU.add)
    p.ts(out_tile, a[0:nrow, 0:ncol], 127.0, -128.0, ALU.min, ALU.max)


def emit_partition_gmax(p, mxcol, mx_dr, ar_in, ar_out, row16, tag):
    p.dma(mx_dr.ap().rearrange("a b -> b a"), mxcol)
    p.memset(row16[0:1, :], 0.0)
    p.dma(row16[0:1, 8:9], mx_dr[0:1, 0:1])
    pass


def emit_l0_ffn(p, X, wisl_d, wdsl_d, clipx_d, clipd_d, tag="g0"):
    qx_dr = p.dram(f"{tag}_qx", [S, D], BF16)
    h_dr = p.dram(f"{tag}_h", [S, I_FF], BF16)
    wq_dr = p.dram(f"{tag}_wq", [2 * I_FF, D], BF16)
    win_dr = p.dram(f"{tag}_wi", [704, D], BF16)
    wdq_dr = p.dram(f"{tag}_wdq", [D, I_FF], BF16)
    wdin_dr = p.dram(f"{tag}_wdi", [128, I_FF], BF16)
    sc_dr = p.dram(f"{tag}_sc", [2 * I_FF, 1], F32)
    scin_dr = p.dram(f"{tag}_sci", [704, 1], F32)
    scd_dr = p.dram(f"{tag}_scd", [D, 1], F32)
    scdin_dr = p.dram(f"{tag}_scdi", [128, 1], F32)
    mx_dr = p.dram(f"{tag}_mx", [1, 128], F32)
    arx_in = p.dram(f"{tag}_axi", [1, 16], F32)
    arx_out = p.dram(f"{tag}_axo", [1, 16], F32)
    arh_in = p.dram(f"{tag}_ahi", [1, 16], F32)
    arh_out = p.dram(f"{tag}_aho", [1, 16], F32)
    scal_dr = p.dram(f"{tag}_scal", [1, 8], F32)
    G8 = [list(range(8))]

    with p.scope() as outer:
        scr = {
            "sq": p.sbuf("sq", [128, D], F32, scope=outer),
            "st": p.sbuf("st", [128, 8], F32, scope=outer),
            "a": p.sbuf("qa", [128, I_FF], F32, scope=outer),
        }
        row16 = p.sbuf("row16", [1, 16], F32, scope=outer)
        scalrow = p.sbuf("scalrow", [1, 8], F32, scope=outer)
        colbc = p.sbuf("colbc", [128, 4], F32, scope=outer)

        with p.scope() as sc:
            wt = p.sbuf("wt", [128, I_FF], F32, scope=sc)
            wq = p.sbuf("wq", [128, I_FF], BF16, scope=sc)
            scol = p.sbuf("scol", [128, 1], F32, scope=sc)
            rows = [128, 128, 128, 128, 128, 64]
            r0 = 0
            for nr in rows:
                p.dma(wt[0:nr, 0:D], wisl_d[r0 : r0 + nr, :])
                emit_int8_wquant(p, wt[0:nr, 0:D], wq[0:nr, 0:D], scol[0:nr, :], scr, nr, D)
                p.dma(win_dr[r0 : r0 + nr, :], wq[0:nr, 0:D])
                p.dma(scin_dr[r0 : r0 + nr, :], scol[0:nr, :])
                r0 += nr
            p.dma(wt[:, 0:I_FF], wdsl_d[:, :])
            emit_int8_wquant(p, wt[:, 0:I_FF], wq[:, 0:I_FF], scol[:], scr, 128, I_FF)
            p.dma(wdin_dr[:, :], wq[:, 0:I_FF])
            p.dma(scdin_dr[:, :], scol[:])
            p.collective("AllGather", ALU.bypass, G8, [win_dr.ap()], [wq_dr.ap()])
            p.collective("AllGather", ALU.bypass, G8, [scin_dr.ap()], [sc_dr.ap()])
            p.collective("AllGather", ALU.bypass, G8, [wdin_dr.ap()], [wdq_dr.ap()])
            p.collective("AllGather", ALU.bypass, G8, [scdin_dr.ap()], [scd_dr.ap()])

        XN = [p.sbuf(f"xn{t}", [128, D], F32, scope=outer) for t in range(NT)]
        mxcol = p.sbuf("mxcol", [128, 1], F32, scope=outer)
        tmpc = p.sbuf("tmpc", [128, 1], F32, scope=outer)
        p.memset(mxcol[:], 0.0)
        for t in range(NT):
            emit_plain_ln(p, X[t][:], XN[t][:], scr)
            p.red(tmpc[:], XN[t][:], ALU.max, absval=True)
            p.tt(mxcol[:], mxcol[:], tmpc[:], ALU.max)
        p.dma(mx_dr.ap().rearrange("a b -> b a"), mxcol[:])
        p.memset(row16[0:1, :], 0.0)
        rowbuf = p.sbuf("rowbuf", [1, 128], F32, scope=outer)
        p.dma(rowbuf[0:1, :], mx_dr[0:1, :])
        p.red(row16[0:1, 0:1], rowbuf[0:1, :], ALU.max)
        p.dma(arx_in[0:1, :], row16[0:1, :])
        p.collective("AllReduce", ALU.max, G8, [arx_in.ap()], [arx_out.ap()])
        p.dma(row16[0:1, 0:16], arx_out[0:1, :])
        sig = scalrow[0:1, 4:5]
        p.dma(sig, clipx_d[0:1, 0:1])
        p.act(sig, sig, AF.Sigmoid)
        sxv, rsxv = scalrow[0:1, 0:1], scalrow[0:1, 1:2]
        p.tt(sxv, sig, row16[0:1, 0:1], ALU.mult)
        p.ts(sxv, sxv, 1.0 / 127.0, None, ALU.mult)
        p.recip(rsxv, sxv)
        p.dma(scal_dr[0:1, 0:2], scalrow[0:1, 0:2])
        p.dma(colbc[:, 0:1], scal_dr[0:1, 1:2].partition_broadcast(128))
        p.dma(colbc[:, 1:2], scal_dr[0:1, 0:1].partition_broadcast(128))

        with p.scope() as sc:
            q32 = p.sbuf("q32", [128, D], F32, scope=sc)
            qx = p.sbuf("qx", [128, D], BF16, scope=sc)
            for t in range(NT):
                p.ts(q32[:], XN[t][:], colbc[:, 0:1], None, ALU.mult)
                p.ts(q32[:], q32[:], MAGIC, -MAGIC, ALU.add, ALU.add)
                p.ts(qx[:], q32[:], 127.0, -128.0, ALU.min, ALU.max)
                p.dma(qx_dr[128 * t : 128 * (t + 1), :], qx[:])

        sgu_bc = p.sbuf("sgubc", [128, 2 * I_FF], F32, scope=outer)
        p.dma(sgu_bc[:], sc_dr.ap().rearrange("a b -> b a").partition_broadcast(128))
        p.ts(sgu_bc[:], sgu_bc[:], colbc[:, 1:2], None, ALU.mult)

        hmx = p.sbuf("hmx", [128, 1], F32, scope=outer)
        p.memset(hmx[:], 0.0)
        with p.scope() as sc:
            qxg = p.sbuf("qxg", [128, DCH * 512], BF16, scope=sc)
            wgt = p.sbuf("wgt", [128, 512], BF16, scope=sc)
            wut = p.sbuf("wut", [128, 512], BF16, scope=sc)
            gs = p.sbuf("gs", [128, 512], F32, scope=sc)
            us = p.sbuf("us", [128, 512], F32, scope=sc)
            hb = p.sbuf("hb", [128, 512], BF16, scope=sc)
            psg = [p.psum(f"psg{i}", [128, 512], scope=sc) for i in range(4)]
            psu = [p.psum(f"psu{i}", [128, 512], scope=sc) for i in range(4)]
            ICH = [512, 512, 512, 512, 512, 256]
            for g in range(4):
                g0 = 512 * g
                for k in range(DCH):
                    p.dmat(qxg[:, 512 * k : 512 * (k + 1)], qx_dr[g0 : g0 + 512, 128 * k : 128 * (k + 1)])
                for ic in range(6):
                    i0 = 512 * ic
                    iw = ICH[ic]
                    for k in range(DCH):
                        p.dmat(wgt[:, 0:iw], wq_dr[i0 : i0 + iw, 128 * k : 128 * (k + 1)])
                        p.dmat(wut[:, 0:iw], wq_dr[I_FF + i0 : I_FF + i0 + iw, 128 * k : 128 * (k + 1)])
                        for tt in range(4):
                            lhsT = qxg[:, 512 * k + 128 * tt : 512 * k + 128 * (tt + 1)]
                            p.mm(psg[tt][:, 0:iw], lhsT, wgt[:, 0:iw], start=(k == 0), stop=(k == DCH - 1))
                            p.mm(psu[tt][:, 0:iw], lhsT, wut[:, 0:iw], start=(k == 0), stop=(k == DCH - 1))
                    for tt in range(4):
                        p.tt(gs[:, 0:iw], psg[tt][:, 0:iw], sgu_bc[:, i0 : i0 + iw], ALU.mult)
                        p.act(gs[:, 0:iw], gs[:, 0:iw], AF.Silu)
                        p.tt(us[:, 0:iw], psu[tt][:, 0:iw], sgu_bc[:, I_FF + i0 : I_FF + i0 + iw], ALU.mult)
                        p.tt(us[:, 0:iw], gs[:, 0:iw], us[:, 0:iw], ALU.mult)
                        p.red(tmpc[:], us[:, 0:iw], ALU.max, absval=True)
                        p.tt(hmx[:], hmx[:], tmpc[:], ALU.max)
                        p.copy(hb[:, 0:iw], us[:, 0:iw])
                        p.dma(h_dr[g0 + 128 * tt : g0 + 128 * (tt + 1), i0 : i0 + iw], hb[:, 0:iw])

        p.dma(mx_dr.ap().rearrange("a b -> b a"), hmx[:])
        p.memset(row16[0:1, :], 0.0)
        p.dma(rowbuf[0:1, :], mx_dr[0:1, :])
        p.red(row16[0:1, 0:1], rowbuf[0:1, :], ALU.max)
        p.dma(arh_in[0:1, :], row16[0:1, :])
        p.collective("AllReduce", ALU.max, G8, [arh_in.ap()], [arh_out.ap()])
        p.dma(row16[0:1, 0:16], arh_out[0:1, :])
        p.dma(sig, clipd_d[0:1, 0:1])
        p.act(sig, sig, AF.Sigmoid)
        shv, rshv = scalrow[0:1, 2:3], scalrow[0:1, 3:4]
        p.tt(shv, sig, row16[0:1, 0:1], ALU.mult)
        p.ts(shv, shv, 1.0 / 127.0, None, ALU.mult)
        p.recip(rshv, shv)
        p.dma(scal_dr[0:1, 2:4], scalrow[0:1, 2:4])
        p.dma(colbc[:, 2:3], scal_dr[0:1, 3:4].partition_broadcast(128))
        p.dma(colbc[:, 3:4], scal_dr[0:1, 2:3].partition_broadcast(128))

        swd_bc = p.sbuf("swdbc", [128, D], F32, scope=outer)
        p.dma(swd_bc[:], scd_dr.ap().rearrange("a b -> b a").partition_broadcast(128))
        p.ts(swd_bc[:], swd_bc[:], colbc[:, 3:4], None, ALU.mult)

        with p.scope() as sc:
            qhg = p.sbuf("qhg", [128, NI * 512], BF16, scope=sc)
            hstg = p.sbuf("hstg", [128, 512], BF16, scope=sc)
            hq32 = p.sbuf("hq32", [128, 512], F32, scope=sc)
            wdt = p.sbuf("wdt", [128, D], BF16, scope=sc)
            psd = [p.psum(f"psd{i}", [128, 512], scope=sc) for i in range(8)]
            for g in range(4):
                g0 = 512 * g
                for k in range(NI):
                    p.dmat(hstg[:], h_dr[g0 : g0 + 512, 128 * k : 128 * (k + 1)])
                    p.ts(hq32[:], hstg[:], colbc[:, 2:3], None, ALU.mult)
                    p.ts(hq32[:], hq32[:], MAGIC, -MAGIC, ALU.add, ALU.add)
                    p.ts(qhg[:, 512 * k : 512 * (k + 1)], hq32[:], 127.0, -128.0, ALU.min, ALU.max)
                for k in range(NI):
                    p.dmat(wdt[:, 0:512], wdq_dr[0:512, 128 * k : 128 * (k + 1)])
                    p.dmat(wdt[:, 512:1024], wdq_dr[512:1024, 128 * k : 128 * (k + 1)])
                    for tt in range(4):
                        lhsT = qhg[:, 512 * k + 128 * tt : 512 * k + 128 * (tt + 1)]
                        for hc in range(2):
                            p.mm(
                                psd[2 * tt + hc][:],
                                lhsT,
                                wdt[:, 512 * hc : 512 * (hc + 1)],
                                start=(k == 0),
                                stop=(k == NI - 1),
                            )
                for tt in range(4):
                    t = 4 * g + tt
                    for hc in range(2):
                        h0 = 512 * hc
                        p.tt(hq32[:], psd[2 * tt + hc][:], swd_bc[:, h0 : h0 + 512], ALU.mult)
                        p.tt(X[t][:, h0 : h0 + 512], X[t][:, h0 : h0 + 512], hq32[:], ALU.add)



NH = 32
HP = 64
DIN = 2048
DSTATE = 128
CONVD = 2304
NCH = 16
DFF = 4096


def emit_affine_ln(p, x_tile, out_tile, scratch, lnw_bc, lnb_bc, u_scratch):
    aff = scratch["aff"][:, 0 : x_tile.shape[1]]
    emit_plain_ln(p, x_tile, u_scratch, scratch)
    emit_plain_ln(p, u_scratch, aff, scratch)
    p.tt(aff, aff, lnw_bc, ALU.mult)
    p.tt(out_tile, aff, lnb_bc, ALU.add)


def emit_l0_attn(
    p, X, inw_d, convw_d, convb_d, dtb_d, alog_d, dskip_d, rmsw_d, outw_d,
    w1_d, b1_d, w2_d, b2_d, lnw_d, lnb_d, xhalo_d, hmask_d, omask_d, tag="m0"
):
    xn_dr = p.dram(f"{tag}_xn", [S, D], BF16)
    z_dr = p.dram(f"{tag}_z", [S, DIN], BF16)
    xss_dr = p.dram(f"{tag}_xss", [DIN, S], BF16)
    caf_dr = p.dram(f"{tag}_caf", [NCH, NH * 128], F32)
    ncl_dr = p.dram(f"{tag}_ncl", [NCH, NH * 128], F32)
    lam_dr = p.dram(f"{tag}_lam", [NH, S], BF16)
    lamtot_dr = p.dram(f"{tag}_lamt", [NH, S], BF16)
    etm_dr = p.dram(f"{tag}_etm", [NH, S], BF16)
    easum_dr = p.dram(f"{tag}_eas", [NCH, NH], F32)
    yraw_dr = p.dram(f"{tag}_yraw", [S, DIN], BF16)
    y4_dr = p.dram(f"{tag}_y4", [S, DIN], BF16)
    sfin_dr = p.dram(f"{tag}_sfin", [128, DIN], F32)
    sag_dr = p.dram(f"{tag}_sag", [256, DIN], F32)
    xn2_dr = p.dram(f"{tag}_xn2", [S, D], BF16)
    h1_dr = p.dram(f"{tag}_h1", [S, DFF], BF16)
    PAIRS = [[0, 1], [2, 3], [4, 5], [6, 7]]

    with p.scope() as outer:
        scr = {
            "sq": p.sbuf("sq", [128, D], F32, scope=outer),
            "st": p.sbuf("st", [128, 8], F32, scope=outer),
            "aff": p.sbuf("aff", [128, D], F32, scope=outer),
        }
        uscr = p.sbuf("uscr", [128, D], F32, scope=outer)
        lnw_bc = p.sbuf("lnwbc", [128, D], F32, scope=outer)
        lnb_bc = p.sbuf("lnbbc", [128, D], F32, scope=outer)
        ident = p.sbuf("ident", [128, 128], F32, scope=outer)
        p.dma(lnw_bc[:], lnw_d[0:1, :].partition_broadcast(128))
        p.dma(lnb_bc[:], lnb_d[0:1, :].partition_broadcast(128))
        p.memset(ident[:], 0.0)
        p.affsel(ident[:], ident[:], [[-1, 128]], ALU.not_equal, 1.0, 0, 1)

        with p.scope() as sc:
            xnb = p.sbuf("xnb", [128, D], BF16, scope=sc)
            for t in range(NT):
                emit_affine_ln(p, X[t][:], xnb[:], scr, lnw_bc[:], lnb_bc[:], uscr[:])
                p.dma(xn_dr[128 * t : 128 * (t + 1), :], xnb[:])
        xh3f = p.sbuf("xh3f", [128, DCH * 4], BF16, scope=outer)
        with p.scope() as sc:
            xh = p.sbuf("xh", [128, D], F32, scope=sc)
            xnh = p.sbuf("xnh", [128, D], F32, scope=sc)
            pt = p.psum("pt", [128, 128], scope=sc)
            p.memset(xh[:], 0.0)
            p.dma(xh[0:3, :], xhalo_d[0:3, :])
            emit_affine_ln(p, xh[:], xnh[:], scr, lnw_bc[:], lnb_bc[:], uscr[:])
            for k in range(DCH):
                p.transpose(pt[:], xnh[:, 128 * k : 128 * (k + 1)], ident[:])
                p.copy(xh3f[:, 4 * k : 4 * k + 3], pt[:, 0:3])

        BM = p.sbuf("Bfm", [128, S], BF16, scope=outer)
        CM = p.sbuf("Cfm", [128, S], BF16, scope=outer)
        dttm = p.sbuf("dttm", [128, NCH * NH], F32, scope=outer)
        haloxbc = p.sbuf("haloxbc", [128, 18 * 3], BF16, scope=outer)
        with p.scope() as sc:
            xnf = p.sbuf("xnf", [128, DCH * S], BF16, scope=sc)
            for k in range(DCH):
                p.dmat(xnf[:, S * k : S * (k + 1)], xn_dr[:, 128 * k : 128 * (k + 1)])
            sc2 = p.scope()
            sc2.__enter__()
            wt = p.sbuf("wt", [128, 512], BF16, scope=sc2)
            zb = p.sbuf("zb", [128, 512], BF16, scope=sc2)
            psz = [p.psum(f"psz{i}", [128, 512], scope=sc2) for i in range(4)]
            for g in range(4):
                for zc in range(4):
                    for k in range(DCH):
                        p.dma(wt[:], inw_d[128 * k : 128 * (k + 1), 512 * zc : 512 * (zc + 1)], eng="gpsimd")
                        for tt in range(4):
                            lhsT = xnf[:, S * k + 512 * g + 128 * tt : S * k + 512 * g + 128 * (tt + 1)]
                            p.mm(psz[tt][:], lhsT, wt[:], start=(k == 0), stop=(k == DCH - 1))
                    for tt in range(4):
                        p.copy(zb[:], psz[tt][:])
                        p.dma(z_dr[512 * g + 128 * tt : 512 * g + 128 * (tt + 1), 512 * zc : 512 * (zc + 1)], zb[:])
            sc2.__exit__(None, None, None)
            sc2 = p.scope()
            sc2.__enter__()
            dtbias_bc = p.sbuf("dtbbc", [128, NH], F32, scope=sc2)
            p.dma(dtbias_bc[:], dtb_d[0:1, :].partition_broadcast(128))
            wdt = p.sbuf("wdt", [128, DCH * NH], BF16, scope=sc2)
            psd = p.psum("psdt", [128, NH], scope=sc2)
            d1 = p.sbuf("d1", [128, NH], F32, scope=sc2)
            d2 = p.sbuf("d2", [128, NH], F32, scope=sc2)
            for k in range(DCH):
                p.dma(wdt[:, NH * k : NH * (k + 1)], inw_d[128 * k : 128 * (k + 1), 4352:4384], eng="gpsimd")
            for tc in range(NT):
                for k in range(DCH):
                    lhsT = xnf[:, S * k + 128 * tc : S * k + 128 * (tc + 1)]
                    p.mm(psd[:], lhsT, wdt[:, NH * k : NH * (k + 1)], start=(k == 0), stop=(k == DCH - 1))
                p.tt(d1[:], psd[:], dtbias_bc[:], ALU.add)
                p.act(d2[:], d1[:], AF.Abs)
                p.act(d2[:], d2[:], AF.Exp, scale=-1.0)
                p.act(d2[:], d2[:], AF.Ln, bias=1.0)
                p.ts(d1[:], d1[:], 0.0, None, ALU.max)
                p.tt(dttm[:, NH * tc : NH * (tc + 1)], d1[:], d2[:], ALU.add)
            sc2.__exit__(None, None, None)
            convw = p.sbuf("convw", [128, 18 * 4], F32, scope=sc)
            convb = p.sbuf("convb", [128, 18], F32, scope=sc)
            p.dma(convw[:].rearrange("p (f k) -> p f k", k=4), convw_d.ap().rearrange("(f p) k -> p f k", p=128))
            p.dma(convb[:].rearrange("p (a f) -> p a f", a=1), convb_d.ap().rearrange("a (f p) -> p a f", p=128), allow_slow_non_contiguous=True)
            hmcol = p.sbuf("hmcol", [128, 1], F32, scope=sc)
            p.dma(hmcol[:], hmask_d[0:1, 0:1].partition_broadcast(128))
            wtile = p.sbuf("wtile", [128, 128], BF16, scope=sc)
            xbch = p.sbuf("xbch", [128, S + 4], F32, scope=sc)
            cacc = p.sbuf("cacc", [128, S], F32, scope=sc)
            sil = p.sbuf("sil", [128, S], BF16, scope=sc)
            psx = [p.psum(f"psx{i}", [128, 512], scope=sc) for i in range(4)]
            psh = p.psum("psh", [128, 16], scope=sc)
            for fc in range(18):
                col0 = 2048 + 128 * fc
                for k in range(DCH):
                    p.dma(wtile[:], inw_d[128 * k : 128 * (k + 1), col0 : col0 + 128], eng="gpsimd")
                    for g in range(4):
                        p.mm(psx[g][:], wtile[:], xnf[:, S * k + 512 * g : S * k + 512 * (g + 1)], start=(k == 0), stop=(k == DCH - 1))
                    p.mm(psh[:, 0:3], wtile[:], xh3f[:, 4 * k : 4 * k + 3], start=(k == 0), stop=(k == DCH - 1))
                p.ts(xbch[:, 1:4], psh[:, 0:3], hmcol[:], None, ALU.mult)
                for g in range(4):
                    p.copy(xbch[:, 4 + 512 * g : 4 + 512 * (g + 1)], psx[g][:])
                p.copy(haloxbc[:, 3 * fc : 3 * fc + 3], xbch[:, 1:4])
                p.ts(cacc[:], xbch[:, 1 : 1 + S], convw[:, 4 * fc : 4 * fc + 1], None, ALU.mult)
                for k in range(1, 4):
                    p.stt(cacc[:], xbch[:, 1 + k : 1 + k + S], convw[:, 4 * fc + k : 4 * fc + k + 1], cacc[:], ALU.mult, ALU.add)
                p.act(sil[:], cacc[:], AF.Silu, bias=convb[:, fc : fc + 1])
                if fc < 16:
                    p.dma(xss_dr[128 * fc : 128 * (fc + 1), :], sil[:])
                elif fc == 16:
                    p.copy(BM[:], sil[:])
                else:
                    p.copy(CM[:], sil[:])

        with p.scope() as sc:
            dtfm = p.sbuf("dtfm", [NH, S], F32, scope=sc)
            pt = p.psum("ptd", [128, 128], scope=sc)
            for tc in range(NT):
                p.transpose(pt[0:NH, 0:128], dttm[:, NH * tc : NH * (tc + 1)], ident[:])
                p.copy(dtfm[0:NH, 128 * tc : 128 * (tc + 1)], pt[0:NH, 0:128])
            acol = p.sbuf("acol", [NH, 1], F32, scope=sc)
            p.dma(acol[0:NH, :], alog_d.ap().rearrange("a b -> b a"), allow_slow_non_contiguous=True)
            p.act(acol[0:NH, :], acol[0:NH, :], AF.Exp)
            p.ts(acol[0:NH, :], acol[0:NH, :], -1.0, None, ALU.mult)
            afm = p.sbuf("afm", [NH, S], F32, scope=sc)
            p.ts(afm[0:NH, :], dtfm[0:NH, :], acol[0:NH, :], None, ALU.mult)
            zrow = p.sbuf("zrow", [NH, 128], F32, scope=sc)
            p.memset(zrow[0:NH, :], 0.0)
            cafm = p.sbuf("cafm", [NH, S], F32, scope=sc)
            for c in range(NCH):
                p.scan(cafm[0:NH, 128 * c : 128 * (c + 1)], afm[0:NH, 128 * c : 128 * (c + 1)], zrow[0:NH, :], 0.0, ALU.add, ALU.add)
            zrow2 = p.sbuf("zrow2", [NH, S], F32, scope=sc)
            p.memset(zrow2[0:NH, :], 0.0)
            catot = p.sbuf("catot", [NH, S], F32, scope=sc)
            p.scan(catot[0:NH, :], afm[0:NH, :], zrow2[0:NH, :], 0.0, ALU.add, ALU.add)
            brow = p.sbuf("brow", [NH, S], BF16, scope=sc)
            p.act(brow[0:NH, :], cafm[0:NH, :], AF.Exp)
            p.dma(lam_dr[0:NH, :], brow[0:NH, :])
            p.act(brow[0:NH, :], catot[0:NH, :], AF.Exp)
            p.dma(lamtot_dr[0:NH, :], brow[0:NH, :])
            asumb = p.sbuf("asumb", [NH, NCH], F32, scope=sc)
            for c in range(NCH):
                p.copy(asumb[0:NH, c : c + 1], cafm[0:NH, 128 * c + 127 : 128 * (c + 1)])
            efm = p.sbuf("efm", [NH, S], F32, scope=sc)
            p.tt(
                efm[0:NH, :].rearrange("h (c t) -> h c t", t=128),
                asumb[0:NH, :].unsqueeze(2).broadcast_to([NH, NCH, 128]),
                cafm[0:NH, :].rearrange("h (c t) -> h c t", t=128),
                ALU.subtract,
            )
            p.act(brow[0:NH, :], efm[0:NH, :], AF.Exp)
            p.dma(etm_dr[0:NH, :], brow[0:NH, :])
            p.act(asumb[0:NH, :], asumb[0:NH, :], AF.Exp)
            p.dma(easum_dr.ap().rearrange("c h -> h c"), asumb[0:NH, :], allow_slow_non_contiguous=True)
            nclfm = p.sbuf("nclfm", [NH, S], F32, scope=sc)
            p.ts(nclfm[0:NH, :], cafm[0:NH, :], -1.0, None, ALU.mult)
            for c in range(NCH):
                p.dma(caf_dr[c : c + 1, :].rearrange("a (h t) -> (a h) t", t=128), cafm[0:NH, 128 * c : 128 * (c + 1)])
                p.dma(ncl_dr[c : c + 1, :].rearrange("a (h t) -> (a h) t", t=128), nclfm[0:NH, 128 * c : 128 * (c + 1)])

        maskDT = p.sbuf("maskDT", [128, 128], F32, scope=outer)
        p.memset(maskDT[:], 0.0)
        p.affsel(maskDT[:], maskDT[:], [[1, 128]], ALU.is_ge, -1e30, 0, -1)
        dskip_bc = p.sbuf("dskipbc", [128, NH], F32, scope=outer)
        p.dma(dskip_bc[:], dskip_d[0:1, :].partition_broadcast(128))
        onesr = p.sbuf("onesr", [1, 128], F32, scope=outer)
        p.memset(onesr[0:1, :], 1.0)
        SST = p.sbuf("SST", [128, DIN], F32, scope=outer)
        p.memset(SST[:], 0.0)
        with p.scope() as sc:
            Sb = p.sbuf("Sb", [128, DIN], BF16, scope=sc)
            Xtm = p.sbuf("Xtm", [128, DIN], BF16, scope=sc)
            Xd = p.sbuf("Xd", [128, DIN], BF16, scope=sc)
            XdE = p.sbuf("XdE", [128, DIN], BF16, scope=sc)
            lamt = p.sbuf("lamt", [128, NH], BF16, scope=sc)
            etmt = p.sbuf("etmt", [128, NH], BF16, scope=sc)
            lambc = p.sbuf("lambc", [128, NH], F32, scope=sc)
            Gt = p.sbuf("Gt", [128, 128], BF16, scope=sc)
            Btm = p.sbuf("Btm", [128, 128], BF16, scope=sc)
            Bf32 = p.sbuf("Bf32", [128, 128], F32, scope=sc)
            carow = p.sbuf("carow", [1, NH * 128], F32, scope=sc)
            nclrow = p.sbuf("nclrow", [1, NH * 128], F32, scope=sc)
            Ws = p.sbuf("Ws", [128, DIN], F32, scope=sc)
            Wp = p.sbuf("Wp", [128, DIN], BF16, scope=sc)
            ybuf = p.sbuf("ybuf", [128, DIN], F32, scope=sc)
            ybuf2 = p.sbuf("ybuf2", [128, DIN], F32, scope=sc)
            yb16 = p.sbuf("yb16", [128, DIN], BF16, scope=sc)
            pd = p.psum("pd", [128, DIN], scope=sc)
            py = p.psum("py", [128, DIN], scope=sc)
            for c in range(NCH):
                t0 = 128 * c
                p.dmat(Xtm[:], xss_dr[:, t0 : t0 + 128])
                p.dmat(lamt[:, 0:NH], lam_dr[0:NH, t0 : t0 + 128])
                p.dmat(etmt[:, 0:NH], etm_dr[0:NH, t0 : t0 + 128])
                p.dma(lambc[:, 0:NH], easum_dr[c : c + 1, :].partition_broadcast(128))
                p.dma(carow[0:1, :], caf_dr[c : c + 1, :])
                p.dma(nclrow[0:1, :], ncl_dr[c : c + 1, :])
                x3 = Xtm[:].rearrange("p (h e) -> p h e", e=HP)
                p.tt(Xd[:].rearrange("p (h e) -> p h e", e=HP), x3, dttm[:, NH * c : NH * (c + 1)].unsqueeze(2).broadcast_to([128, NH, HP]), ALU.mult)
                p.tt(XdE[:].rearrange("p (h e) -> p h e", e=HP), Xd[:].rearrange("p (h e) -> p h e", e=HP), etmt[:, 0:NH].unsqueeze(2).broadcast_to([128, NH, HP]), ALU.mult)
                p.mm(pd[:, 0:128], BM[:, t0 : t0 + 128], CM[:, t0 : t0 + 128])
                p.copy(Gt[:], pd[:, 0:128])
                p.copy(Bf32[:], BM[:, t0 : t0 + 128])
                p.transpose(pd[:, 128:256], Bf32[:], ident[:])
                p.copy(Btm[:], pd[:, 128:256])
                for hh in range(2):
                    o0 = DIN * hh
                    for s4 in range(4):
                        p.mm(pd[:, 512 * s4 : 512 * (s4 + 1)], onesr[0:1, :], carow[0:1, 2048 * hh + 512 * s4 : 2048 * hh + 512 * (s4 + 1)], start=True, stop=False)
                    for j in range(16):
                        h = 16 * hh + j
                        p.mm(pd[:, 128 * j : 128 * (j + 1)], nclrow[0:1, 128 * h : 128 * (h + 1)], onesr[0:1, :], start=False, stop=True)
                    p.tt(
                        Ws[:, o0 : o0 + 2048].rearrange("p (j t) -> p j t", t=128) if False else Ws[:, 0:2048].rearrange("p (j t) -> p j t", t=128),
                        pd[:].rearrange("p (j t) -> p j t", t=128),
                        maskDT[:].unsqueeze(1).broadcast_to([128, 16, 128]),
                        ALU.add,
                    )
                    p.act(Wp[:, 0:2048], Ws[:, 0:2048], AF.Exp)
                    p.tt(
                        Wp[:, 0:2048].rearrange("p (j t) -> p j t", t=128),
                        Wp[:, 0:2048].rearrange("p (j t) -> p j t", t=128),
                        Gt[:].unsqueeze(1).broadcast_to([128, 16, 128]),
                        ALU.mult,
                    )
                    for j in range(16):
                        h = 16 * hh + j
                        p.mm(py[:, HP * h : HP * (h + 1)], Wp[:, 128 * j : 128 * (j + 1)], Xd[:, HP * h : HP * (h + 1)], start=True, stop=True)
                p.copy(Sb[:], SST[:])
                for h in range(NH):
                    p.mm(pd[:, HP * h : HP * (h + 1)], CM[:, t0 : t0 + 128], Sb[:, HP * h : HP * (h + 1)], start=True, stop=True)
                p.tt(ybuf[:].rearrange("p (h e) -> p h e", e=HP), pd[:].rearrange("p (h e) -> p h e", e=HP), lamt[:, 0:NH].unsqueeze(2).broadcast_to([128, NH, HP]), ALU.mult)
                p.tt(ybuf[:], ybuf[:], py[:], ALU.add)
                p.tt(ybuf2[:].rearrange("p (h e) -> p h e", e=HP), x3, dskip_bc[:].unsqueeze(2).broadcast_to([128, NH, HP]), ALU.mult)
                p.tt(yb16[:], ybuf[:], ybuf2[:], ALU.add)
                p.dma(yraw_dr[t0 : t0 + 128, :], yb16[:])
                p.ts(SST[:].rearrange("p (h e) -> p h e", e=HP), SST[:].rearrange("p (h e) -> p h e", e=HP), None, None, ALU.mult) if False else None
                p.tt(SST[:].rearrange("p (h e) -> p h e", e=HP), SST[:].rearrange("p (h e) -> p h e", e=HP), lambc[:, 0:NH].unsqueeze(2).broadcast_to([128, NH, HP]), ALU.mult)
                for h in range(NH):
                    p.mm(py[:, HP * h : HP * (h + 1)], Btm[:], XdE[:, HP * h : HP * (h + 1)], start=True, stop=True)
                p.tt(SST[:], SST[:], py[:], ALU.add)
            p.dma(sfin_dr[:, :], SST[:])
        p.collective("AllGather", ALU.bypass, PAIRS, [sfin_dr.ap()], [sag_dr.ap()])

        rms_bc = p.sbuf("rmsbc", [128, DIN], F32, scope=outer)
        p.dma(rms_bc[:], rmsw_d[0:1, :].partition_broadcast(128))
        with p.scope() as sc:
            omcol = p.sbuf("omcol", [128, 1], F32, scope=sc)
            p.dma(omcol[:], omask_d[0:1, 0:1].partition_broadcast(128))
            Sin = p.sbuf("Sin", [128, DIN], F32, scope=sc)
            Sinb = p.sbuf("Sinb", [128, DIN], BF16, scope=sc)
            p.dma(Sin[:], sag_dr[0:128, :])
            p.ts(Sin[:], Sin[:], omcol[:], None, ALU.mult)
            p.copy(Sinb[:], Sin[:])
            lamtt = p.sbuf("lamtt", [128, NH], BF16, scope=sc)
            yr = p.sbuf("yr", [128, DIN], BF16, scope=sc)
            zt = p.sbuf("zt", [128, DIN], BF16, scope=sc)
            zs = p.sbuf("zs", [128, DIN], F32, scope=sc)
            yv = p.sbuf("yv", [128, DIN], F32, scope=sc)
            yo = p.sbuf("yo", [128, DIN], BF16, scope=sc)
            rcol = p.sbuf("rcol", [128, 1], F32, scope=sc)
            sqs = p.sbuf("sqs", [128, DIN], F32, scope=sc)
            pz = p.psum("pz", [128, DIN], scope=sc)
            for c in range(NCH):
                t0 = 128 * c
                p.dmat(lamtt[:, 0:NH], lamtot_dr[0:NH, t0 : t0 + 128])
                p.dma(yr[:], yraw_dr[t0 : t0 + 128, :])
                p.dma(zt[:], z_dr[t0 : t0 + 128, :], eng="gpsimd")
                for h in range(NH):
                    p.mm(pz[:, HP * h : HP * (h + 1)], CM[:, t0 : t0 + 128], Sinb[:, HP * h : HP * (h + 1)], start=True, stop=True)
                p.tt(yv[:].rearrange("p (h e) -> p h e", e=HP), pz[:].rearrange("p (h e) -> p h e", e=HP), lamtt[:, 0:NH].unsqueeze(2).broadcast_to([128, NH, HP]), ALU.mult)
                p.tt(yv[:], yv[:], yr[:], ALU.add)
                p.act(zs[:], zt[:], AF.Silu)
                p.tt(yv[:], yv[:], zs[:], ALU.mult)
                p.memset(rcol[:], 0.0)
                p.act(sqs[:], yv[:], AF.Square, accum_out=rcol[:])
                p.ts(rcol[:], rcol[:], 1.0 / DIN, None, ALU.mult)
                p.act(rcol[:], rcol[:], AF.Sqrt, bias=p.constv(1e-5))
                p.recip(rcol[:], rcol[:])
                p.ts(yv[:], yv[:], rcol[:], None, ALU.mult)
                p.tt(yo[:], yv[:], rms_bc[:], ALU.mult)
                p.dma(y4_dr[t0 : t0 + 128, :], yo[:])

        out1_dr = p.dram(f"{tag}_out1", [S, D], F32)
        with p.scope() as sc:
            yf = p.sbuf("yf", [128, 16 * 512], BF16, scope=sc)
            wot = p.sbuf("wot", [128, 512], BF16, scope=sc)
            o1t = p.sbuf("o1t", [128, D], F32, scope=sc)
            pso = [p.psum(f"pso{i}", [128, 512], scope=sc) for i in range(8)]
            for g in range(4):
                g0 = 512 * g
                for k in range(16):
                    p.dmat(yf[:, 512 * k : 512 * (k + 1)], y4_dr[g0 : g0 + 512, 128 * k : 128 * (k + 1)])
                for k in range(16):
                    p.dma(wot[:], outw_d[128 * k : 128 * (k + 1), 0:512], eng="gpsimd")
                    for tt in range(4):
                        lhsT = yf[:, 512 * k + 128 * tt : 512 * k + 128 * (tt + 1)]
                        p.mm(pso[2 * tt][:], lhsT, wot[:], start=(k == 0), stop=(k == 15))
                    p.dma(wot[:], outw_d[128 * k : 128 * (k + 1), 512:1024], eng="gpsimd")
                    for tt in range(4):
                        lhsT = yf[:, 512 * k + 128 * tt : 512 * k + 128 * (tt + 1)]
                        p.mm(pso[2 * tt + 1][:], lhsT, wot[:], start=(k == 0), stop=(k == 15))
                for tt in range(4):
                    t = 4 * g + tt
                    emit_plain_ln(p, X[t][:], uscr[:], scr)
                    for hc in range(2):
                        p.tt(o1t[:, 512 * hc : 512 * (hc + 1)], pso[2 * tt + hc][:], uscr[:, 512 * hc : 512 * (hc + 1)], ALU.add)
                    p.dma(out1_dr[128 * t : 128 * (t + 1), :], o1t[:])
                    p.tt(X[t][:], X[t][:], o1t[:], ALU.add)

        with p.scope() as sc:
            xn2b = p.sbuf("xn2b", [128, D], BF16, scope=sc)
            o1l = p.sbuf("o1l", [128, D], F32, scope=sc)
            for t in range(NT):
                p.dma(o1l[:], out1_dr[128 * t : 128 * (t + 1), :])
                emit_affine_ln(p, o1l[:], xn2b[:], scr, lnw_bc[:], lnb_bc[:], uscr[:])
                p.dma(xn2_dr[128 * t : 128 * (t + 1), :], xn2b[:])
        with p.scope() as sc:
            xnf = p.sbuf("xnf2", [128, DCH * 512], BF16, scope=sc)
            wt = p.sbuf("wt2", [128, 512], BF16, scope=sc)
            b1bc = p.sbuf("b1bc", [128, DFF], F32, scope=sc)
            p.dma(b1bc[:], b1_d[0:1, :].partition_broadcast(128))
            hb = p.sbuf("hb1", [128, 512], F32, scope=sc)
            hbb = p.sbuf("hbb1", [128, 512], BF16, scope=sc)
            ps1 = [p.psum(f"ps1{i}", [128, 512], scope=sc) for i in range(4)]
            for g in range(4):
                g0 = 512 * g
                for k in range(DCH):
                    p.dmat(xnf[:, 512 * k : 512 * (k + 1)], xn2_dr[g0 : g0 + 512, 128 * k : 128 * (k + 1)])
                for jc in range(8):
                    j0 = 512 * jc
                    for k in range(DCH):
                        p.dma(wt[:], w1_d[128 * k : 128 * (k + 1), j0 : j0 + 512], eng="gpsimd")
                        for tt in range(4):
                            lhsT = xnf[:, 512 * k + 128 * tt : 512 * k + 128 * (tt + 1)]
                            p.mm(ps1[tt][:], lhsT, wt[:], start=(k == 0), stop=(k == DCH - 1))
                    for tt in range(4):
                        p.tt(hb[:], ps1[tt][:], b1bc[:, j0 : j0 + 512], ALU.add)
                        p.act(hbb[:], hb[:], AF.Relu, eng="scalar")
                        p.dma(h1_dr[g0 + 128 * tt : g0 + 128 * (tt + 1), j0 : j0 + 512], hbb[:])
        with p.scope() as sc:
            h1f = p.sbuf("h1f", [128, 32 * 512], BF16, scope=sc)
            wt = p.sbuf("wt3", [128, 512], BF16, scope=sc)
            b2bc = p.sbuf("b2bc", [128, D], F32, scope=sc)
            p.dma(b2bc[:], b2_d[0:1, :].partition_broadcast(128))
            mo = p.sbuf("mo", [128, 512], F32, scope=sc)
            ps2 = [p.psum(f"ps2{i}", [128, 512], scope=sc) for i in range(8)]
            for g in range(4):
                g0 = 512 * g
                for k in range(32):
                    p.dmat(h1f[:, 512 * k : 512 * (k + 1)], h1_dr[g0 : g0 + 512, 128 * k : 128 * (k + 1)])
                for k in range(32):
                    for hc in range(2):
                        p.dma(wt[:], w2_d[128 * k : 128 * (k + 1), 512 * hc : 512 * (hc + 1)], eng="gpsimd")
                        for tt in range(4):
                            lhsT = h1f[:, 512 * k + 128 * tt : 512 * k + 128 * (tt + 1)]
                            p.mm(ps2[2 * tt + hc][:], lhsT, wt[:], start=(k == 0), stop=(k == 31))
                for tt in range(4):
                    t = 4 * g + tt
                    for hc in range(2):
                        h0 = 512 * hc
                        p.tt(mo[:], ps2[2 * tt + hc][:], b2bc[:, h0 : h0 + 512], ALU.add)
                        p.tt(X[t][:, h0 : h0 + 512], X[t][:, h0 : h0 + 512], mo[:], ALU.add)


def _l0attn_host_inputs(x_full):
    outs = []
    for c in range(8):
        b, half = c // 2, c % 2
        if half == 1:
            halo = np.ascontiguousarray(x_full[b, 2045:2048])
            hm = 1.0
        else:
            halo = np.ascontiguousarray(x_full[b, 0:3])
            hm = 0.0
        outs.append((halo, np.array([[hm]], np.float32), np.array([[1.0 if half == 1 else 0.0]], np.float32)))
    return outs


def build_l0_attn_launch():
    p = Prog("l0attn")
    nc = p.nc
    x_d = nc.dram_tensor("x", [S, D], F32, kind="ExternalInput")
    inw_d = nc.dram_tensor("inw", [D, 4384], F32, kind="ExternalInput")
    convw_d = nc.dram_tensor("convw", [CONVD, 4], F32, kind="ExternalInput")
    convb_d = nc.dram_tensor("convb", [1, CONVD], F32, kind="ExternalInput")
    dtb_d = nc.dram_tensor("dtb", [1, NH], F32, kind="ExternalInput")
    alog_d = nc.dram_tensor("alog", [1, NH], F32, kind="ExternalInput")
    dskip_d = nc.dram_tensor("dskip", [1, NH], F32, kind="ExternalInput")
    rmsw_d = nc.dram_tensor("rmsw", [1, DIN], F32, kind="ExternalInput")
    outw_d = nc.dram_tensor("outw", [DIN, D], F32, kind="ExternalInput")
    w1_d = nc.dram_tensor("w1", [D, DFF], F32, kind="ExternalInput")
    b1_d = nc.dram_tensor("b1", [1, DFF], F32, kind="ExternalInput")
    w2_d = nc.dram_tensor("w2", [DFF, D], F32, kind="ExternalInput")
    b2_d = nc.dram_tensor("b2", [1, D], F32, kind="ExternalInput")
    lnw_d = nc.dram_tensor("lnw", [1, D], F32, kind="ExternalInput")
    lnb_d = nc.dram_tensor("lnb", [1, D], F32, kind="ExternalInput")
    xhalo_d = nc.dram_tensor("xhalo", [3, D], F32, kind="ExternalInput")
    hmask_d = nc.dram_tensor("hmask", [1, 1], F32, kind="ExternalInput")
    omask_d = nc.dram_tensor("omask", [1, 1], F32, kind="ExternalInput")
    y_d = nc.dram_tensor("y", [S, D], F32, kind="ExternalOutput")
    X = [p.sbuf(f"x{t}", [128, D], F32) for t in range(NT)]
    for t in range(NT):
        p.dma(X[t][:], x_d[128 * t : 128 * (t + 1), :])
    emit_l0_attn(
        p, X, inw_d, convw_d, convb_d, dtb_d, alog_d, dskip_d, rmsw_d, outw_d,
        w1_d, b1_d, w2_d, b2_d, lnw_d, lnb_d, xhalo_d, hmask_d, omask_d
    )
    for t in range(NT):
        p.dma(y_d[128 * t : 128 * (t + 1), :], X[t][:])
    return p.finish()



SEQ = 4096
NJC = SEQ // 128


def emit_l1_attn(p, X, wq_d, wk_d, wv_d, bq_d, bk_d, bv_d, thr_d, msel_d, tag="a1"):
    xn_dr = p.dram(f"{tag}_xn", [S, D], BF16)
    q_dr = p.dram(f"{tag}_q", [S, D], BF16)
    kv_in = p.dram(f"{tag}_kvi", [S, 2 * D], BF16)
    kv_dr = p.dram(f"{tag}_kv", [SEQ, 2 * D], BF16)
    e_dr = p.dram(f"{tag}_e", [SEQ, S], BF16)
    st_in = p.dram(f"{tag}_sti", [2 * 128, NJC], F32)
    st_out = p.dram(f"{tag}_sto", [4 * 128, NJC], F32)
    v2_dr = p.dram(f"{tag}_v2", [SEQ, D], BF16)
    PAIRS = [[0, 1], [2, 3], [4, 5], [6, 7]]

    with p.scope() as outer:
        scr = {
            "sq": p.sbuf("sq", [128, D], F32, scope=outer),
            "st": p.sbuf("st", [128, 8], F32, scope=outer),
        }
        with p.scope() as sc:
            xn = p.sbuf("xn", [128, D], F32, scope=sc)
            xnb = p.sbuf("xnb", [128, D], BF16, scope=sc)
            for t in range(NT):
                emit_plain_ln(p, X[t][:], xn[:], scr)
                p.tt(X[t][:], X[t][:], xn[:], ALU.add)
                p.copy(xnb[:], xn[:])
                p.dma(xn_dr[128 * t : 128 * (t + 1), :], xnb[:])

        with p.scope() as sc:
            xnf = p.sbuf("xnf", [128, DCH * 512], BF16, scope=sc)
            wt = p.sbuf("wt", [128, 512], BF16, scope=sc)
            bbc = p.sbuf("bbc", [128, 3 * D], F32, scope=sc)
            qstg = p.sbuf("qstg", [128, 4 * D], BF16, scope=sc)
            kvstg = p.sbuf("kvstg", [128, 4 * 2 * D], BF16, scope=sc)
            ps = [p.psum(f"pqkv{i}", [128, 512], scope=sc) for i in range(4)]
            p.dma(bbc[:, 0:D], bq_d[0:1, :].partition_broadcast(128))
            p.dma(bbc[:, D : 2 * D], bk_d[0:1, :].partition_broadcast(128))
            p.dma(bbc[:, 2 * D : 3 * D], bv_d[0:1, :].partition_broadcast(128))
            for g in range(4):
                g0 = 512 * g
                for k in range(DCH):
                    p.dmat(xnf[:, 512 * k : 512 * (k + 1)], xn_dr[g0 : g0 + 512, 128 * k : 128 * (k + 1)])
                for wi, w_d in enumerate((wq_d, wk_d, wv_d)):
                    for jh in range(2):
                        j0 = 512 * jh
                        for k in range(DCH):
                            p.dma(wt[:], w_d[128 * k : 128 * (k + 1), j0 : j0 + 512], eng="gpsimd")
                            for tt in range(4):
                                lhsT = xnf[:, 512 * k + 128 * tt : 512 * k + 128 * (tt + 1)]
                                p.mm(ps[tt][:], lhsT, wt[:], start=(k == 0), stop=(k == DCH - 1))
                        for tt in range(4):
                            dst = (
                                qstg[:, D * tt + j0 : D * tt + j0 + 512]
                                if wi == 0
                                else kvstg[:, 2 * D * tt + (wi - 1) * D + j0 : 2 * D * tt + (wi - 1) * D + j0 + 512]
                            )
                            p.tt(dst, ps[tt][:], bbc[:, wi * D + j0 : wi * D + j0 + 512], ALU.add)
                for tt in range(4):
                    r0 = g0 + 128 * tt
                    p.dma(q_dr[r0 : r0 + 128, :], qstg[:, D * tt : D * (tt + 1)])
                    p.dma(kv_in[r0 : r0 + 128, :], kvstg[:, 2 * D * tt : 2 * D * (tt + 1)])
        p.collective("AllGather", ALU.bypass, PAIRS, [kv_in.ap()], [kv_dr.ap()])

        iotaT = p.sbuf("iotaT", [128, S], F32, scope=outer)
        p.iota(iotaT[:], [[1, S]], base=0, channel_multiplier=-1)
        thrbc = p.sbuf("thrbc", [128, NJC], F32, scope=outer)
        p.dma(thrbc[:], thr_d[0:1, :].partition_broadcast(128))
        with p.scope() as sc:
            qf = p.sbuf("qf", [128, DCH * S], BF16, scope=sc)
            kf = p.sbuf("kf", [128, DCH * SEQ], BF16, scope=sc)
            for dc in range(DCH):
                p.dmat(qf[:, S * dc : S * (dc + 1)], q_dr[:, 128 * dc : 128 * (dc + 1)])
                p.dmat(kf[:, SEQ * dc : SEQ * (dc + 1)], kv_dr[:, 128 * dc : 128 * (dc + 1)])
            sm = p.sbuf("sm", [128, S], F32, scope=sc)
            eb = p.sbuf("eb", [128, S], BF16, scope=sc)
            maskA = p.sbuf("maskA", [128, S], F32, scope=sc)
            mcol = p.sbuf("mcol", [128, 1], F32, scope=sc)
            nmcol = p.sbuf("nmcol", [128, 1], F32, scope=sc)
            dcol = p.sbuf("dcol", [128, 1], F32, scope=sc)
            pss = [p.psum(f"pss{i}", [128, 512], scope=sc) for i in range(4)]
            for jc in range(NJC):
                for it in range(4):
                    i0 = 512 * it
                    for dc in range(DCH):
                        lhsT = kf[:, SEQ * dc + 128 * jc : SEQ * dc + 128 * (jc + 1)]
                        rhs = qf[:, S * dc + i0 : S * dc + i0 + 512]
                        p.mm(pss[it][:], lhsT, rhs, start=(dc == 0), stop=(dc == DCH - 1))
                p.ts(maskA[:], iotaT[:, 0:S], thrbc[:, jc : jc + 1], -1e30, ALU.is_lt, ALU.mult)
                for it in range(4):
                    p.tt(sm[:, 512 * it : 512 * (it + 1)], pss[it][:], maskA[:, 512 * it : 512 * (it + 1)], ALU.add)
                p.red(mcol[:], sm[:], ALU.max)
                p.ts(nmcol[:], mcol[:], -1.0, None, ALU.mult)
                p.memset(dcol[:], 0.0)
                p.act(eb[:], sm[:], AF.Exp, bias=nmcol[:], accum_out=dcol[:])
                p.dma(e_dr[128 * jc : 128 * (jc + 1), :], eb[:])
                p.dma(st_in[0:128, jc : jc + 1], mcol[:], allow_slow_non_contiguous=True)
                p.dma(st_in[128:256, jc : jc + 1], dcol[:], allow_slow_non_contiguous=True)
        p.collective("AllGather", ALU.bypass, PAIRS, [st_in.ap()], [st_out.ap()])

        fmat = p.sbuf("fmat", [128, NJC], F32, scope=outer)
        with p.scope() as sc:
            me = p.sbuf("cme", [128, NJC], F32, scope=sc)
            de = p.sbuf("cde", [128, NJC], F32, scope=sc)
            mo = p.sbuf("cmo", [128, NJC], F32, scope=sc)
            do_ = p.sbuf("cdo", [128, NJC], F32, scope=sc)
            mm_ = p.sbuf("cmm", [128, NJC], F32, scope=sc)
            tmp = p.sbuf("ctmp", [128, NJC], F32, scope=sc)
            eme = p.sbuf("ceme", [128, NJC], F32, scope=sc)
            emo = p.sbuf("cemo", [128, NJC], F32, scope=sc)
            mselc = p.sbuf("mselc", [128, 1], F32, scope=sc)
            p.dma(me[:], st_out[0:128, :])
            p.dma(de[:], st_out[128:256, :])
            p.dma(mo[:], st_out[256:384, :])
            p.dma(do_[:], st_out[384:512, :])
            p.dma(mselc[:], msel_d[0:1, 0:1].partition_broadcast(128))
            p.tt(mm_[:], me[:], mo[:], ALU.max)
            p.tt(tmp[:], me[:], mm_[:], ALU.subtract)
            p.act(eme[:], tmp[:], AF.Exp)
            p.tt(tmp[:], mo[:], mm_[:], ALU.subtract)
            p.act(emo[:], tmp[:], AF.Exp)
            p.tt(de[:], de[:], eme[:], ALU.mult)
            p.tt(do_[:], do_[:], emo[:], ALU.mult)
            p.tt(de[:], de[:], do_[:], ALU.add)
            p.ts(tmp[:], eme[:], mselc[:], None, ALU.mult)
            p.ts(eme[:], emo[:], mselc[:], None, ALU.mult)
            p.tt(emo[:], emo[:], eme[:], ALU.subtract)
            p.tt(tmp[:], tmp[:], emo[:], ALU.add)
            p.ts(de[:], de[:], 32.0, None, ALU.mult)
            p.recip(de[:], de[:])
            p.tt(fmat[:], tmp[:], de[:], ALU.mult)
        with p.scope() as sc:
            vt = p.sbuf("vt", [128, D], BF16, scope=sc)
            for jc in range(NJC):
                p.dma(vt[:], kv_dr[128 * jc : 128 * (jc + 1), D : 2 * D])
                p.ts(vt[:], vt[:], fmat[:, jc : jc + 1], None, ALU.mult)
                p.dma(v2_dr[128 * jc : 128 * (jc + 1), :], vt[:])

        with p.scope() as sc:
            ers = p.sbuf("ers", [128, 512], BF16, scope=sc)
            vt = p.sbuf("vt2", [128, D], BF16, scope=sc)
            psr = [p.psum(f"psr{i}", [128, 512], scope=sc) for i in range(8)]
            for ig in range(4):
                i0 = 512 * ig
                for jc in range(NJC):
                    p.dma(ers[:], e_dr[128 * jc : 128 * (jc + 1), i0 : i0 + 512])
                    p.dma(vt[:], v2_dr[128 * jc : 128 * (jc + 1), :], eng="gpsimd")
                    for isub in range(4):
                        lhsT = ers[:, 128 * isub : 128 * (isub + 1)]
                        for vfh in range(2):
                            p.mm(
                                psr[2 * isub + vfh][:],
                                lhsT,
                                vt[:, 512 * vfh : 512 * (vfh + 1)],
                                start=(jc == 0),
                                stop=(jc == NJC - 1),
                            )
                for isub in range(4):
                    t = 4 * ig + isub
                    for vfh in range(2):
                        h0 = 512 * vfh
                        p.tt(X[t][:, h0 : h0 + 512], X[t][:, h0 : h0 + 512], psr[2 * isub + vfh][:], ALU.add)


def _l1attn_host_inputs():
    outs = []
    for c in range(8):
        base = 2048 * (c % 2)
        thr = np.array([[128 * jc - base for jc in range(NJC)]], np.float32)
        msel = np.array([[1.0 if c % 2 == 0 else 0.0]], np.float32)
        outs.append((thr, msel))
    return outs


def build_l1_attn_launch():
    p = Prog("l1attn")
    nc = p.nc
    x_d = nc.dram_tensor("x", [S, D], F32, kind="ExternalInput")
    wq_d = nc.dram_tensor("wq", [D, D], F32, kind="ExternalInput")
    wk_d = nc.dram_tensor("wk", [D, D], F32, kind="ExternalInput")
    wv_d = nc.dram_tensor("wv", [D, D], F32, kind="ExternalInput")
    bq_d = nc.dram_tensor("bq", [1, D], F32, kind="ExternalInput")
    bk_d = nc.dram_tensor("bk", [1, D], F32, kind="ExternalInput")
    bv_d = nc.dram_tensor("bv", [1, D], F32, kind="ExternalInput")
    thr_d = nc.dram_tensor("thr", [1, NJC], F32, kind="ExternalInput")
    msel_d = nc.dram_tensor("msel", [1, 1], F32, kind="ExternalInput")
    y_d = nc.dram_tensor("y", [S, D], F32, kind="ExternalOutput")
    X = [p.sbuf(f"x{t}", [128, D], F32) for t in range(NT)]
    for t in range(NT):
        p.dma(X[t][:], x_d[128 * t : 128 * (t + 1), :])
    emit_l1_attn(p, X, wq_d, wk_d, wv_d, bq_d, bk_d, bv_d, thr_d, msel_d)
    for t in range(NT):
        p.dma(y_d[128 * t : 128 * (t + 1), :], X[t][:])
    return p.finish()


def _l0ffn_host_inputs(wg, wu, wd):
    outs = []
    stacked = np.concatenate([wg, wu], axis=0)
    for c in range(8):
        wisl = stacked[704 * c : 704 * (c + 1), :]
        wdsl = wd[128 * c : 128 * (c + 1), :]
        outs.append((np.ascontiguousarray(wisl), np.ascontiguousarray(wdsl)))
    return outs


def build_l0_ffn_launch():
    p = Prog("l0ffn")
    nc = p.nc
    x_d = nc.dram_tensor("x", [S, D], F32, kind="ExternalInput")
    wisl_d = nc.dram_tensor("wisl", [704, D], F32, kind="ExternalInput")
    wdsl_d = nc.dram_tensor("wdsl", [128, I_FF], F32, kind="ExternalInput")
    clipx_d = nc.dram_tensor("clipx", [1, 1], F32, kind="ExternalInput")
    clipd_d = nc.dram_tensor("clipd", [1, 1], F32, kind="ExternalInput")
    y_d = nc.dram_tensor("y", [S, D], F32, kind="ExternalOutput")
    X = [p.sbuf(f"x{t}", [128, D], F32) for t in range(NT)]
    for t in range(NT):
        p.dma(X[t][:], x_d[128 * t : 128 * (t + 1), :])
    emit_l0_ffn(p, X, wisl_d, wdsl_d, clipx_d, clipd_d)
    for t in range(NT):
        p.dma(y_d[128 * t : 128 * (t + 1), :], X[t][:])
    return p.finish()


def _l1ffn_host_inputs(wg, wu, wd):
    outs = []
    for c in range(8):
        wsl = np.concatenate(
            [wg[:, 128 * c : 128 * (c + 1)], wu[:, 128 * c : 128 * (c + 1)]], axis=0
        )
        wdsl = wd[128 * c : 128 * (c + 1), :]
        outs.append((np.ascontiguousarray(wsl), np.ascontiguousarray(wdsl)))
    return outs




def build_l1_ffn_launch(debug=False):
    p = Prog("l1ffn")
    nc = p.nc
    x_d = nc.dram_tensor("x", [S, D], F32, kind="ExternalInput")
    wsl_d = nc.dram_tensor("wsl", [2 * I_FF, 128], F32, kind="ExternalInput")
    wdsl_d = nc.dram_tensor("wdsl", [128, I_FF], F32, kind="ExternalInput")
    y_d = nc.dram_tensor("y", [S, D], F32, kind="ExternalOutput")

    X = [p.sbuf(f"x{t}", [128, D], F32) for t in range(NT)]
    for t in range(NT):
        p.dma(X[t][:], x_d[128 * t : 128 * (t + 1), :])
    emit_l1_ffn(p, X, wsl_d, wdsl_d)
    for t in range(NT):
        p.dma(y_d[128 * t : 128 * (t + 1), :], X[t][:])
    if debug:
        for nm, h in DBG.items():
            out = nc.dram_tensor("dbg_" + nm, list(h.ap().shape), h.ap().dtype, kind="ExternalOutput")
            p.dma(out.ap(), h.ap())
    return p.finish()


DBG = {}


def build_fused():
    p = Prog("mk")
    nc = p.nc
    x_d = nc.dram_tensor("x", [S, D], F32, kind="ExternalInput")
    inw_d = nc.dram_tensor("inw", [D, 4384], F32, kind="ExternalInput")
    convw_d = nc.dram_tensor("convw", [CONVD, 4], F32, kind="ExternalInput")
    convb_d = nc.dram_tensor("convb", [1, CONVD], F32, kind="ExternalInput")
    dtb_d = nc.dram_tensor("dtb", [1, NH], F32, kind="ExternalInput")
    alog_d = nc.dram_tensor("alog", [1, NH], F32, kind="ExternalInput")
    dskip_d = nc.dram_tensor("dskip", [1, NH], F32, kind="ExternalInput")
    rmsw_d = nc.dram_tensor("rmsw", [1, DIN], F32, kind="ExternalInput")
    outw_d = nc.dram_tensor("outw", [DIN, D], F32, kind="ExternalInput")
    w1_d = nc.dram_tensor("w1", [D, DFF], F32, kind="ExternalInput")
    b1_d = nc.dram_tensor("b1", [1, DFF], F32, kind="ExternalInput")
    w2_d = nc.dram_tensor("w2", [DFF, D], F32, kind="ExternalInput")
    b2_d = nc.dram_tensor("b2", [1, D], F32, kind="ExternalInput")
    lnw_d = nc.dram_tensor("lnw", [1, D], F32, kind="ExternalInput")
    lnb_d = nc.dram_tensor("lnb", [1, D], F32, kind="ExternalInput")
    xhalo_d = nc.dram_tensor("xhalo", [3, D], F32, kind="ExternalInput")
    hmask_d = nc.dram_tensor("hmask", [1, 1], F32, kind="ExternalInput")
    omask_d = nc.dram_tensor("omask", [1, 1], F32, kind="ExternalInput")
    wisl_d = nc.dram_tensor("wisl", [704, D], F32, kind="ExternalInput")
    wdsl0_d = nc.dram_tensor("wdsl0", [128, I_FF], F32, kind="ExternalInput")
    clipx_d = nc.dram_tensor("clipx", [1, 1], F32, kind="ExternalInput")
    clipd_d = nc.dram_tensor("clipd", [1, 1], F32, kind="ExternalInput")
    wq_d = nc.dram_tensor("wq", [D, D], F32, kind="ExternalInput")
    wk_d = nc.dram_tensor("wk", [D, D], F32, kind="ExternalInput")
    wv_d = nc.dram_tensor("wv", [D, D], F32, kind="ExternalInput")
    bq_d = nc.dram_tensor("bq", [1, D], F32, kind="ExternalInput")
    bk_d = nc.dram_tensor("bk", [1, D], F32, kind="ExternalInput")
    bv_d = nc.dram_tensor("bv", [1, D], F32, kind="ExternalInput")
    thr_d = nc.dram_tensor("thr", [1, NJC], F32, kind="ExternalInput")
    msel_d = nc.dram_tensor("msel", [1, 1], F32, kind="ExternalInput")
    wsl1_d = nc.dram_tensor("wsl1", [2 * I_FF, 128], F32, kind="ExternalInput")
    wdsl1_d = nc.dram_tensor("wdsl1", [128, I_FF], F32, kind="ExternalInput")
    y_d = nc.dram_tensor("y", [S, D], F32, kind="ExternalOutput")

    X = [p.sbuf(f"x{t}", [128, D], F32) for t in range(NT)]
    for t in range(NT):
        p.dma(X[t][:], x_d[128 * t : 128 * (t + 1), :])
    emit_l0_attn(
        p, X, inw_d, convw_d, convb_d, dtb_d, alog_d, dskip_d, rmsw_d, outw_d,
        w1_d, b1_d, w2_d, b2_d, lnw_d, lnb_d, xhalo_d, hmask_d, omask_d
    )
    emit_l0_ffn(p, X, wisl_d, wdsl0_d, clipx_d, clipd_d)
    emit_l1_attn(p, X, wq_d, wk_d, wv_d, bq_d, bk_d, bv_d, thr_d, msel_d)
    emit_l1_ffn(p, X, wsl1_d, wdsl1_d)
    for t in range(NT):
        p.dma(y_d[128 * t : 128 * (t + 1), :], X[t][:])
    return p.finish()


def _fused_in_maps(inputs):
    x = np.asarray(inputs["L0_attn_x"], np.float32)
    xs = _shard_x(x)
    hi0 = _l0attn_host_inputs(x)
    wsl0 = _l0ffn_host_inputs(
        np.asarray(inputs["L0_ffn_w_gate"], np.float32),
        np.asarray(inputs["L0_ffn_w_up"], np.float32),
        np.asarray(inputs["L0_ffn_w_down"], np.float32),
    )
    hi1 = _l1attn_host_inputs()
    wsl1 = _l1ffn_host_inputs(
        np.asarray(inputs["L1_ffn_w_gate"], np.float32),
        np.asarray(inputs["L1_ffn_w_up"], np.float32),
        np.asarray(inputs["L1_ffn_w_down"], np.float32),
    )
    base = {
        "inw": np.asarray(inputs["L0_attn_in_proj_w"], np.float32),
        "convw": np.asarray(inputs["L0_attn_conv_w"], np.float32),
        "convb": np.asarray(inputs["L0_attn_conv_b"], np.float32).reshape(1, -1),
        "dtb": np.asarray(inputs["L0_attn_dt_bias"], np.float32).reshape(1, -1),
        "alog": np.asarray(inputs["L0_attn_A_log"], np.float32).reshape(1, -1),
        "dskip": np.asarray(inputs["L0_attn_D_skip"], np.float32).reshape(1, -1),
        "rmsw": np.asarray(inputs["L0_attn_rms_w"], np.float32).reshape(1, -1),
        "outw": np.asarray(inputs["L0_attn_out_proj_w"], np.float32),
        "w1": np.asarray(inputs["L0_attn_mlp_w1"], np.float32),
        "b1": np.asarray(inputs["L0_attn_mlp_b1"], np.float32).reshape(1, -1),
        "w2": np.asarray(inputs["L0_attn_mlp_w2"], np.float32),
        "b2": np.asarray(inputs["L0_attn_mlp_b2"], np.float32).reshape(1, -1),
        "lnw": np.asarray(inputs["L0_attn_ln_w"], np.float32).reshape(1, -1),
        "lnb": np.asarray(inputs["L0_attn_ln_b"], np.float32).reshape(1, -1),
        "clipx": np.asarray(inputs["L0_ffn_clip_x"], np.float32).reshape(1, 1),
        "clipd": np.asarray(inputs["L0_ffn_clip_down"], np.float32).reshape(1, 1),
        "wq": np.asarray(inputs["L1_attn_Wq"], np.float32),
        "wk": np.asarray(inputs["L1_attn_Wk"], np.float32),
        "wv": np.asarray(inputs["L1_attn_Wv"], np.float32),
        "bq": np.asarray(inputs["L1_attn_bq"], np.float32).reshape(1, -1),
        "bk": np.asarray(inputs["L1_attn_bk"], np.float32).reshape(1, -1),
        "bv": np.asarray(inputs["L1_attn_bv"], np.float32).reshape(1, -1),
    }
    in_maps = []
    for c in range(8):
        m = dict(base)
        m["x"] = xs[c]
        m["xhalo"], m["hmask"], m["omask"] = hi0[c]
        m["wisl"], m["wdsl0"] = wsl0[c]
        m["thr"], m["msel"] = hi1[c]
        m["wsl1"], m["wdsl1"] = wsl1[c]
        in_maps.append(m)
    return in_maps


def kernel(**inputs):
    nc = build_fused()
    in_maps = _fused_in_maps(inputs)
    res = run_bass_kernel_spmd(nc, in_maps, core_ids=list(range(8)))
    out = np.zeros((4, 4096, 1024), np.float32)
    for c in range(8):
        out[c // 2, 2048 * (c % 2) : 2048 * (c % 2 + 1)] = np.asarray(res.results[c]["y"])
    return out


def _shard_x(x):
    return [np.ascontiguousarray(x[c // 2, 2048 * (c % 2) : 2048 * (c % 2 + 1)]) for c in range(8)]


def kernel_phase_a(**inputs):
    x = np.asarray(inputs["L0_attn_x"], np.float32)
    xs = _shard_x(x)

    nc = build_l0_attn_launch()
    hi = _l0attn_host_inputs(x)
    base = {
        "inw": np.asarray(inputs["L0_attn_in_proj_w"], np.float32),
        "convw": np.asarray(inputs["L0_attn_conv_w"], np.float32),
        "convb": np.asarray(inputs["L0_attn_conv_b"], np.float32).reshape(1, -1),
        "dtb": np.asarray(inputs["L0_attn_dt_bias"], np.float32).reshape(1, -1),
        "alog": np.asarray(inputs["L0_attn_A_log"], np.float32).reshape(1, -1),
        "dskip": np.asarray(inputs["L0_attn_D_skip"], np.float32).reshape(1, -1),
        "rmsw": np.asarray(inputs["L0_attn_rms_w"], np.float32).reshape(1, -1),
        "outw": np.asarray(inputs["L0_attn_out_proj_w"], np.float32),
        "w1": np.asarray(inputs["L0_attn_mlp_w1"], np.float32),
        "b1": np.asarray(inputs["L0_attn_mlp_b1"], np.float32).reshape(1, -1),
        "w2": np.asarray(inputs["L0_attn_mlp_w2"], np.float32),
        "b2": np.asarray(inputs["L0_attn_mlp_b2"], np.float32).reshape(1, -1),
        "lnw": np.asarray(inputs["L0_attn_ln_w"], np.float32).reshape(1, -1),
        "lnb": np.asarray(inputs["L0_attn_ln_b"], np.float32).reshape(1, -1),
    }
    in_maps = []
    for c in range(8):
        m = dict(base)
        m["x"] = xs[c]
        m["xhalo"], m["hmask"], m["omask"] = hi[c]
        in_maps.append(m)
    res = run_bass_kernel_spmd(nc, in_maps, core_ids=list(range(8)))
    xs = [np.asarray(res.results[c]["y"]) for c in range(8)]

    nc = build_l0_ffn_launch()
    wsl = _l0ffn_host_inputs(
        np.asarray(inputs["L0_ffn_w_gate"], np.float32),
        np.asarray(inputs["L0_ffn_w_up"], np.float32),
        np.asarray(inputs["L0_ffn_w_down"], np.float32),
    )
    cx = np.asarray(inputs["L0_ffn_clip_x"], np.float32).reshape(1, 1)
    cd = np.asarray(inputs["L0_ffn_clip_down"], np.float32).reshape(1, 1)
    in_maps = [
        {"x": xs[c], "wisl": wsl[c][0], "wdsl": wsl[c][1], "clipx": cx, "clipd": cd}
        for c in range(8)
    ]
    res = run_bass_kernel_spmd(nc, in_maps, core_ids=list(range(8)))
    xs = [np.asarray(res.results[c]["y"]) for c in range(8)]

    nc = build_l1_attn_launch()
    hi = _l1attn_host_inputs()
    in_maps = [
        {
            "x": xs[c],
            "wq": np.asarray(inputs["L1_attn_Wq"], np.float32),
            "wk": np.asarray(inputs["L1_attn_Wk"], np.float32),
            "wv": np.asarray(inputs["L1_attn_Wv"], np.float32),
            "bq": np.asarray(inputs["L1_attn_bq"], np.float32).reshape(1, -1),
            "bk": np.asarray(inputs["L1_attn_bk"], np.float32).reshape(1, -1),
            "bv": np.asarray(inputs["L1_attn_bv"], np.float32).reshape(1, -1),
            "thr": hi[c][0],
            "msel": hi[c][1],
        }
        for c in range(8)
    ]
    res = run_bass_kernel_spmd(nc, in_maps, core_ids=list(range(8)))
    xs = [np.asarray(res.results[c]["y"]) for c in range(8)]

    nc = build_l1_ffn_launch()
    wsl = _l1ffn_host_inputs(
        np.asarray(inputs["L1_ffn_w_gate"], np.float32),
        np.asarray(inputs["L1_ffn_w_up"], np.float32),
        np.asarray(inputs["L1_ffn_w_down"], np.float32),
    )
    in_maps = [{"x": xs[c], "wsl": wsl[c][0], "wdsl": wsl[c][1]} for c in range(8)]
    res = run_bass_kernel_spmd(nc, in_maps, core_ids=list(range(8)))
    xs = [np.asarray(res.results[c]["y"]) for c in range(8)]

    out = np.zeros((4, 4096, 1024), np.float32)
    for c in range(8):
        out[c // 2, 2048 * (c % 2) : 2048 * (c % 2 + 1)] = xs[c]
    return out


# version 06
# speedup vs baseline: 1.4441x; 1.4441x over previous
import numpy as np
from contextlib import ExitStack

import concourse.bass as bass
import concourse.mybir as mybir
from concourse.bass_utils import run_bass_kernel_spmd

F32 = mybir.dt.float32
I32 = mybir.dt.int32
BF16 = mybir.dt.bfloat16
AF = mybir.ActivationFunctionType
ALU = mybir.AluOpType
AX = mybir.AxisListType

MAGIC = 12582912.0
SEM_LIM = 24000

S = 2048
NT = S // 128
D = 1024
DCH = D // 128

COMPUTE = ("scalar", "vector", "tensor", "gpsimd")


class Prog:

    def __init__(self, name="prog"):
        self.nc = bass.Bass("TRN2", target_bir_lowering=False)
        self._stack = ExitStack()
        self._stack.__enter__()
        self.cnt = 0
        self.items = []
        self._last = None
        self._uid = 0
        self._consts = {}
        self._const_tile = self.sbuf("constpool", [128, 32], F32)

    def uid(self):
        self._uid += 1
        return self._uid

    def sbuf(self, name, shape, dtype, scope=None):
        st = scope if scope is not None else self._stack
        return st.enter_context(self.nc.sbuf_tensor(f"{name}_{self.uid()}", shape, dtype))

    def psum(self, name, shape, dtype=F32, scope=None):
        st = scope if scope is not None else self._stack
        return st.enter_context(self.nc.psum_tensor(f"{name}_{self.uid()}", shape, dtype))

    def scope(self):
        return ExitStack()

    def semaphore(self, name):
        return self._stack.enter_context(self.nc.semaphore(f"{name}_{self.uid()}"))

    def dram(self, name, shape, dtype, kind="Internal"):
        return self.nc.dram_tensor(name, shape, dtype, kind=kind)

    def constv(self, v):
        v = float(v)
        if v in (0.0, 1.0):
            return v
        if v not in self._consts:
            j = len(self._consts)
            assert j < 32
            self._consts[v] = j
            col = self._const_tile[:, j : j + 1]
            self.memset(col, v)
            self.nc.const_aps.register(F32, v, col)
        return v

    def par(self, key=None):
        prog = self
        import os
        disabled = os.environ.get("PAR_OFF", "")
        off = disabled == "all" or (key is not None and key in disabled.split(","))

        class _Par:
            def __enter__(self_):
                if off:
                    return self_
                prog._par_wait = prog.cnt
                prog._par_seen = set()
                prog._par_on = True
                return self_

            def __exit__(self_, *a):
                prog._par_on = False
                prog._last = None
                return False

        return _Par()

    def op(self, eng, emit, inc=1, is_async=False, barrier=False, chain_ok=False):
        chain_ok = False
        if (
            chain_ok
            and not barrier
            and not getattr(self, "_par_on", False)
            and self._last is not None
            and self._last[0] == eng
            and not self._last[1]
        ):
            self.items.append((eng, None, emit, inc))
            self.cnt += inc
            self._last = (eng, is_async)
            return self.cnt
        if getattr(self, "_par_on", False) and not barrier:
            if eng in self._par_seen:
                wait = None
            else:
                wait = self._par_wait
                self._par_seen.add(eng)
            self.items.append((eng, wait, emit, inc))
            self.cnt += inc
            self._last = (eng, is_async)
            return self.cnt
        chain = (
            not barrier
            and self._last is not None
            and self._last[0] == eng
            and not self._last[1]
            and eng == "tensor"
        )
        wait = None if chain else self.cnt
        self.items.append((eng, wait, emit, inc))
        self.cnt += inc
        self._last = (eng, is_async)
        return self.cnt

    def dma(self, out, in_, eng="sync", **kw):
        return self.op(eng, lambda e, o=out, i=in_, k=kw: e.dma_start(o, i, **k), 16, is_async=True)

    def dmat(self, out, in_, eng="scalar"):
        return self.op(eng, lambda e, o=out, i=in_: e.dma_start_transpose(o, i), 16, is_async=True)

    def mm(self, out, lhsT, rhs, start=True, stop=True):
        return self.op(
            "tensor",
            lambda e, o=out, l=lhsT, r=rhs, s0=start, s1=stop: e.matmul(o, l, r, start=s0, stop=s1),
        )

    def transpose(self, out, in_, ident):
        return self.op("tensor", lambda e, o=out, i=in_, d=ident: e.transpose(o, i, d))

    def act(self, out, in_, func, bias=0.0, scale=1.0, accum_out=None, eng="scalar"):
        return self.op(
            eng,
            lambda e, o=out, i=in_, f=func, b=bias, s=scale, a=accum_out: e.activation(
                o, i, f, bias=b, scale=s, accum_out=a
            ),
        )

    def tt(self, out, a, b, alu, eng="vector", chain_ok=False):
        return self.op(eng, lambda e, o=out, x=a, y=b, p=alu: e.tensor_tensor(o, x, y, p), chain_ok=chain_ok)

    def ts(self, out, in_, s1, s2, op0, op1=ALU.bypass, accum_out=None, eng="vector", chain_ok=False):
        return self.op(
            eng,
            lambda e, o=out, i=in_, a=s1, b=s2, p0=op0, p1=op1, ac=accum_out: e.tensor_scalar(
                o, i, a, b, p0, p1, accum_out=ac
            ),
            chain_ok=chain_ok,
        )

    def tss(self, out, in_, s, alu, eng="vector", chain_ok=False):
        return self.op(eng, lambda e, o=out, i=in_, a=s, p=alu: e.tensor_single_scalar(o, i, a, p), chain_ok=chain_ok)

    def stt(self, out, in0, scalar, in1, op0, op1, eng="vector", chain_ok=False):
        return self.op(
            eng,
            lambda e, o=out, i0=in0, sc=scalar, i1=in1, p0=op0, p1=op1: e.scalar_tensor_tensor(
                o, i0, sc, i1, p0, p1
            ),
            chain_ok=chain_ok,
        )

    def red(self, out, in_, alu, absval=False, eng="vector"):
        return self.op(
            eng,
            lambda e, o=out, i=in_, p=alu, a=absval: e.tensor_reduce(
                o, i, AX.X, p, apply_absolute_value=a
            ),
        )

    def scan(self, out, d0, d1, init, op0, op1, eng="vector"):
        return self.op(
            eng,
            lambda e, o=out, a=d0, b=d1, ini=init, p0=op0, p1=op1: e.tensor_tensor_scan(
                o, a, b, ini, p0, p1
            ),
        )

    def copy(self, out, in_, eng="vector"):
        return self.op(eng, lambda e, o=out, i=in_: e.tensor_copy(o, i))

    def memset(self, ap, v, eng="vector"):
        return self.op(eng, lambda e, o=ap, x=v: e.memset(o, x))

    def recip(self, out, in_):
        return self.op("vector", lambda e, o=out, i=in_: e.reciprocal(o, i))

    def iota(self, out, pattern, base=0, channel_multiplier=0):
        return self.op(
            "gpsimd",
            lambda e, o=out, p=pattern, b=base, c=channel_multiplier: e.iota(
                o, p, base=b, channel_multiplier=c, allow_small_or_imprecise_dtypes=True
            ),
        )

    def affsel(self, out, in_, pattern, compare_op, fill, base=0, channel_multiplier=1):
        return self.op(
            "gpsimd",
            lambda e, o=out, i=in_, p=pattern, c=compare_op, f=fill, b=base, m=channel_multiplier: e.affine_select(
                out=o, in_=i, pattern=p, compare_op=c, fill=f, base=b, channel_multiplier=m
            ),
        )

    def collective(self, kind, alu, groups, ins, outs):
        return self.op(
            "gpsimd",
            lambda e, k=kind, o=alu, g=groups, i=ins, u=outs: e.collective_compute(
                k, o, replica_groups=g, ins=i, outs=u
            ),
            1,
            is_async=True,
        )

    def finish(self):
        run = 0
        assign = []
        tot = []
        after_map = {0: (0, 0)}
        for eng, wait, emit, inc in self.items:
            after = run + inc
            k = (after - 1) // SEM_LIM
            while len(tot) <= k:
                tot.append(0)
            tot[k] += inc
            assign.append((eng, wait, emit, inc, k, tot[k]))
            after_map[after] = (k, tot[k])
            run = after

        sems = [self.semaphore("gs") for _ in range(len(tot))]
        by_eng = {}
        for eng, wait, emit, inc, k, valk in assign:
            by_eng.setdefault(eng, []).append((wait, emit, inc, k, valk))

        block = self._stack.enter_context(self.nc.Block())
        amap = after_map
        totals = tot

        def emit_engine(engobj, lst):
            for wait, emit, inc, k, valk in lst:
                if wait is not None and wait > 0:
                    wk, wv = amap[wait]
                    for j in range(wk):
                        engobj.wait_ge(sems[j], totals[j])
                    engobj.wait_ge(sems[wk], wv)
                inst = emit(engobj)
                inst.then_inc(sems[k], inc)

        def final_wait(engobj):
            for j in range(len(totals)):
                engobj.wait_ge(sems[j], totals[j])

        lists = {e: by_eng.get(e, []) for e in ("sync", "scalar", "vector", "tensor", "gpsimd")}

        @block.sync
        def _(e):
            emit_engine(e, lists["sync"])
            final_wait(e)

        @block.scalar
        def _(e):
            emit_engine(e, lists["scalar"])

        @block.vector
        def _(e):
            emit_engine(e, lists["vector"])

        @block.tensor
        def _(e):
            emit_engine(e, lists["tensor"])

        @block.gpsimd
        def _(e):
            emit_engine(e, lists["gpsimd"])

        self._stack.close()
        return self.nc




def emit_plain_ln(p, x_tile, out_tile, scratch, eps=1e-5, out_is_bf16=False):
    st = scratch["st"]
    mu, ms, var, rstd = st[:, 0:1], st[:, 1:2], st[:, 2:3], st[:, 3:4]
    n = x_tile.shape[1]
    p.memset(st[:, 0:4], 0.0)
    p.act(scratch["sq"][:, 0:n], x_tile, AF.Square, accum_out=ms)
    p.red(mu, x_tile, ALU.add)
    p.ts(mu, mu, 1.0 / n, None, ALU.mult)
    p.ts(ms, ms, 1.0 / n, None, ALU.mult)
    p.act(var, mu, AF.Square)
    p.tt(var, ms, var, ALU.subtract)
    p.act(var, var, AF.Sqrt, bias=p.constv(eps))
    p.recip(rstd, var)
    p.ts(out_tile, x_tile, mu, rstd, ALU.subtract, ALU.mult)


def emit_fp4_quant(p, x_tile, out_tile, scratch, ncol):
    nb = ncol // 16
    a, b2 = scratch["a"][:, 0:ncol], scratch["b"][:, 0:ncol]
    bm, br = scratch["bm"][:, 0:nb], scratch["br"][:, 0:nb]
    x3 = x_tile.rearrange("p (a b) -> p a b", b=16)
    sgn = scratch["sg4"][:, 0:ncol]
    p.op("scalar", lambda e, o=sgn, i=x_tile: e.sign(o, i))
    p.red(bm, x3, ALU.max, absval=True)
    p.ts(bm, bm, 1.0 / 12.0, 5e-13, ALU.mult, ALU.max, chain_ok=True)
    p.recip(br, bm)
    p.ts(a.bitcast(I32), x_tile.bitcast(I32), 0x7FFFFFFF, None, ALU.bitwise_and)
    a3 = a.rearrange("p (a b) -> p a b", b=16)
    p.tt(a3, a3, br.unsqueeze(2).broadcast_to([128, nb, 16]), ALU.mult, chain_ok=True)
    c1, c2 = b2, scratch["c2"][:, 0:ncol]
    p.tss(c1, a, 0.5, ALU.is_gt, chain_ok=True)
    p.stt(c1, a, 1.5, c1, ALU.is_gt, ALU.add, chain_ok=True)
    p.stt(c1, a, 2.5, c1, ALU.is_gt, ALU.add, chain_ok=True)
    p.stt(c1, a, 3.5, c1, ALU.is_gt, ALU.add, chain_ok=True)
    p.tss(c2, a, 5.0, ALU.is_gt, chain_ok=True)
    p.stt(c2, a, 7.0, c2, ALU.is_gt, ALU.add, chain_ok=True)
    p.stt(c2, a, 10.0, c2, ALU.is_gt, ALU.add, chain_ok=True)
    p.stt(c2, a, 10.0, c2, ALU.is_gt, ALU.add, chain_ok=True)
    p.stt(c1, c2, 2.0, c1, ALU.mult, ALU.add, chain_ok=True)
    p.tt(c1, c1, sgn, ALU.mult)
    c13 = c1.rearrange("p (a b) -> p a b", b=16)
    o3 = out_tile.rearrange("p (a b) -> p a b", b=16)
    p.tt(o3, c13, bm.unsqueeze(2).broadcast_to([128, nb, 16]), ALU.mult, chain_ok=True)



I_FF = 2816
NI = I_FF // 128


def emit_l1_ffn(p, X, wsl_d, wdsl_d, tag="f1"):
    qx_dr = p.dram(f"{tag}_qx", [S, D], BF16)
    qh_dr = p.dram(f"{tag}_qh", [S, I_FF], BF16)
    wq_dr = p.dram(f"{tag}_wq", [8 * 2 * I_FF, 128], BF16)
    wqin_dr = p.dram(f"{tag}_wqi", [2 * I_FF, 128], BF16)
    wdq_dr = p.dram(f"{tag}_wdq", [8 * 128, I_FF], BF16)
    wdqin_dr = p.dram(f"{tag}_wdqi", [128, I_FF], BF16)
    DBG.update(qx=qx_dr, qh=qh_dr, wq=wq_dr, wdq=wdq_dr)

    with p.scope() as outer:
        scr = {
            "sq": p.sbuf("sq", [128, D], F32, scope=outer),
            "st": p.sbuf("st", [128, 8], F32, scope=outer),
            "a": p.sbuf("qa", [128, D], F32, scope=outer),
            "b": p.sbuf("qb", [128, D], F32, scope=outer),
            "c2": p.sbuf("qc", [128, D], F32, scope=outer),
            "sg4": p.sbuf("qsg", [128, D], F32, scope=outer),
            "bm": p.sbuf("qbm", [128, D // 16], F32, scope=outer),
            "br": p.sbuf("qbr", [128, D // 16], F32, scope=outer),
        }
        emit_l1_ffn_body(p, X, wsl_d, wdsl_d, scr, qx_dr, qh_dr, wq_dr, wqin_dr, wdq_dr, wdqin_dr)


def emit_l1_ffn_body(p, X, wsl_d, wdsl_d, scr, qx_dr, qh_dr, wq_dr, wqin_dr, wdq_dr, wdqin_dr):
    with p.scope() as sc:
        wtile = [p.sbuf(f"wt{i}", [128, 1024], F32, scope=sc) for i in range(2)]
        wqt = [p.sbuf(f"wqt{i}", [128, 1024], BF16, scope=sc) for i in range(2)]
        jobs = []
        for t in range(2 * NI):
            r0 = 128 * t
            jobs.append((wsl_d[r0 : r0 + 128, :], wqin_dr[r0 : r0 + 128, :], 128))
        for j in range(4):
            c0 = 704 * j
            jobs.append((wdsl_d[:, c0 : c0 + 704], wdqin_dr[:, c0 : c0 + 704], 704))
        with p.par("wq"):
            p.dma(wtile[0][:, 0 : jobs[0][2]], jobs[0][0])
        for i, (srcap, dstap, ncol) in enumerate(jobs):
            b = i % 2
            with p.par("wq"):
                if i + 1 < len(jobs):
                    p.dma(wtile[1 - b][:, 0 : jobs[i + 1][2]], jobs[i + 1][0])
                if i > 0:
                    p.dma(dst_prev, wqt_prev, eng="gpsimd")
            emit_fp4_quant(p, wtile[b][:, 0:ncol], wqt[b][:, 0:ncol], scr, ncol)
            dst_prev, wqt_prev = dstap, wqt[b][:, 0:ncol]
        with p.par("wq"):
            p.dma(dst_prev, wqt_prev, eng="gpsimd")
        p.collective("AllGather", ALU.bypass, [list(range(8))], [wqin_dr.ap()], [wq_dr.ap()])
        p.collective("AllGather", ALU.bypass, [list(range(8))], [wdqin_dr.ap()], [wdq_dr.ap()])

        xn = p.sbuf("xn", [128, D], F32, scope=sc)
        qx = [p.sbuf(f"qx{i}", [128, D], BF16, scope=sc) for i in range(2)]
        for t in range(NT):
            emit_plain_ln(p, X[t][:], xn[:], scr)
            emit_fp4_quant(p, xn[:], qx[t % 2][:], scr, D)
            with p.par("lnq"):
                p.dma(qx_dr[128 * t : 128 * (t + 1), :], qx[t % 2][:])

    with p.scope() as sc:
        qxg = p.sbuf("qxg", [128, DCH * 512], BF16, scope=sc)
        wgt = [p.sbuf(f"wgt{i}", [128, 512], BF16, scope=sc) for i in range(2)]
        wut = [p.sbuf(f"wut{i}", [128, 512], BF16, scope=sc) for i in range(2)]
        sg = p.sbuf("sg", [128, 512], F32, scope=sc)
        hb = p.sbuf("hb", [128, 512], F32, scope=sc)
        qh = p.sbuf("qh", [128, 512], BF16, scope=sc)
        psg = [p.psum(f"psg{i}", [128, 512], scope=sc) for i in range(4)]
        psu = [p.psum(f"psu{i}", [128, 512], scope=sc) for i in range(4)]
        ICH = [512, 512, 512, 512, 512, 256]

        def ldw(ic, k, b):
            base = 2 * I_FF * k
            i0 = 512 * ic
            iw = ICH[ic]
            p.dmat(wgt[b][:, 0:iw], wq_dr[base + i0 : base + i0 + iw, :])
            p.dmat(wut[b][:, 0:iw], wq_dr[base + I_FF + i0 : base + I_FF + i0 + iw, :])

        for g in range(4):
            g0 = 512 * g
            with p.par("gu"):
                for k in range(DCH):
                    p.dmat(qxg[:, 512 * k : 512 * (k + 1)], qx_dr[g0 : g0 + 512, 128 * k : 128 * (k + 1)])
                ldw(0, 0, 0)
            it = 0
            for ic in range(6):
                i0 = 512 * ic
                iw = ICH[ic]
                for k in range(DCH):
                    b = it % 2
                    with p.par("gu"):
                        if not (ic == 5 and k == DCH - 1):
                            nic, nk = (ic, k + 1) if k < DCH - 1 else (ic + 1, 0)
                            ldw(nic, nk, 1 - b)
                        for tt in range(4):
                            lhsT = qxg[:, 512 * k + 128 * tt : 512 * k + 128 * (tt + 1)]
                            p.mm(psg[tt][:, 0:iw], lhsT, wgt[b][:, 0:iw], start=(k == 0), stop=(k == DCH - 1))
                            p.mm(psu[tt][:, 0:iw], lhsT, wut[b][:, 0:iw], start=(k == 0), stop=(k == DCH - 1))
                    it += 1
                for tt in range(4):
                    p.act(sg[:, 0:iw], psg[tt][:, 0:iw], AF.Silu)
                    p.tt(hb[:, 0:iw], sg[:, 0:iw], psu[tt][:, 0:iw], ALU.mult)
                    emit_fp4_quant(p, hb[:, 0:iw], qh[:, 0:iw], scr, iw)
                    p.dma(qh_dr[g0 + 128 * tt : g0 + 128 * (tt + 1), i0 : i0 + iw], qh[:, 0:iw])

    with p.scope() as sc:
        qhg = p.sbuf("qhg", [128, NI * 512], BF16, scope=sc)
        wdt = [p.sbuf(f"wdt{i}", [128, D], BF16, scope=sc) for i in range(2)]
        psd = [p.psum(f"psd{i}", [128, 512], scope=sc) for i in range(8)]

        def ldd(k, b):
            p.dmat(wdt[b][:, 0:512], wdq_dr[0:512, 128 * k : 128 * (k + 1)])
            p.dmat(wdt[b][:, 512:1024], wdq_dr[512:1024, 128 * k : 128 * (k + 1)])

        for g in range(4):
            g0 = 512 * g
            with p.par("dn"):
                for k in range(NI):
                    p.dmat(qhg[:, 512 * k : 512 * (k + 1)], qh_dr[g0 : g0 + 512, 128 * k : 128 * (k + 1)])
                ldd(0, 0)
            for k in range(NI):
                b = k % 2
                with p.par("dn"):
                    if k < NI - 1:
                        ldd(k + 1, 1 - b)
                    for tt in range(4):
                        lhsT = qhg[:, 512 * k + 128 * tt : 512 * k + 128 * (tt + 1)]
                        for hc in range(2):
                            p.mm(
                                psd[2 * tt + hc][:],
                                lhsT,
                                wdt[b][:, 512 * hc : 512 * (hc + 1)],
                                start=(k == 0),
                                stop=(k == NI - 1),
                            )
            for tt in range(4):
                t = 4 * g + tt
                for hc in range(2):
                    h0 = 512 * hc
                    p.tt(X[t][:, h0 : h0 + 512], X[t][:, h0 : h0 + 512], psd[2 * tt + hc][:], ALU.add)




def emit_int8_wquant(p, w_tile, out_tile, sc_col, scratch, nrow, ncol):
    a = scratch["a"]
    r = scratch["st"][:, 4:5]
    p.red(sc_col, w_tile, ALU.max, absval=True)
    p.ts(sc_col, sc_col, 1.0 / 127.0, 1e-30, ALU.mult, ALU.max)
    p.recip(r[0:nrow, :], sc_col)
    p.ts(a[0:nrow, 0:ncol], w_tile, r[0:nrow, :], None, ALU.mult)
    p.ts(a[0:nrow, 0:ncol], a[0:nrow, 0:ncol], MAGIC, -MAGIC, ALU.add, ALU.add)
    p.ts(out_tile, a[0:nrow, 0:ncol], 127.0, -128.0, ALU.min, ALU.max)


def emit_partition_gmax(p, mxcol, mx_dr, ar_in, ar_out, row16, tag):
    p.dma(mx_dr.ap().rearrange("a b -> b a"), mxcol)
    p.memset(row16[0:1, :], 0.0)
    p.dma(row16[0:1, 8:9], mx_dr[0:1, 0:1])
    pass


def emit_l0_ffn(p, X, wisl_d, wdsl_d, clipx_d, clipd_d, tag="g0"):
    qx_dr = p.dram(f"{tag}_qx", [S, D], BF16)
    h_dr = p.dram(f"{tag}_h", [S, I_FF], BF16)
    wq_dr = p.dram(f"{tag}_wq", [2 * I_FF, D], BF16)
    win_dr = p.dram(f"{tag}_wi", [704, D], BF16)
    wdq_dr = p.dram(f"{tag}_wdq", [D, I_FF], BF16)
    wdin_dr = p.dram(f"{tag}_wdi", [128, I_FF], BF16)
    sc_dr = p.dram(f"{tag}_sc", [2 * I_FF, 1], F32)
    scin_dr = p.dram(f"{tag}_sci", [704, 1], F32)
    scd_dr = p.dram(f"{tag}_scd", [D, 1], F32)
    scdin_dr = p.dram(f"{tag}_scdi", [128, 1], F32)
    mx_dr = p.dram(f"{tag}_mx", [1, 128], F32)
    arx_in = p.dram(f"{tag}_axi", [1, 16], F32)
    arx_out = p.dram(f"{tag}_axo", [1, 16], F32)
    arh_in = p.dram(f"{tag}_ahi", [1, 16], F32)
    arh_out = p.dram(f"{tag}_aho", [1, 16], F32)
    scal_dr = p.dram(f"{tag}_scal", [1, 8], F32)
    G8 = [list(range(8))]

    with p.scope() as outer:
        scr = {
            "sq": p.sbuf("sq", [128, D], F32, scope=outer),
            "st": p.sbuf("st", [128, 8], F32, scope=outer),
            "a": p.sbuf("qa", [128, I_FF], F32, scope=outer),
        }
        row16 = p.sbuf("row16", [1, 16], F32, scope=outer)
        scalrow = p.sbuf("scalrow", [1, 8], F32, scope=outer)
        colbc = p.sbuf("colbc", [128, 4], F32, scope=outer)

        with p.scope() as sc:
            wt = [p.sbuf(f"wt{i}", [128, I_FF], F32, scope=sc) for i in range(2)]
            wq = [p.sbuf(f"wq{i}", [128, I_FF], BF16, scope=sc) for i in range(2)]
            scol = [p.sbuf(f"scol{i}", [128, 1], F32, scope=sc) for i in range(2)]
            rows = [128, 128, 128, 128, 128, 64]
            jobs = []
            r0 = 0
            for nr in rows:
                jobs.append((wisl_d[r0 : r0 + nr, :], win_dr[r0 : r0 + nr, :], scin_dr[r0 : r0 + nr, :], nr, D))
                r0 += nr
            jobs.append((wdsl_d[:, :], wdin_dr[:, :], scdin_dr[:, :], 128, I_FF))
            with p.par():
                p.dma(wt[0][0 : jobs[0][3], 0 : jobs[0][4]], jobs[0][0])
            prev = None
            for i, (srcap, dstap, scap, nr, ncol) in enumerate(jobs):
                b = i % 2
                with p.par():
                    if i + 1 < len(jobs):
                        p.dma(wt[1 - b][0 : jobs[i + 1][3], 0 : jobs[i + 1][4]], jobs[i + 1][0])
                    if prev is not None:
                        p.dma(prev[0], prev[1], eng="gpsimd")
                        p.dma(prev[2], prev[3], eng="gpsimd")
                emit_int8_wquant(p, wt[b][0:nr, 0:ncol], wq[b][0:nr, 0:ncol], scol[b][0:nr, :], scr, nr, ncol)
                prev = (dstap, wq[b][0:nr, 0:ncol], scap, scol[b][0:nr, :])
            with p.par():
                p.dma(prev[0], prev[1], eng="gpsimd")
                p.dma(prev[2], prev[3], eng="gpsimd")
            p.collective("AllGather", ALU.bypass, G8, [win_dr.ap()], [wq_dr.ap()])
            p.collective("AllGather", ALU.bypass, G8, [scin_dr.ap()], [sc_dr.ap()])
            p.collective("AllGather", ALU.bypass, G8, [wdin_dr.ap()], [wdq_dr.ap()])
            p.collective("AllGather", ALU.bypass, G8, [scdin_dr.ap()], [scd_dr.ap()])

        XN = [p.sbuf(f"xn{t}", [128, D], F32, scope=outer) for t in range(NT)]
        mxcol = p.sbuf("mxcol", [128, 1], F32, scope=outer)
        tmpc = p.sbuf("tmpc", [128, 1], F32, scope=outer)
        p.memset(mxcol[:], 0.0)
        for t in range(NT):
            emit_plain_ln(p, X[t][:], XN[t][:], scr)
            p.red(tmpc[:], XN[t][:], ALU.max, absval=True)
            p.tt(mxcol[:], mxcol[:], tmpc[:], ALU.max)
        p.dma(mx_dr.ap().rearrange("a b -> b a"), mxcol[:])
        p.memset(row16[0:1, :], 0.0)
        rowbuf = p.sbuf("rowbuf", [1, 128], F32, scope=outer)
        p.dma(rowbuf[0:1, :], mx_dr[0:1, :])
        p.red(row16[0:1, 0:1], rowbuf[0:1, :], ALU.max)
        p.dma(arx_in[0:1, :], row16[0:1, :])
        p.collective("AllReduce", ALU.max, G8, [arx_in.ap()], [arx_out.ap()])
        p.dma(row16[0:1, 0:16], arx_out[0:1, :])
        sig = scalrow[0:1, 4:5]
        p.dma(sig, clipx_d[0:1, 0:1])
        p.act(sig, sig, AF.Sigmoid)
        sxv, rsxv = scalrow[0:1, 0:1], scalrow[0:1, 1:2]
        p.tt(sxv, sig, row16[0:1, 0:1], ALU.mult)
        p.ts(sxv, sxv, 1.0 / 127.0, None, ALU.mult)
        p.recip(rsxv, sxv)
        p.dma(scal_dr[0:1, 0:2], scalrow[0:1, 0:2])
        p.dma(colbc[:, 0:1], scal_dr[0:1, 1:2].partition_broadcast(128))
        p.dma(colbc[:, 1:2], scal_dr[0:1, 0:1].partition_broadcast(128))

        with p.scope() as sc:
            q32 = p.sbuf("q32", [128, D], F32, scope=sc)
            qx = [p.sbuf(f"qx{i}", [128, D], BF16, scope=sc) for i in range(2)]
            for t in range(NT):
                p.ts(q32[:], XN[t][:], colbc[:, 0:1], None, ALU.mult)
                p.ts(q32[:], q32[:], MAGIC, -MAGIC, ALU.add, ALU.add, chain_ok=True)
                p.ts(qx[t % 2][:], q32[:], 127.0, -128.0, ALU.min, ALU.max, chain_ok=True)
                with p.par():
                    p.dma(qx_dr[128 * t : 128 * (t + 1), :], qx[t % 2][:])

        sgu_bc = p.sbuf("sgubc", [128, 2 * I_FF], F32, scope=outer)
        p.dma(sgu_bc[:], sc_dr.ap().rearrange("a b -> b a").partition_broadcast(128))
        p.ts(sgu_bc[:], sgu_bc[:], colbc[:, 1:2], None, ALU.mult)

        hmx = p.sbuf("hmx", [128, 1], F32, scope=outer)
        p.memset(hmx[:], 0.0)
        with p.scope() as sc:
            qxg = p.sbuf("qxg", [128, DCH * 512], BF16, scope=sc)
            wgt = [p.sbuf(f"wgt{i}", [128, 512], BF16, scope=sc) for i in range(2)]
            wut = [p.sbuf(f"wut{i}", [128, 512], BF16, scope=sc) for i in range(2)]
            gs = p.sbuf("gs", [128, 512], F32, scope=sc)
            us = p.sbuf("us", [128, 512], F32, scope=sc)
            hb = p.sbuf("hb", [128, 512], BF16, scope=sc)
            psg = [p.psum(f"psg{i}", [128, 512], scope=sc) for i in range(4)]
            psu = [p.psum(f"psu{i}", [128, 512], scope=sc) for i in range(4)]
            ICH = [512, 512, 512, 512, 512, 256]

            def ldw0(ic, k, b):
                i0 = 512 * ic
                iw = ICH[ic]
                p.dmat(wgt[b][:, 0:iw], wq_dr[i0 : i0 + iw, 128 * k : 128 * (k + 1)])
                p.dmat(wut[b][:, 0:iw], wq_dr[I_FF + i0 : I_FF + i0 + iw, 128 * k : 128 * (k + 1)])

            for g in range(4):
                g0 = 512 * g
                with p.par():
                    for k in range(DCH):
                        p.dmat(qxg[:, 512 * k : 512 * (k + 1)], qx_dr[g0 : g0 + 512, 128 * k : 128 * (k + 1)])
                    ldw0(0, 0, 0)
                it = 0
                for ic in range(6):
                    i0 = 512 * ic
                    iw = ICH[ic]
                    for k in range(DCH):
                        b = it % 2
                        with p.par():
                            if not (ic == 5 and k == DCH - 1):
                                nic, nk = (ic, k + 1) if k < DCH - 1 else (ic + 1, 0)
                                ldw0(nic, nk, 1 - b)
                            for tt in range(4):
                                lhsT = qxg[:, 512 * k + 128 * tt : 512 * k + 128 * (tt + 1)]
                                p.mm(psg[tt][:, 0:iw], lhsT, wgt[b][:, 0:iw], start=(k == 0), stop=(k == DCH - 1))
                                p.mm(psu[tt][:, 0:iw], lhsT, wut[b][:, 0:iw], start=(k == 0), stop=(k == DCH - 1))
                        it += 1
                    for tt in range(4):
                        p.tt(gs[:, 0:iw], psg[tt][:, 0:iw], sgu_bc[:, i0 : i0 + iw], ALU.mult)
                        p.act(gs[:, 0:iw], gs[:, 0:iw], AF.Silu)
                        p.tt(us[:, 0:iw], psu[tt][:, 0:iw], sgu_bc[:, I_FF + i0 : I_FF + i0 + iw], ALU.mult)
                        p.tt(us[:, 0:iw], gs[:, 0:iw], us[:, 0:iw], ALU.mult)
                        p.red(tmpc[:], us[:, 0:iw], ALU.max, absval=True)
                        p.tt(hmx[:], hmx[:], tmpc[:], ALU.max)
                        p.copy(hb[:, 0:iw], us[:, 0:iw])
                        p.dma(h_dr[g0 + 128 * tt : g0 + 128 * (tt + 1), i0 : i0 + iw], hb[:, 0:iw])

        p.dma(mx_dr.ap().rearrange("a b -> b a"), hmx[:])
        p.memset(row16[0:1, :], 0.0)
        p.dma(rowbuf[0:1, :], mx_dr[0:1, :])
        p.red(row16[0:1, 0:1], rowbuf[0:1, :], ALU.max)
        p.dma(arh_in[0:1, :], row16[0:1, :])
        p.collective("AllReduce", ALU.max, G8, [arh_in.ap()], [arh_out.ap()])
        p.dma(row16[0:1, 0:16], arh_out[0:1, :])
        p.dma(sig, clipd_d[0:1, 0:1])
        p.act(sig, sig, AF.Sigmoid)
        shv, rshv = scalrow[0:1, 2:3], scalrow[0:1, 3:4]
        p.tt(shv, sig, row16[0:1, 0:1], ALU.mult)
        p.ts(shv, shv, 1.0 / 127.0, None, ALU.mult)
        p.recip(rshv, shv)
        p.dma(scal_dr[0:1, 2:4], scalrow[0:1, 2:4])
        p.dma(colbc[:, 2:3], scal_dr[0:1, 3:4].partition_broadcast(128))
        p.dma(colbc[:, 3:4], scal_dr[0:1, 2:3].partition_broadcast(128))

        swd_bc = p.sbuf("swdbc", [128, D], F32, scope=outer)
        p.dma(swd_bc[:], scd_dr.ap().rearrange("a b -> b a").partition_broadcast(128))
        p.ts(swd_bc[:], swd_bc[:], colbc[:, 3:4], None, ALU.mult)

        with p.scope() as sc:
            qhg = p.sbuf("qhg", [128, NI * 512], BF16, scope=sc)
            hstg = [p.sbuf(f"hstg{i}", [128, 512], BF16, scope=sc) for i in range(2)]
            hq32 = p.sbuf("hq32", [128, 512], F32, scope=sc)
            wdt = [p.sbuf(f"wdt{i}", [128, D], BF16, scope=sc) for i in range(2)]
            psd = [p.psum(f"psd{i}", [128, 512], scope=sc) for i in range(8)]

            def ldd0(k, b):
                p.dmat(wdt[b][:, 0:512], wdq_dr[0:512, 128 * k : 128 * (k + 1)])
                p.dmat(wdt[b][:, 512:1024], wdq_dr[512:1024, 128 * k : 128 * (k + 1)])

            for g in range(4):
                g0 = 512 * g
                with p.par():
                    p.dmat(hstg[0][:], h_dr[g0 : g0 + 512, 0:128])
                for k in range(NI):
                    b = k % 2
                    with p.par():
                        if k < NI - 1:
                            p.dmat(hstg[1 - b][:], h_dr[g0 : g0 + 512, 128 * (k + 1) : 128 * (k + 2)])
                        p.ts(hq32[:], hstg[b][:], colbc[:, 2:3], None, ALU.mult)
                        p.ts(hq32[:], hq32[:], MAGIC, -MAGIC, ALU.add, ALU.add)
                        p.ts(qhg[:, 512 * k : 512 * (k + 1)], hq32[:], 127.0, -128.0, ALU.min, ALU.max)
                    if k == NI - 1:
                        with p.par():
                            ldd0(0, 0)
                for k in range(NI):
                    b = k % 2
                    with p.par():
                        if k < NI - 1:
                            ldd0(k + 1, 1 - b)
                        for tt in range(4):
                            lhsT = qhg[:, 512 * k + 128 * tt : 512 * k + 128 * (tt + 1)]
                            for hc in range(2):
                                p.mm(
                                    psd[2 * tt + hc][:],
                                    lhsT,
                                    wdt[b][:, 512 * hc : 512 * (hc + 1)],
                                    start=(k == 0),
                                    stop=(k == NI - 1),
                                )
                for tt in range(4):
                    t = 4 * g + tt
                    for hc in range(2):
                        h0 = 512 * hc
                        p.tt(hq32[:], psd[2 * tt + hc][:], swd_bc[:, h0 : h0 + 512], ALU.mult)
                        p.tt(X[t][:, h0 : h0 + 512], X[t][:, h0 : h0 + 512], hq32[:], ALU.add)



NH = 32
HP = 64
DIN = 2048
DSTATE = 128
CONVD = 2304
NCH = 16
DFF = 4096


def emit_affine_ln(p, x_tile, out_tile, scratch, lnw_bc, lnb_bc, u_scratch):
    aff = scratch["aff"][:, 0 : x_tile.shape[1]]
    emit_plain_ln(p, x_tile, u_scratch, scratch)
    emit_plain_ln(p, u_scratch, aff, scratch)
    p.tt(aff, aff, lnw_bc, ALU.mult)
    p.tt(out_tile, aff, lnb_bc, ALU.add)


def emit_l0_attn(
    p, X, inw_d, convw_d, convb_d, dtb_d, alog_d, dskip_d, rmsw_d, outw_d,
    w1_d, b1_d, w2_d, b2_d, lnw_d, lnb_d, xhalo_d, hmask_d, omask_d, tag="m0"
):
    xn_dr = p.dram(f"{tag}_xn", [S, D], BF16)
    z_dr = p.dram(f"{tag}_z", [S, DIN], BF16)
    xss_dr = p.dram(f"{tag}_xss", [DIN, S], BF16)
    caf_dr = p.dram(f"{tag}_caf", [NCH, NH * 128], F32)
    ncl_dr = p.dram(f"{tag}_ncl", [NCH, NH * 128], F32)
    lam_dr = p.dram(f"{tag}_lam", [NH, S], BF16)
    lamtot_dr = p.dram(f"{tag}_lamt", [NH, S], BF16)
    etm_dr = p.dram(f"{tag}_etm", [NH, S], BF16)
    easum_dr = p.dram(f"{tag}_eas", [NCH, NH], F32)
    yraw_dr = p.dram(f"{tag}_yraw", [S, DIN], BF16)
    y4_dr = p.dram(f"{tag}_y4", [S, DIN], BF16)
    sfin_dr = p.dram(f"{tag}_sfin", [128, DIN], F32)
    sag_dr = p.dram(f"{tag}_sag", [256, DIN], F32)
    xn2_dr = p.dram(f"{tag}_xn2", [S, D], BF16)
    h1_dr = p.dram(f"{tag}_h1", [S, DFF], BF16)
    PAIRS = [[0, 1], [2, 3], [4, 5], [6, 7]]

    with p.scope() as outer:
        scr = {
            "sq": p.sbuf("sq", [128, D], F32, scope=outer),
            "st": p.sbuf("st", [128, 8], F32, scope=outer),
            "aff": p.sbuf("aff", [128, D], F32, scope=outer),
        }
        uscr = p.sbuf("uscr", [128, D], F32, scope=outer)
        lnw_bc = p.sbuf("lnwbc", [128, D], F32, scope=outer)
        lnb_bc = p.sbuf("lnbbc", [128, D], F32, scope=outer)
        ident = p.sbuf("ident", [128, 128], F32, scope=outer)
        p.dma(lnw_bc[:], lnw_d[0:1, :].partition_broadcast(128))
        p.dma(lnb_bc[:], lnb_d[0:1, :].partition_broadcast(128))
        p.memset(ident[:], 0.0)
        p.affsel(ident[:], ident[:], [[-1, 128]], ALU.not_equal, 1.0, 0, 1)

        with p.scope() as sc:
            xnb = p.sbuf("xnb", [128, D], BF16, scope=sc)
            for t in range(NT):
                emit_affine_ln(p, X[t][:], xnb[:], scr, lnw_bc[:], lnb_bc[:], uscr[:])
                p.dma(xn_dr[128 * t : 128 * (t + 1), :], xnb[:])
        xh3f = p.sbuf("xh3f", [128, DCH * 4], BF16, scope=outer)
        with p.scope() as sc:
            xh = p.sbuf("xh", [128, D], F32, scope=sc)
            xnh = p.sbuf("xnh", [128, D], F32, scope=sc)
            pt = p.psum("pt", [128, 128], scope=sc)
            p.memset(xh[:], 0.0)
            p.dma(xh[0:3, :], xhalo_d[0:3, :])
            emit_affine_ln(p, xh[:], xnh[:], scr, lnw_bc[:], lnb_bc[:], uscr[:])
            for k in range(DCH):
                p.transpose(pt[:], xnh[:, 128 * k : 128 * (k + 1)], ident[:])
                p.copy(xh3f[:, 4 * k : 4 * k + 3], pt[:, 0:3])

        BM = p.sbuf("Bfm", [128, S], BF16, scope=outer)
        CM = p.sbuf("Cfm", [128, S], BF16, scope=outer)
        dttm = p.sbuf("dttm", [128, NCH * NH], F32, scope=outer)
        haloxbc = p.sbuf("haloxbc", [128, 18 * 3], BF16, scope=outer)
        with p.scope() as sc:
            xnf = p.sbuf("xnf", [128, DCH * S], BF16, scope=sc)
            for k in range(DCH):
                p.dmat(xnf[:, S * k : S * (k + 1)], xn_dr[:, 128 * k : 128 * (k + 1)])
            sc2 = p.scope()
            sc2.__enter__()
            wt = [p.sbuf(f"wt{i}", [128, 512], BF16, scope=sc2) for i in range(2)]
            zb = [p.sbuf(f"zb{i}", [128, 512], BF16, scope=sc2) for i in range(4)]
            psz = [p.psum(f"psz{i}", [128, 512], scope=sc2) for i in range(4)]
            def ldz(zc, k, b):
                p.dma(wt[b][:], inw_d[128 * k : 128 * (k + 1), 512 * zc : 512 * (zc + 1)], eng="gpsimd")

            for g in range(4):
                with p.par():
                    ldz(0, 0, 0)
                it = 0
                for zc in range(4):
                    for k in range(DCH):
                        b = it % 2
                        with p.par():
                            if not (zc == 3 and k == DCH - 1):
                                nzc, nk = (zc, k + 1) if k < DCH - 1 else (zc + 1, 0)
                                ldz(nzc, nk, 1 - b)
                            for tt in range(4):
                                lhsT = xnf[:, S * k + 512 * g + 128 * tt : S * k + 512 * g + 128 * (tt + 1)]
                                p.mm(psz[tt][:], lhsT, wt[b][:], start=(k == 0), stop=(k == DCH - 1))
                        it += 1
                    with p.par():
                        for tt in range(4):
                            p.copy(zb[tt][:], psz[tt][:])
                    with p.par():
                        for tt in range(4):
                            p.dma(z_dr[512 * g + 128 * tt : 512 * g + 128 * (tt + 1), 512 * zc : 512 * (zc + 1)], zb[tt][:])
            sc2.__exit__(None, None, None)
            sc2 = p.scope()
            sc2.__enter__()
            dtbias_bc = p.sbuf("dtbbc", [128, NH], F32, scope=sc2)
            p.dma(dtbias_bc[:], dtb_d[0:1, :].partition_broadcast(128))
            wdt = p.sbuf("wdt", [128, DCH * NH], BF16, scope=sc2)
            psd = p.psum("psdt", [128, NH], scope=sc2)
            d1 = p.sbuf("d1", [128, NH], F32, scope=sc2)
            d2 = p.sbuf("d2", [128, NH], F32, scope=sc2)
            for k in range(DCH):
                p.dma(wdt[:, NH * k : NH * (k + 1)], inw_d[128 * k : 128 * (k + 1), 4352:4384], eng="gpsimd")
            for tc in range(NT):
                for k in range(DCH):
                    lhsT = xnf[:, S * k + 128 * tc : S * k + 128 * (tc + 1)]
                    p.mm(psd[:], lhsT, wdt[:, NH * k : NH * (k + 1)], start=(k == 0), stop=(k == DCH - 1))
                p.tt(d1[:], psd[:], dtbias_bc[:], ALU.add)
                p.act(d2[:], d1[:], AF.Abs)
                p.act(d2[:], d2[:], AF.Exp, scale=-1.0)
                p.act(d2[:], d2[:], AF.Ln, bias=1.0)
                p.ts(d1[:], d1[:], 0.0, None, ALU.max)
                p.tt(dttm[:, NH * tc : NH * (tc + 1)], d1[:], d2[:], ALU.add)
            sc2.__exit__(None, None, None)
            convw = p.sbuf("convw", [128, 18 * 4], F32, scope=sc)
            convb = p.sbuf("convb", [128, 18], F32, scope=sc)
            p.dma(convw[:].rearrange("p (f k) -> p f k", k=4), convw_d.ap().rearrange("(f p) k -> p f k", p=128))
            p.dma(convb[:].rearrange("p (a f) -> p a f", a=1), convb_d.ap().rearrange("a (f p) -> p a f", p=128), allow_slow_non_contiguous=True)
            hmcol = p.sbuf("hmcol", [128, 1], F32, scope=sc)
            p.dma(hmcol[:], hmask_d[0:1, 0:1].partition_broadcast(128))
            wtile = [p.sbuf(f"wtile{i}", [128, 128], BF16, scope=sc) for i in range(2)]
            xbch = p.sbuf("xbch", [128, S + 4], F32, scope=sc)
            cacc = p.sbuf("cacc", [128, S], F32, scope=sc)
            sil = p.sbuf("sil", [128, S], BF16, scope=sc)
            psx = [p.psum(f"psx{i}", [128, 512], scope=sc) for i in range(4)]
            psh = p.psum("psh", [128, 16], scope=sc)
            def ldxbc(fc, k, b):
                col0 = 2048 + 128 * fc
                p.dma(wtile[b][:], inw_d[128 * k : 128 * (k + 1), col0 : col0 + 128], eng="gpsimd")

            with p.par():
                ldxbc(0, 0, 0)
            itx = 0
            for fc in range(18):
                for k in range(DCH):
                    b = itx % 2
                    with p.par():
                        if not (fc == 17 and k == DCH - 1):
                            nfc, nk = (fc, k + 1) if k < DCH - 1 else (fc + 1, 0)
                            ldxbc(nfc, nk, 1 - b)
                        for g in range(4):
                            p.mm(psx[g][:], wtile[b][:], xnf[:, S * k + 512 * g : S * k + 512 * (g + 1)], start=(k == 0), stop=(k == DCH - 1))
                        p.mm(psh[:, 0:3], wtile[b][:], xh3f[:, 4 * k : 4 * k + 3], start=(k == 0), stop=(k == DCH - 1))
                    itx += 1
                p.ts(xbch[:, 1:4], psh[:, 0:3], hmcol[:], None, ALU.mult)
                for g in range(4):
                    p.copy(xbch[:, 4 + 512 * g : 4 + 512 * (g + 1)], psx[g][:])
                p.copy(haloxbc[:, 3 * fc : 3 * fc + 3], xbch[:, 1:4])
                p.ts(cacc[:], xbch[:, 1 : 1 + S], convw[:, 4 * fc : 4 * fc + 1], None, ALU.mult)
                for k in range(1, 4):
                    p.stt(cacc[:], xbch[:, 1 + k : 1 + k + S], convw[:, 4 * fc + k : 4 * fc + k + 1], cacc[:], ALU.mult, ALU.add)
                p.act(sil[:], cacc[:], AF.Silu, bias=convb[:, fc : fc + 1])
                if fc < 16:
                    p.dma(xss_dr[128 * fc : 128 * (fc + 1), :], sil[:])
                elif fc == 16:
                    p.copy(BM[:], sil[:])
                else:
                    p.copy(CM[:], sil[:])

        with p.scope() as sc:
            dtfm = p.sbuf("dtfm", [NH, S], F32, scope=sc)
            pt = p.psum("ptd", [128, 128], scope=sc)
            for tc in range(NT):
                p.transpose(pt[0:NH, 0:128], dttm[:, NH * tc : NH * (tc + 1)], ident[:])
                p.copy(dtfm[0:NH, 128 * tc : 128 * (tc + 1)], pt[0:NH, 0:128])
            acol = p.sbuf("acol", [NH, 1], F32, scope=sc)
            p.dma(acol[0:NH, :], alog_d.ap().rearrange("a b -> b a"), allow_slow_non_contiguous=True)
            p.act(acol[0:NH, :], acol[0:NH, :], AF.Exp)
            p.ts(acol[0:NH, :], acol[0:NH, :], -1.0, None, ALU.mult)
            afm = p.sbuf("afm", [NH, S], F32, scope=sc)
            p.ts(afm[0:NH, :], dtfm[0:NH, :], acol[0:NH, :], None, ALU.mult)
            zrow = p.sbuf("zrow", [NH, 128], F32, scope=sc)
            p.memset(zrow[0:NH, :], 0.0)
            cafm = p.sbuf("cafm", [NH, S], F32, scope=sc)
            for c in range(NCH):
                p.scan(cafm[0:NH, 128 * c : 128 * (c + 1)], afm[0:NH, 128 * c : 128 * (c + 1)], zrow[0:NH, :], 0.0, ALU.add, ALU.add)
            zrow2 = p.sbuf("zrow2", [NH, S], F32, scope=sc)
            p.memset(zrow2[0:NH, :], 0.0)
            catot = p.sbuf("catot", [NH, S], F32, scope=sc)
            p.scan(catot[0:NH, :], afm[0:NH, :], zrow2[0:NH, :], 0.0, ALU.add, ALU.add)
            brow = p.sbuf("brow", [NH, S], BF16, scope=sc)
            p.act(brow[0:NH, :], cafm[0:NH, :], AF.Exp)
            p.dma(lam_dr[0:NH, :], brow[0:NH, :])
            p.act(brow[0:NH, :], catot[0:NH, :], AF.Exp)
            p.dma(lamtot_dr[0:NH, :], brow[0:NH, :])
            asumb = p.sbuf("asumb", [NH, NCH], F32, scope=sc)
            for c in range(NCH):
                p.copy(asumb[0:NH, c : c + 1], cafm[0:NH, 128 * c + 127 : 128 * (c + 1)])
            efm = p.sbuf("efm", [NH, S], F32, scope=sc)
            p.tt(
                efm[0:NH, :].rearrange("h (c t) -> h c t", t=128),
                asumb[0:NH, :].unsqueeze(2).broadcast_to([NH, NCH, 128]),
                cafm[0:NH, :].rearrange("h (c t) -> h c t", t=128),
                ALU.subtract,
            )
            p.act(brow[0:NH, :], efm[0:NH, :], AF.Exp)
            p.dma(etm_dr[0:NH, :], brow[0:NH, :])
            p.act(asumb[0:NH, :], asumb[0:NH, :], AF.Exp)
            p.dma(easum_dr.ap().rearrange("c h -> h c"), asumb[0:NH, :], allow_slow_non_contiguous=True)
            nclfm = p.sbuf("nclfm", [NH, S], F32, scope=sc)
            p.ts(nclfm[0:NH, :], cafm[0:NH, :], -1.0, None, ALU.mult)
            for c in range(NCH):
                p.dma(caf_dr[c : c + 1, :].rearrange("a (h t) -> (a h) t", t=128), cafm[0:NH, 128 * c : 128 * (c + 1)])
                p.dma(ncl_dr[c : c + 1, :].rearrange("a (h t) -> (a h) t", t=128), nclfm[0:NH, 128 * c : 128 * (c + 1)])

        maskDT = p.sbuf("maskDT", [128, 128], F32, scope=outer)
        p.memset(maskDT[:], 0.0)
        p.affsel(maskDT[:], maskDT[:], [[1, 128]], ALU.is_ge, -1e30, 0, -1)
        dskip_bc = p.sbuf("dskipbc", [128, NH], F32, scope=outer)
        p.dma(dskip_bc[:], dskip_d[0:1, :].partition_broadcast(128))
        onesr = p.sbuf("onesr", [1, 128], F32, scope=outer)
        p.memset(onesr[0:1, :], 1.0)
        SST = p.sbuf("SST", [128, DIN], F32, scope=outer)
        p.memset(SST[:], 0.0)
        with p.scope() as sc:
            Sb = p.sbuf("Sb", [128, DIN], BF16, scope=sc)
            Xtm = p.sbuf("Xtm", [128, DIN], BF16, scope=sc)
            Xd = p.sbuf("Xd", [128, DIN], BF16, scope=sc)
            XdE = p.sbuf("XdE", [128, DIN], BF16, scope=sc)
            lamt = p.sbuf("lamt", [128, NH], BF16, scope=sc)
            etmt = p.sbuf("etmt", [128, NH], BF16, scope=sc)
            lambc = p.sbuf("lambc", [128, NH], F32, scope=sc)
            Gt = p.sbuf("Gt", [128, 128], BF16, scope=sc)
            Btm = p.sbuf("Btm", [128, 128], BF16, scope=sc)
            Bf32 = p.sbuf("Bf32", [128, 128], F32, scope=sc)
            carow = p.sbuf("carow", [1, NH * 128], F32, scope=sc)
            nclrow = p.sbuf("nclrow", [1, NH * 128], F32, scope=sc)
            Ws = p.sbuf("Ws", [128, DIN], F32, scope=sc)
            Wp = p.sbuf("Wp", [128, DIN], BF16, scope=sc)
            ybuf = p.sbuf("ybuf", [128, DIN], F32, scope=sc)
            ybuf2 = p.sbuf("ybuf2", [128, DIN], F32, scope=sc)
            yb16 = p.sbuf("yb16", [128, DIN], BF16, scope=sc)
            pd = p.psum("pd", [128, DIN], scope=sc)
            py = p.psum("py", [128, DIN], scope=sc)
            for c in range(NCH):
                t0 = 128 * c
                p.dmat(Xtm[:], xss_dr[:, t0 : t0 + 128])
                p.dmat(lamt[:, 0:NH], lam_dr[0:NH, t0 : t0 + 128])
                p.dmat(etmt[:, 0:NH], etm_dr[0:NH, t0 : t0 + 128])
                p.dma(lambc[:, 0:NH], easum_dr[c : c + 1, :].partition_broadcast(128))
                p.dma(carow[0:1, :], caf_dr[c : c + 1, :])
                p.dma(nclrow[0:1, :], ncl_dr[c : c + 1, :])
                x3 = Xtm[:].rearrange("p (h e) -> p h e", e=HP)
                p.tt(Xd[:].rearrange("p (h e) -> p h e", e=HP), x3, dttm[:, NH * c : NH * (c + 1)].unsqueeze(2).broadcast_to([128, NH, HP]), ALU.mult)
                p.tt(XdE[:].rearrange("p (h e) -> p h e", e=HP), Xd[:].rearrange("p (h e) -> p h e", e=HP), etmt[:, 0:NH].unsqueeze(2).broadcast_to([128, NH, HP]), ALU.mult)
                p.mm(pd[:, 0:128], BM[:, t0 : t0 + 128], CM[:, t0 : t0 + 128])
                p.copy(Gt[:], pd[:, 0:128])
                p.copy(Bf32[:], BM[:, t0 : t0 + 128])
                p.transpose(pd[:, 128:256], Bf32[:], ident[:])
                p.copy(Btm[:], pd[:, 128:256])
                for hh in range(2):
                    o0 = DIN * hh
                    for s4 in range(4):
                        p.mm(pd[:, 512 * s4 : 512 * (s4 + 1)], onesr[0:1, :], carow[0:1, 2048 * hh + 512 * s4 : 2048 * hh + 512 * (s4 + 1)], start=True, stop=False)
                    for j in range(16):
                        h = 16 * hh + j
                        p.mm(pd[:, 128 * j : 128 * (j + 1)], nclrow[0:1, 128 * h : 128 * (h + 1)], onesr[0:1, :], start=False, stop=True)
                    p.tt(
                        Ws[:, o0 : o0 + 2048].rearrange("p (j t) -> p j t", t=128) if False else Ws[:, 0:2048].rearrange("p (j t) -> p j t", t=128),
                        pd[:].rearrange("p (j t) -> p j t", t=128),
                        maskDT[:].unsqueeze(1).broadcast_to([128, 16, 128]),
                        ALU.add,
                    )
                    p.act(Wp[:, 0:2048], Ws[:, 0:2048], AF.Exp)
                    p.tt(
                        Wp[:, 0:2048].rearrange("p (j t) -> p j t", t=128),
                        Wp[:, 0:2048].rearrange("p (j t) -> p j t", t=128),
                        Gt[:].unsqueeze(1).broadcast_to([128, 16, 128]),
                        ALU.mult,
                    )
                    for j in range(16):
                        h = 16 * hh + j
                        p.mm(py[:, HP * h : HP * (h + 1)], Wp[:, 128 * j : 128 * (j + 1)], Xd[:, HP * h : HP * (h + 1)], start=True, stop=True)
                p.copy(Sb[:], SST[:])
                for h in range(NH):
                    p.mm(pd[:, HP * h : HP * (h + 1)], CM[:, t0 : t0 + 128], Sb[:, HP * h : HP * (h + 1)], start=True, stop=True)
                p.tt(ybuf[:].rearrange("p (h e) -> p h e", e=HP), pd[:].rearrange("p (h e) -> p h e", e=HP), lamt[:, 0:NH].unsqueeze(2).broadcast_to([128, NH, HP]), ALU.mult)
                p.tt(ybuf[:], ybuf[:], py[:], ALU.add)
                p.tt(ybuf2[:].rearrange("p (h e) -> p h e", e=HP), x3, dskip_bc[:].unsqueeze(2).broadcast_to([128, NH, HP]), ALU.mult)
                p.tt(yb16[:], ybuf[:], ybuf2[:], ALU.add)
                p.dma(yraw_dr[t0 : t0 + 128, :], yb16[:])
                p.ts(SST[:].rearrange("p (h e) -> p h e", e=HP), SST[:].rearrange("p (h e) -> p h e", e=HP), None, None, ALU.mult) if False else None
                p.tt(SST[:].rearrange("p (h e) -> p h e", e=HP), SST[:].rearrange("p (h e) -> p h e", e=HP), lambc[:, 0:NH].unsqueeze(2).broadcast_to([128, NH, HP]), ALU.mult)
                for h in range(NH):
                    p.mm(py[:, HP * h : HP * (h + 1)], Btm[:], XdE[:, HP * h : HP * (h + 1)], start=True, stop=True)
                p.tt(SST[:], SST[:], py[:], ALU.add)
            p.dma(sfin_dr[:, :], SST[:])
        p.collective("AllGather", ALU.bypass, PAIRS, [sfin_dr.ap()], [sag_dr.ap()])

        rms_bc = p.sbuf("rmsbc", [128, DIN], F32, scope=outer)
        p.dma(rms_bc[:], rmsw_d[0:1, :].partition_broadcast(128))
        with p.scope() as sc:
            omcol = p.sbuf("omcol", [128, 1], F32, scope=sc)
            p.dma(omcol[:], omask_d[0:1, 0:1].partition_broadcast(128))
            Sin = p.sbuf("Sin", [128, DIN], F32, scope=sc)
            Sinb = p.sbuf("Sinb", [128, DIN], BF16, scope=sc)
            p.dma(Sin[:], sag_dr[0:128, :])
            p.ts(Sin[:], Sin[:], omcol[:], None, ALU.mult)
            p.copy(Sinb[:], Sin[:])
            lamtt = p.sbuf("lamtt", [128, NH], BF16, scope=sc)
            yr = p.sbuf("yr", [128, DIN], BF16, scope=sc)
            zt = p.sbuf("zt", [128, DIN], BF16, scope=sc)
            zs = p.sbuf("zs", [128, DIN], F32, scope=sc)
            yv = p.sbuf("yv", [128, DIN], F32, scope=sc)
            yo = p.sbuf("yo", [128, DIN], BF16, scope=sc)
            rcol = p.sbuf("rcol", [128, 1], F32, scope=sc)
            sqs = p.sbuf("sqs", [128, DIN], F32, scope=sc)
            pz = p.psum("pz", [128, DIN], scope=sc)
            for c in range(NCH):
                t0 = 128 * c
                p.dmat(lamtt[:, 0:NH], lamtot_dr[0:NH, t0 : t0 + 128])
                p.dma(yr[:], yraw_dr[t0 : t0 + 128, :])
                p.dma(zt[:], z_dr[t0 : t0 + 128, :], eng="gpsimd")
                for h in range(NH):
                    p.mm(pz[:, HP * h : HP * (h + 1)], CM[:, t0 : t0 + 128], Sinb[:, HP * h : HP * (h + 1)], start=True, stop=True)
                p.tt(yv[:].rearrange("p (h e) -> p h e", e=HP), pz[:].rearrange("p (h e) -> p h e", e=HP), lamtt[:, 0:NH].unsqueeze(2).broadcast_to([128, NH, HP]), ALU.mult)
                p.tt(yv[:], yv[:], yr[:], ALU.add)
                p.act(zs[:], zt[:], AF.Silu)
                p.tt(yv[:], yv[:], zs[:], ALU.mult)
                p.memset(rcol[:], 0.0)
                p.act(sqs[:], yv[:], AF.Square, accum_out=rcol[:])
                p.ts(rcol[:], rcol[:], 1.0 / DIN, None, ALU.mult)
                p.act(rcol[:], rcol[:], AF.Sqrt, bias=p.constv(1e-5))
                p.recip(rcol[:], rcol[:])
                p.ts(yv[:], yv[:], rcol[:], None, ALU.mult)
                p.tt(yo[:], yv[:], rms_bc[:], ALU.mult)
                p.dma(y4_dr[t0 : t0 + 128, :], yo[:])

        out1_dr = p.dram(f"{tag}_out1", [S, D], F32)
        with p.scope() as sc:
            yf = p.sbuf("yf", [128, 16 * 512], BF16, scope=sc)
            wot = [p.sbuf(f"wot{i}", [128, D], BF16, scope=sc) for i in range(2)]
            o1t = p.sbuf("o1t", [128, D], F32, scope=sc)
            pso = [p.psum(f"pso{i}", [128, 512], scope=sc) for i in range(8)]

            def ldo(k, b):
                p.dma(wot[b][:], outw_d[128 * k : 128 * (k + 1), 0:1024], eng="gpsimd")

            for g in range(4):
                g0 = 512 * g
                with p.par():
                    for k in range(16):
                        p.dmat(yf[:, 512 * k : 512 * (k + 1)], y4_dr[g0 : g0 + 512, 128 * k : 128 * (k + 1)])
                    ldo(0, 0)
                for k in range(16):
                    b = k % 2
                    with p.par():
                        if k < 15:
                            ldo(k + 1, 1 - b)
                        for tt in range(4):
                            lhsT = yf[:, 512 * k + 128 * tt : 512 * k + 128 * (tt + 1)]
                            p.mm(pso[2 * tt][:], lhsT, wot[b][:, 0:512], start=(k == 0), stop=(k == 15))
                            p.mm(pso[2 * tt + 1][:], lhsT, wot[b][:, 512:1024], start=(k == 0), stop=(k == 15))
                for tt in range(4):
                    t = 4 * g + tt
                    emit_plain_ln(p, X[t][:], uscr[:], scr)
                    for hc in range(2):
                        p.tt(o1t[:, 512 * hc : 512 * (hc + 1)], pso[2 * tt + hc][:], uscr[:, 512 * hc : 512 * (hc + 1)], ALU.add)
                    with p.par():
                        p.dma(out1_dr[128 * t : 128 * (t + 1), :], o1t[:])
                        p.tt(X[t][:], X[t][:], o1t[:], ALU.add)

        with p.scope() as sc:
            xn2b = p.sbuf("xn2b", [128, D], BF16, scope=sc)
            o1l = p.sbuf("o1l", [128, D], F32, scope=sc)
            for t in range(NT):
                p.dma(o1l[:], out1_dr[128 * t : 128 * (t + 1), :])
                emit_affine_ln(p, o1l[:], xn2b[:], scr, lnw_bc[:], lnb_bc[:], uscr[:])
                p.dma(xn2_dr[128 * t : 128 * (t + 1), :], xn2b[:])
        with p.scope() as sc:
            xnf = p.sbuf("xnf2", [128, DCH * 512], BF16, scope=sc)
            wt = [p.sbuf(f"wt2{i}", [128, 512], BF16, scope=sc) for i in range(2)]
            b1bc = p.sbuf("b1bc", [128, DFF], F32, scope=sc)
            p.dma(b1bc[:], b1_d[0:1, :].partition_broadcast(128))
            hb = [p.sbuf(f"hb1{i}", [128, 512], F32, scope=sc) for i in range(4)]
            hbb = [p.sbuf(f"hbb1{i}", [128, 512], BF16, scope=sc) for i in range(4)]
            ps1 = [p.psum(f"ps1{i}", [128, 512], scope=sc) for i in range(4)]
            def ldw1(jc, k, b):
                p.dma(wt[b][:], w1_d[128 * k : 128 * (k + 1), 512 * jc : 512 * (jc + 1)], eng="gpsimd")

            for g in range(4):
                g0 = 512 * g
                with p.par():
                    for k in range(DCH):
                        p.dmat(xnf[:, 512 * k : 512 * (k + 1)], xn2_dr[g0 : g0 + 512, 128 * k : 128 * (k + 1)])
                    ldw1(0, 0, 0)
                it = 0
                for jc in range(8):
                    j0 = 512 * jc
                    for k in range(DCH):
                        b = it % 2
                        with p.par():
                            if not (jc == 7 and k == DCH - 1):
                                njc, nk = (jc, k + 1) if k < DCH - 1 else (jc + 1, 0)
                                ldw1(njc, nk, 1 - b)
                            for tt in range(4):
                                lhsT = xnf[:, 512 * k + 128 * tt : 512 * k + 128 * (tt + 1)]
                                p.mm(ps1[tt][:], lhsT, wt[b][:], start=(k == 0), stop=(k == DCH - 1))
                        it += 1
                    for tt in range(4):
                        p.tt(hb[tt][:], ps1[tt][:], b1bc[:, j0 : j0 + 512], ALU.add)
                        p.act(hbb[tt][:], hb[tt][:], AF.Relu, eng="scalar")
                    with p.par():
                        for tt in range(4):
                            p.dma(h1_dr[g0 + 128 * tt : g0 + 128 * (tt + 1), j0 : j0 + 512], hbb[tt][:])
        with p.scope() as sc:
            h1f = p.sbuf("h1f", [128, 32 * 512], BF16, scope=sc)
            wt = [p.sbuf(f"wt3{i}", [128, D], BF16, scope=sc) for i in range(2)]
            b2bc = p.sbuf("b2bc", [128, D], F32, scope=sc)
            p.dma(b2bc[:], b2_d[0:1, :].partition_broadcast(128))
            mo = p.sbuf("mo", [128, 512], F32, scope=sc)
            ps2 = [p.psum(f"ps2{i}", [128, 512], scope=sc) for i in range(8)]
            def ldw2(k, b):
                p.dma(wt[b][:], w2_d[128 * k : 128 * (k + 1), 0:1024], eng="gpsimd")

            for g in range(4):
                g0 = 512 * g
                with p.par():
                    for k in range(32):
                        p.dmat(h1f[:, 512 * k : 512 * (k + 1)], h1_dr[g0 : g0 + 512, 128 * k : 128 * (k + 1)])
                    ldw2(0, 0)
                for k in range(32):
                    b = k % 2
                    with p.par():
                        if k < 31:
                            ldw2(k + 1, 1 - b)
                        for tt in range(4):
                            lhsT = h1f[:, 512 * k + 128 * tt : 512 * k + 128 * (tt + 1)]
                            for hc in range(2):
                                p.mm(ps2[2 * tt + hc][:], lhsT, wt[b][:, 512 * hc : 512 * (hc + 1)], start=(k == 0), stop=(k == 31))
                for tt in range(4):
                    t = 4 * g + tt
                    for hc in range(2):
                        h0 = 512 * hc
                        p.tt(mo[:], ps2[2 * tt + hc][:], b2bc[:, h0 : h0 + 512], ALU.add)
                        p.tt(X[t][:, h0 : h0 + 512], X[t][:, h0 : h0 + 512], mo[:], ALU.add, chain_ok=True)


def _l0attn_host_inputs(x_full):
    outs = []
    for c in range(8):
        b, half = c // 2, c % 2
        if half == 1:
            halo = np.ascontiguousarray(x_full[b, 2045:2048])
            hm = 1.0
        else:
            halo = np.ascontiguousarray(x_full[b, 0:3])
            hm = 0.0
        outs.append((halo, np.array([[hm]], np.float32), np.array([[1.0 if half == 1 else 0.0]], np.float32)))
    return outs


def build_l0_attn_launch():
    p = Prog("l0attn")
    nc = p.nc
    x_d = nc.dram_tensor("x", [S, D], F32, kind="ExternalInput")
    inw_d = nc.dram_tensor("inw", [D, 4384], F32, kind="ExternalInput")
    convw_d = nc.dram_tensor("convw", [CONVD, 4], F32, kind="ExternalInput")
    convb_d = nc.dram_tensor("convb", [1, CONVD], F32, kind="ExternalInput")
    dtb_d = nc.dram_tensor("dtb", [1, NH], F32, kind="ExternalInput")
    alog_d = nc.dram_tensor("alog", [1, NH], F32, kind="ExternalInput")
    dskip_d = nc.dram_tensor("dskip", [1, NH], F32, kind="ExternalInput")
    rmsw_d = nc.dram_tensor("rmsw", [1, DIN], F32, kind="ExternalInput")
    outw_d = nc.dram_tensor("outw", [DIN, D], F32, kind="ExternalInput")
    w1_d = nc.dram_tensor("w1", [D, DFF], F32, kind="ExternalInput")
    b1_d = nc.dram_tensor("b1", [1, DFF], F32, kind="ExternalInput")
    w2_d = nc.dram_tensor("w2", [DFF, D], F32, kind="ExternalInput")
    b2_d = nc.dram_tensor("b2", [1, D], F32, kind="ExternalInput")
    lnw_d = nc.dram_tensor("lnw", [1, D], F32, kind="ExternalInput")
    lnb_d = nc.dram_tensor("lnb", [1, D], F32, kind="ExternalInput")
    xhalo_d = nc.dram_tensor("xhalo", [3, D], F32, kind="ExternalInput")
    hmask_d = nc.dram_tensor("hmask", [1, 1], F32, kind="ExternalInput")
    omask_d = nc.dram_tensor("omask", [1, 1], F32, kind="ExternalInput")
    y_d = nc.dram_tensor("y", [S, D], F32, kind="ExternalOutput")
    X = [p.sbuf(f"x{t}", [128, D], F32) for t in range(NT)]
    for t in range(NT):
        p.dma(X[t][:], x_d[128 * t : 128 * (t + 1), :])
    emit_l0_attn(
        p, X, inw_d, convw_d, convb_d, dtb_d, alog_d, dskip_d, rmsw_d, outw_d,
        w1_d, b1_d, w2_d, b2_d, lnw_d, lnb_d, xhalo_d, hmask_d, omask_d
    )
    for t in range(NT):
        p.dma(y_d[128 * t : 128 * (t + 1), :], X[t][:])
    return p.finish()



SEQ = 4096
NJC = SEQ // 128


def emit_l1_attn(p, X, wq_d, wk_d, wv_d, bq_d, bk_d, bv_d, thr_d, msel_d, tag="a1"):
    xn_dr = p.dram(f"{tag}_xn", [S, D], BF16)
    q_dr = p.dram(f"{tag}_q", [S, D], BF16)
    kv_in = p.dram(f"{tag}_kvi", [S, 2 * D], BF16)
    kv_dr = p.dram(f"{tag}_kv", [SEQ, 2 * D], BF16)
    e_dr = p.dram(f"{tag}_e", [SEQ, S], BF16)
    st_in = p.dram(f"{tag}_sti", [2 * 128, NJC], F32)
    st_out = p.dram(f"{tag}_sto", [4 * 128, NJC], F32)
    v2_dr = p.dram(f"{tag}_v2", [SEQ, D], BF16)
    PAIRS = [[0, 1], [2, 3], [4, 5], [6, 7]]

    with p.scope() as outer:
        scr = {
            "sq": p.sbuf("sq", [128, D], F32, scope=outer),
            "st": p.sbuf("st", [128, 8], F32, scope=outer),
        }
        with p.scope() as sc:
            xn = p.sbuf("xn", [128, D], F32, scope=sc)
            xnb = p.sbuf("xnb", [128, D], BF16, scope=sc)
            for t in range(NT):
                emit_plain_ln(p, X[t][:], xn[:], scr)
                p.tt(X[t][:], X[t][:], xn[:], ALU.add)
                p.copy(xnb[:], xn[:])
                p.dma(xn_dr[128 * t : 128 * (t + 1), :], xnb[:])

        with p.scope() as sc:
            xnf = p.sbuf("xnf", [128, DCH * 512], BF16, scope=sc)
            wt = [p.sbuf(f"wt{i}", [128, 512], BF16, scope=sc) for i in range(2)]
            bbc = p.sbuf("bbc", [128, 3 * D], F32, scope=sc)
            qstg = p.sbuf("qstg", [128, 4 * D], BF16, scope=sc)
            kvstg = p.sbuf("kvstg", [128, 4 * 2 * D], BF16, scope=sc)
            ps = [p.psum(f"pqkv{i}", [128, 512], scope=sc) for i in range(4)]
            p.dma(bbc[:, 0:D], bq_d[0:1, :].partition_broadcast(128))
            p.dma(bbc[:, D : 2 * D], bk_d[0:1, :].partition_broadcast(128))
            p.dma(bbc[:, 2 * D : 3 * D], bv_d[0:1, :].partition_broadcast(128))
            WL = [(wi, w_d, jh) for wi, w_d in enumerate((wq_d, wk_d, wv_d)) for jh in range(2)]

            def ldqkv(i, k, b):
                wi, w_d, jh = WL[i]
                p.dma(wt[b][:], w_d[128 * k : 128 * (k + 1), 512 * jh : 512 * jh + 512], eng="gpsimd")

            for g in range(4):
                g0 = 512 * g
                with p.par():
                    for k in range(DCH):
                        p.dmat(xnf[:, 512 * k : 512 * (k + 1)], xn_dr[g0 : g0 + 512, 128 * k : 128 * (k + 1)])
                    ldqkv(0, 0, 0)
                it = 0
                for i in range(6):
                    wi, w_d, jh = WL[i]
                    j0 = 512 * jh
                    for k in range(DCH):
                        b = it % 2
                        with p.par():
                            if not (i == 5 and k == DCH - 1):
                                ni, nk = (i, k + 1) if k < DCH - 1 else (i + 1, 0)
                                ldqkv(ni, nk, 1 - b)
                            for tt in range(4):
                                lhsT = xnf[:, 512 * k + 128 * tt : 512 * k + 128 * (tt + 1)]
                                p.mm(ps[tt][:], lhsT, wt[b][:], start=(k == 0), stop=(k == DCH - 1))
                        it += 1
                    with p.par():
                        for tt in range(4):
                            dst = (
                                qstg[:, D * tt + j0 : D * tt + j0 + 512]
                                if wi == 0
                                else kvstg[:, 2 * D * tt + (wi - 1) * D + j0 : 2 * D * tt + (wi - 1) * D + j0 + 512]
                            )
                            p.tt(dst, ps[tt][:], bbc[:, wi * D + j0 : wi * D + j0 + 512], ALU.add)
                with p.par():
                    for tt in range(4):
                        r0 = g0 + 128 * tt
                        p.dma(q_dr[r0 : r0 + 128, :], qstg[:, D * tt : D * (tt + 1)])
                        p.dma(kv_in[r0 : r0 + 128, :], kvstg[:, 2 * D * tt : 2 * D * (tt + 1)])
        p.collective("AllGather", ALU.bypass, PAIRS, [kv_in.ap()], [kv_dr.ap()])

        iotaT = p.sbuf("iotaT", [128, S], F32, scope=outer)
        p.iota(iotaT[:], [[1, S]], base=0, channel_multiplier=-1)
        thrbc = p.sbuf("thrbc", [128, NJC], F32, scope=outer)
        p.dma(thrbc[:], thr_d[0:1, :].partition_broadcast(128))
        with p.scope() as sc:
            qf = p.sbuf("qf", [128, DCH * S], BF16, scope=sc)
            kf = p.sbuf("kf", [128, DCH * SEQ], BF16, scope=sc)
            for dc in range(DCH):
                p.dmat(qf[:, S * dc : S * (dc + 1)], q_dr[:, 128 * dc : 128 * (dc + 1)])
                p.dmat(kf[:, SEQ * dc : SEQ * (dc + 1)], kv_dr[:, 128 * dc : 128 * (dc + 1)])
            sm = p.sbuf("sm", [128, S], F32, scope=sc)
            eb = p.sbuf("eb", [128, S], BF16, scope=sc)
            maskA = p.sbuf("maskA", [128, S], F32, scope=sc)
            mcol = p.sbuf("mcol", [128, 1], F32, scope=sc)
            nmcol = p.sbuf("nmcol", [128, 1], F32, scope=sc)
            dcol = p.sbuf("dcol", [128, 1], F32, scope=sc)
            pss = [p.psum(f"pss{i}", [128, 512], scope=sc) for i in range(4)]
            for jc in range(NJC):
                for it in range(4):
                    i0 = 512 * it
                    for dc in range(DCH):
                        lhsT = kf[:, SEQ * dc + 128 * jc : SEQ * dc + 128 * (jc + 1)]
                        rhs = qf[:, S * dc + i0 : S * dc + i0 + 512]
                        p.mm(pss[it][:], lhsT, rhs, start=(dc == 0), stop=(dc == DCH - 1))
                p.ts(maskA[:], iotaT[:, 0:S], thrbc[:, jc : jc + 1], -1e30, ALU.is_lt, ALU.mult)
                for it in range(4):
                    p.tt(sm[:, 512 * it : 512 * (it + 1)], pss[it][:], maskA[:, 512 * it : 512 * (it + 1)], ALU.add)
                p.red(mcol[:], sm[:], ALU.max)
                p.ts(nmcol[:], mcol[:], -1.0, None, ALU.mult)
                p.memset(dcol[:], 0.0)
                p.act(eb[:], sm[:], AF.Exp, bias=nmcol[:], accum_out=dcol[:])
                with p.par():
                    p.dma(e_dr[128 * jc : 128 * (jc + 1), :], eb[:])
                    p.dma(st_in[0:128, jc : jc + 1], mcol[:], allow_slow_non_contiguous=True, eng="gpsimd")
                    p.dma(st_in[128:256, jc : jc + 1], dcol[:], allow_slow_non_contiguous=True, eng="gpsimd")
        p.collective("AllGather", ALU.bypass, PAIRS, [st_in.ap()], [st_out.ap()])

        fmat = p.sbuf("fmat", [128, NJC], F32, scope=outer)
        with p.scope() as sc:
            me = p.sbuf("cme", [128, NJC], F32, scope=sc)
            de = p.sbuf("cde", [128, NJC], F32, scope=sc)
            mo = p.sbuf("cmo", [128, NJC], F32, scope=sc)
            do_ = p.sbuf("cdo", [128, NJC], F32, scope=sc)
            mm_ = p.sbuf("cmm", [128, NJC], F32, scope=sc)
            tmp = p.sbuf("ctmp", [128, NJC], F32, scope=sc)
            eme = p.sbuf("ceme", [128, NJC], F32, scope=sc)
            emo = p.sbuf("cemo", [128, NJC], F32, scope=sc)
            mselc = p.sbuf("mselc", [128, 1], F32, scope=sc)
            p.dma(me[:], st_out[0:128, :])
            p.dma(de[:], st_out[128:256, :])
            p.dma(mo[:], st_out[256:384, :])
            p.dma(do_[:], st_out[384:512, :])
            p.dma(mselc[:], msel_d[0:1, 0:1].partition_broadcast(128))
            p.tt(mm_[:], me[:], mo[:], ALU.max)
            p.tt(tmp[:], me[:], mm_[:], ALU.subtract)
            p.act(eme[:], tmp[:], AF.Exp)
            p.tt(tmp[:], mo[:], mm_[:], ALU.subtract)
            p.act(emo[:], tmp[:], AF.Exp)
            p.tt(de[:], de[:], eme[:], ALU.mult)
            p.tt(do_[:], do_[:], emo[:], ALU.mult)
            p.tt(de[:], de[:], do_[:], ALU.add)
            p.ts(tmp[:], eme[:], mselc[:], None, ALU.mult)
            p.ts(eme[:], emo[:], mselc[:], None, ALU.mult)
            p.tt(emo[:], emo[:], eme[:], ALU.subtract)
            p.tt(tmp[:], tmp[:], emo[:], ALU.add)
            p.ts(de[:], de[:], 32.0, None, ALU.mult)
            p.recip(de[:], de[:])
            p.tt(fmat[:], tmp[:], de[:], ALU.mult)
        with p.scope() as sc:
            vt = p.sbuf("vt", [128, D], BF16, scope=sc)
            for jc in range(NJC):
                p.dma(vt[:], kv_dr[128 * jc : 128 * (jc + 1), D : 2 * D])
                p.ts(vt[:], vt[:], fmat[:, jc : jc + 1], None, ALU.mult)
                p.dma(v2_dr[128 * jc : 128 * (jc + 1), :], vt[:])

        with p.scope() as sc:
            ers = [p.sbuf(f"ers{i}", [128, 512], BF16, scope=sc) for i in range(2)]
            vt = [p.sbuf(f"vt2{i}", [128, D], BF16, scope=sc) for i in range(2)]
            psr = [p.psum(f"psr{i}", [128, 512], scope=sc) for i in range(8)]

            def ldpv(ig, jc, b):
                p.dma(ers[b][:], e_dr[128 * jc : 128 * (jc + 1), 512 * ig : 512 * ig + 512])
                p.dma(vt[b][:], v2_dr[128 * jc : 128 * (jc + 1), :], eng="gpsimd")

            with p.par():
                ldpv(0, 0, 0)
            for ig in range(4):
                for jc in range(NJC):
                    b = jc % 2
                    with p.par():
                        if not (ig == 3 and jc == NJC - 1):
                            nig, njc = (ig, jc + 1) if jc < NJC - 1 else (ig + 1, 0)
                            ldpv(nig, njc, 1 - b)
                        for isub in range(4):
                            lhsT = ers[b][:, 128 * isub : 128 * (isub + 1)]
                            for vfh in range(2):
                                p.mm(
                                    psr[2 * isub + vfh][:],
                                    lhsT,
                                    vt[b][:, 512 * vfh : 512 * (vfh + 1)],
                                    start=(jc == 0),
                                    stop=(jc == NJC - 1),
                                )
                with p.par():
                    for isub in range(4):
                        t = 4 * ig + isub
                        for vfh in range(2):
                            h0 = 512 * vfh
                            p.tt(X[t][:, h0 : h0 + 512], X[t][:, h0 : h0 + 512], psr[2 * isub + vfh][:], ALU.add)


def _l1attn_host_inputs():
    outs = []
    for c in range(8):
        base = 2048 * (c % 2)
        thr = np.array([[128 * jc - base for jc in range(NJC)]], np.float32)
        msel = np.array([[1.0 if c % 2 == 0 else 0.0]], np.float32)
        outs.append((thr, msel))
    return outs


def build_l1_attn_launch():
    p = Prog("l1attn")
    nc = p.nc
    x_d = nc.dram_tensor("x", [S, D], F32, kind="ExternalInput")
    wq_d = nc.dram_tensor("wq", [D, D], F32, kind="ExternalInput")
    wk_d = nc.dram_tensor("wk", [D, D], F32, kind="ExternalInput")
    wv_d = nc.dram_tensor("wv", [D, D], F32, kind="ExternalInput")
    bq_d = nc.dram_tensor("bq", [1, D], F32, kind="ExternalInput")
    bk_d = nc.dram_tensor("bk", [1, D], F32, kind="ExternalInput")
    bv_d = nc.dram_tensor("bv", [1, D], F32, kind="ExternalInput")
    thr_d = nc.dram_tensor("thr", [1, NJC], F32, kind="ExternalInput")
    msel_d = nc.dram_tensor("msel", [1, 1], F32, kind="ExternalInput")
    y_d = nc.dram_tensor("y", [S, D], F32, kind="ExternalOutput")
    X = [p.sbuf(f"x{t}", [128, D], F32) for t in range(NT)]
    for t in range(NT):
        p.dma(X[t][:], x_d[128 * t : 128 * (t + 1), :])
    emit_l1_attn(p, X, wq_d, wk_d, wv_d, bq_d, bk_d, bv_d, thr_d, msel_d)
    for t in range(NT):
        p.dma(y_d[128 * t : 128 * (t + 1), :], X[t][:])
    return p.finish()


def _l0ffn_host_inputs(wg, wu, wd):
    outs = []
    stacked = np.concatenate([wg, wu], axis=0)
    for c in range(8):
        wisl = stacked[704 * c : 704 * (c + 1), :]
        wdsl = wd[128 * c : 128 * (c + 1), :]
        outs.append((np.ascontiguousarray(wisl), np.ascontiguousarray(wdsl)))
    return outs


def build_l0_ffn_launch():
    p = Prog("l0ffn")
    nc = p.nc
    x_d = nc.dram_tensor("x", [S, D], F32, kind="ExternalInput")
    wisl_d = nc.dram_tensor("wisl", [704, D], F32, kind="ExternalInput")
    wdsl_d = nc.dram_tensor("wdsl", [128, I_FF], F32, kind="ExternalInput")
    clipx_d = nc.dram_tensor("clipx", [1, 1], F32, kind="ExternalInput")
    clipd_d = nc.dram_tensor("clipd", [1, 1], F32, kind="ExternalInput")
    y_d = nc.dram_tensor("y", [S, D], F32, kind="ExternalOutput")
    X = [p.sbuf(f"x{t}", [128, D], F32) for t in range(NT)]
    for t in range(NT):
        p.dma(X[t][:], x_d[128 * t : 128 * (t + 1), :])
    emit_l0_ffn(p, X, wisl_d, wdsl_d, clipx_d, clipd_d)
    for t in range(NT):
        p.dma(y_d[128 * t : 128 * (t + 1), :], X[t][:])
    return p.finish()


def _l1ffn_host_inputs(wg, wu, wd):
    outs = []
    for c in range(8):
        wsl = np.concatenate(
            [wg[:, 128 * c : 128 * (c + 1)], wu[:, 128 * c : 128 * (c + 1)]], axis=0
        )
        wdsl = wd[128 * c : 128 * (c + 1), :]
        outs.append((np.ascontiguousarray(wsl), np.ascontiguousarray(wdsl)))
    return outs




def build_l1_ffn_launch(debug=False):
    p = Prog("l1ffn")
    nc = p.nc
    x_d = nc.dram_tensor("x", [S, D], F32, kind="ExternalInput")
    wsl_d = nc.dram_tensor("wsl", [2 * I_FF, 128], F32, kind="ExternalInput")
    wdsl_d = nc.dram_tensor("wdsl", [128, I_FF], F32, kind="ExternalInput")
    y_d = nc.dram_tensor("y", [S, D], F32, kind="ExternalOutput")

    X = [p.sbuf(f"x{t}", [128, D], F32) for t in range(NT)]
    for t in range(NT):
        p.dma(X[t][:], x_d[128 * t : 128 * (t + 1), :])
    emit_l1_ffn(p, X, wsl_d, wdsl_d)
    for t in range(NT):
        p.dma(y_d[128 * t : 128 * (t + 1), :], X[t][:])
    if debug:
        for nm, h in DBG.items():
            out = nc.dram_tensor("dbg_" + nm, list(h.ap().shape), h.ap().dtype, kind="ExternalOutput")
            p.dma(out.ap(), h.ap())
    return p.finish()


DBG = {}


def build_fused():
    p = Prog("mk")
    nc = p.nc
    x_d = nc.dram_tensor("x", [S, D], F32, kind="ExternalInput")
    inw_d = nc.dram_tensor("inw", [D, 4384], F32, kind="ExternalInput")
    convw_d = nc.dram_tensor("convw", [CONVD, 4], F32, kind="ExternalInput")
    convb_d = nc.dram_tensor("convb", [1, CONVD], F32, kind="ExternalInput")
    dtb_d = nc.dram_tensor("dtb", [1, NH], F32, kind="ExternalInput")
    alog_d = nc.dram_tensor("alog", [1, NH], F32, kind="ExternalInput")
    dskip_d = nc.dram_tensor("dskip", [1, NH], F32, kind="ExternalInput")
    rmsw_d = nc.dram_tensor("rmsw", [1, DIN], F32, kind="ExternalInput")
    outw_d = nc.dram_tensor("outw", [DIN, D], F32, kind="ExternalInput")
    w1_d = nc.dram_tensor("w1", [D, DFF], F32, kind="ExternalInput")
    b1_d = nc.dram_tensor("b1", [1, DFF], F32, kind="ExternalInput")
    w2_d = nc.dram_tensor("w2", [DFF, D], F32, kind="ExternalInput")
    b2_d = nc.dram_tensor("b2", [1, D], F32, kind="ExternalInput")
    lnw_d = nc.dram_tensor("lnw", [1, D], F32, kind="ExternalInput")
    lnb_d = nc.dram_tensor("lnb", [1, D], F32, kind="ExternalInput")
    xhalo_d = nc.dram_tensor("xhalo", [3, D], F32, kind="ExternalInput")
    hmask_d = nc.dram_tensor("hmask", [1, 1], F32, kind="ExternalInput")
    omask_d = nc.dram_tensor("omask", [1, 1], F32, kind="ExternalInput")
    wisl_d = nc.dram_tensor("wisl", [704, D], F32, kind="ExternalInput")
    wdsl0_d = nc.dram_tensor("wdsl0", [128, I_FF], F32, kind="ExternalInput")
    clipx_d = nc.dram_tensor("clipx", [1, 1], F32, kind="ExternalInput")
    clipd_d = nc.dram_tensor("clipd", [1, 1], F32, kind="ExternalInput")
    wq_d = nc.dram_tensor("wq", [D, D], F32, kind="ExternalInput")
    wk_d = nc.dram_tensor("wk", [D, D], F32, kind="ExternalInput")
    wv_d = nc.dram_tensor("wv", [D, D], F32, kind="ExternalInput")
    bq_d = nc.dram_tensor("bq", [1, D], F32, kind="ExternalInput")
    bk_d = nc.dram_tensor("bk", [1, D], F32, kind="ExternalInput")
    bv_d = nc.dram_tensor("bv", [1, D], F32, kind="ExternalInput")
    thr_d = nc.dram_tensor("thr", [1, NJC], F32, kind="ExternalInput")
    msel_d = nc.dram_tensor("msel", [1, 1], F32, kind="ExternalInput")
    wsl1_d = nc.dram_tensor("wsl1", [2 * I_FF, 128], F32, kind="ExternalInput")
    wdsl1_d = nc.dram_tensor("wdsl1", [128, I_FF], F32, kind="ExternalInput")
    y_d = nc.dram_tensor("y", [S, D], F32, kind="ExternalOutput")

    X = [p.sbuf(f"x{t}", [128, D], F32) for t in range(NT)]
    for t in range(NT):
        p.dma(X[t][:], x_d[128 * t : 128 * (t + 1), :])
    emit_l0_attn(
        p, X, inw_d, convw_d, convb_d, dtb_d, alog_d, dskip_d, rmsw_d, outw_d,
        w1_d, b1_d, w2_d, b2_d, lnw_d, lnb_d, xhalo_d, hmask_d, omask_d
    )
    emit_l0_ffn(p, X, wisl_d, wdsl0_d, clipx_d, clipd_d)
    emit_l1_attn(p, X, wq_d, wk_d, wv_d, bq_d, bk_d, bv_d, thr_d, msel_d)
    emit_l1_ffn(p, X, wsl1_d, wdsl1_d)
    for t in range(NT):
        p.dma(y_d[128 * t : 128 * (t + 1), :], X[t][:])
    return p.finish()


def _fused_in_maps(inputs):
    x = np.asarray(inputs["L0_attn_x"], np.float32)
    xs = _shard_x(x)
    hi0 = _l0attn_host_inputs(x)
    wsl0 = _l0ffn_host_inputs(
        np.asarray(inputs["L0_ffn_w_gate"], np.float32),
        np.asarray(inputs["L0_ffn_w_up"], np.float32),
        np.asarray(inputs["L0_ffn_w_down"], np.float32),
    )
    hi1 = _l1attn_host_inputs()
    wsl1 = _l1ffn_host_inputs(
        np.asarray(inputs["L1_ffn_w_gate"], np.float32),
        np.asarray(inputs["L1_ffn_w_up"], np.float32),
        np.asarray(inputs["L1_ffn_w_down"], np.float32),
    )
    base = {
        "inw": np.asarray(inputs["L0_attn_in_proj_w"], np.float32),
        "convw": np.asarray(inputs["L0_attn_conv_w"], np.float32),
        "convb": np.asarray(inputs["L0_attn_conv_b"], np.float32).reshape(1, -1),
        "dtb": np.asarray(inputs["L0_attn_dt_bias"], np.float32).reshape(1, -1),
        "alog": np.asarray(inputs["L0_attn_A_log"], np.float32).reshape(1, -1),
        "dskip": np.asarray(inputs["L0_attn_D_skip"], np.float32).reshape(1, -1),
        "rmsw": np.asarray(inputs["L0_attn_rms_w"], np.float32).reshape(1, -1),
        "outw": np.asarray(inputs["L0_attn_out_proj_w"], np.float32),
        "w1": np.asarray(inputs["L0_attn_mlp_w1"], np.float32),
        "b1": np.asarray(inputs["L0_attn_mlp_b1"], np.float32).reshape(1, -1),
        "w2": np.asarray(inputs["L0_attn_mlp_w2"], np.float32),
        "b2": np.asarray(inputs["L0_attn_mlp_b2"], np.float32).reshape(1, -1),
        "lnw": np.asarray(inputs["L0_attn_ln_w"], np.float32).reshape(1, -1),
        "lnb": np.asarray(inputs["L0_attn_ln_b"], np.float32).reshape(1, -1),
        "clipx": np.asarray(inputs["L0_ffn_clip_x"], np.float32).reshape(1, 1),
        "clipd": np.asarray(inputs["L0_ffn_clip_down"], np.float32).reshape(1, 1),
        "wq": np.asarray(inputs["L1_attn_Wq"], np.float32),
        "wk": np.asarray(inputs["L1_attn_Wk"], np.float32),
        "wv": np.asarray(inputs["L1_attn_Wv"], np.float32),
        "bq": np.asarray(inputs["L1_attn_bq"], np.float32).reshape(1, -1),
        "bk": np.asarray(inputs["L1_attn_bk"], np.float32).reshape(1, -1),
        "bv": np.asarray(inputs["L1_attn_bv"], np.float32).reshape(1, -1),
    }
    in_maps = []
    for c in range(8):
        m = dict(base)
        m["x"] = xs[c]
        m["xhalo"], m["hmask"], m["omask"] = hi0[c]
        m["wisl"], m["wdsl0"] = wsl0[c]
        m["thr"], m["msel"] = hi1[c]
        m["wsl1"], m["wdsl1"] = wsl1[c]
        in_maps.append(m)
    return in_maps


def kernel(**inputs):
    nc = build_fused()
    in_maps = _fused_in_maps(inputs)
    res = run_bass_kernel_spmd(nc, in_maps, core_ids=list(range(8)))
    out = np.zeros((4, 4096, 1024), np.float32)
    for c in range(8):
        out[c // 2, 2048 * (c % 2) : 2048 * (c % 2 + 1)] = np.asarray(res.results[c]["y"])
    return out


def _shard_x(x):
    return [np.ascontiguousarray(x[c // 2, 2048 * (c % 2) : 2048 * (c % 2 + 1)]) for c in range(8)]


def kernel_phase_a(**inputs):
    x = np.asarray(inputs["L0_attn_x"], np.float32)
    xs = _shard_x(x)

    nc = build_l0_attn_launch()
    hi = _l0attn_host_inputs(x)
    base = {
        "inw": np.asarray(inputs["L0_attn_in_proj_w"], np.float32),
        "convw": np.asarray(inputs["L0_attn_conv_w"], np.float32),
        "convb": np.asarray(inputs["L0_attn_conv_b"], np.float32).reshape(1, -1),
        "dtb": np.asarray(inputs["L0_attn_dt_bias"], np.float32).reshape(1, -1),
        "alog": np.asarray(inputs["L0_attn_A_log"], np.float32).reshape(1, -1),
        "dskip": np.asarray(inputs["L0_attn_D_skip"], np.float32).reshape(1, -1),
        "rmsw": np.asarray(inputs["L0_attn_rms_w"], np.float32).reshape(1, -1),
        "outw": np.asarray(inputs["L0_attn_out_proj_w"], np.float32),
        "w1": np.asarray(inputs["L0_attn_mlp_w1"], np.float32),
        "b1": np.asarray(inputs["L0_attn_mlp_b1"], np.float32).reshape(1, -1),
        "w2": np.asarray(inputs["L0_attn_mlp_w2"], np.float32),
        "b2": np.asarray(inputs["L0_attn_mlp_b2"], np.float32).reshape(1, -1),
        "lnw": np.asarray(inputs["L0_attn_ln_w"], np.float32).reshape(1, -1),
        "lnb": np.asarray(inputs["L0_attn_ln_b"], np.float32).reshape(1, -1),
    }
    in_maps = []
    for c in range(8):
        m = dict(base)
        m["x"] = xs[c]
        m["xhalo"], m["hmask"], m["omask"] = hi[c]
        in_maps.append(m)
    res = run_bass_kernel_spmd(nc, in_maps, core_ids=list(range(8)))
    xs = [np.asarray(res.results[c]["y"]) for c in range(8)]

    nc = build_l0_ffn_launch()
    wsl = _l0ffn_host_inputs(
        np.asarray(inputs["L0_ffn_w_gate"], np.float32),
        np.asarray(inputs["L0_ffn_w_up"], np.float32),
        np.asarray(inputs["L0_ffn_w_down"], np.float32),
    )
    cx = np.asarray(inputs["L0_ffn_clip_x"], np.float32).reshape(1, 1)
    cd = np.asarray(inputs["L0_ffn_clip_down"], np.float32).reshape(1, 1)
    in_maps = [
        {"x": xs[c], "wisl": wsl[c][0], "wdsl": wsl[c][1], "clipx": cx, "clipd": cd}
        for c in range(8)
    ]
    res = run_bass_kernel_spmd(nc, in_maps, core_ids=list(range(8)))
    xs = [np.asarray(res.results[c]["y"]) for c in range(8)]

    nc = build_l1_attn_launch()
    hi = _l1attn_host_inputs()
    in_maps = [
        {
            "x": xs[c],
            "wq": np.asarray(inputs["L1_attn_Wq"], np.float32),
            "wk": np.asarray(inputs["L1_attn_Wk"], np.float32),
            "wv": np.asarray(inputs["L1_attn_Wv"], np.float32),
            "bq": np.asarray(inputs["L1_attn_bq"], np.float32).reshape(1, -1),
            "bk": np.asarray(inputs["L1_attn_bk"], np.float32).reshape(1, -1),
            "bv": np.asarray(inputs["L1_attn_bv"], np.float32).reshape(1, -1),
            "thr": hi[c][0],
            "msel": hi[c][1],
        }
        for c in range(8)
    ]
    res = run_bass_kernel_spmd(nc, in_maps, core_ids=list(range(8)))
    xs = [np.asarray(res.results[c]["y"]) for c in range(8)]

    nc = build_l1_ffn_launch()
    wsl = _l1ffn_host_inputs(
        np.asarray(inputs["L1_ffn_w_gate"], np.float32),
        np.asarray(inputs["L1_ffn_w_up"], np.float32),
        np.asarray(inputs["L1_ffn_w_down"], np.float32),
    )
    in_maps = [{"x": xs[c], "wsl": wsl[c][0], "wdsl": wsl[c][1]} for c in range(8)]
    res = run_bass_kernel_spmd(nc, in_maps, core_ids=list(range(8)))
    xs = [np.asarray(res.results[c]["y"]) for c in range(8)]

    out = np.zeros((4, 4096, 1024), np.float32)
    for c in range(8):
        out[c // 2, 2048 * (c % 2) : 2048 * (c % 2 + 1)] = xs[c]
    return out


# version 07
# speedup vs baseline: 2.1002x; 1.4544x over previous
import numpy as np
from contextlib import ExitStack

import concourse.bass as bass
import concourse.mybir as mybir
from concourse.bass_utils import run_bass_kernel_spmd

F32 = mybir.dt.float32
I32 = mybir.dt.int32
BF16 = mybir.dt.bfloat16
AF = mybir.ActivationFunctionType
ALU = mybir.AluOpType
AX = mybir.AxisListType

MAGIC = 12582912.0
SEM_LIM = 24000

S = 2048
NT = S // 128
D = 1024
DCH = D // 128

COMPUTE = ("scalar", "vector", "tensor", "gpsimd")


class Prog:

    def __init__(self, name="prog"):
        self.nc = bass.Bass("TRN2", target_bir_lowering=False)
        self._stack = ExitStack()
        self._stack.__enter__()
        self.cnt = 0
        self.items = []
        self._tens = {}
        self._dep_floor = None
        self._last = None
        self._uid = 0
        self._consts = {}
        self._const_tile = self.sbuf("constpool", [128, 32], F32)
        self._barrier_tile = self.sbuf("barrier", [1, 8], F32)

    def uid(self):
        self._uid += 1
        return self._uid

    def sbuf(self, name, shape, dtype, scope=None):
        st = scope if scope is not None else self._stack
        return st.enter_context(self.nc.sbuf_tensor(f"{name}_{self.uid()}", shape, dtype))

    def psum(self, name, shape, dtype=F32, scope=None):
        bank_elems = 512 if dtype == F32 else 1024
        shape = [shape[0], ((shape[1] + bank_elems - 1) // bank_elems) * bank_elems]
        st = scope if scope is not None else self._stack
        return st.enter_context(self.nc.psum_tensor(f"{name}_{self.uid()}", shape, dtype))

    def scope(self):
        prog = self

        class _Scope(ExitStack):
            def __exit__(self_, *a):
                prog.op(
                    "vector",
                    lambda e, o=prog._barrier_tile[0:1, :]: e.memset(o, 0.0),
                    barrier=True,
                )
                prog._dep_floor = len(prog.items) - 1
                return ExitStack.__exit__(self_, *a)

        return _Scope()

    def semaphore(self, name):
        return self._stack.enter_context(self.nc.semaphore(f"{name}_{self.uid()}"))

    def dram(self, name, shape, dtype, kind="Internal"):
        return self.nc.dram_tensor(name, shape, dtype, kind=kind)

    def constv(self, v):
        v = float(v)
        if v in (0.0, 1.0):
            return v
        if v not in self._consts:
            j = len(self._consts)
            assert j < 32
            self._consts[v] = j
            col = self._const_tile[:, j : j + 1]
            self.memset(col, v)
            self.nc.const_aps.register(F32, v, col)
        return v

    def par(self, key=None):

        class _Par:
            def __enter__(self_):
                return self_

            def __exit__(self_, *a):
                return False

        return _Par()

    @staticmethod
    def _apkeys(x):
        if x is None or isinstance(x, (int, float)):
            return ()
        try:
            return (x.name,)
        except Exception:
            return ()

    def op(self, eng, emit, inc=1, is_async=False, barrier=False, chain_ok=False,
           reads=(), writes=()):
        idx = len(self.items)
        rks = set()
        wks = set()
        for x in reads:
            rks.update(self._apkeys(x))
        for x in writes:
            wks.update(self._apkeys(x))
        deps = set()
        if self._dep_floor is not None:
            deps.add(self._dep_floor)
        if barrier:
            deps.update(range(idx))
        else:
            for k in rks:
                t = self._tens.get(k)
                if t is not None:
                    deps.update(t[0])
            for k in wks:
                t = self._tens.get(k)
                if t is not None:
                    deps.update(t[1])
        for k in rks:
            self._tens.setdefault(k, [set(), set()])[1].add(idx)
        for k in wks:
            t = self._tens.setdefault(k, [set(), set()])
            t[0].add(idx)
            t[1] = set()
        self.items.append([eng, emit, inc, is_async, deps])
        self.cnt += inc
        return self.cnt

    def dma(self, out, in_, eng="sync", **kw):
        return self.op(
            eng, lambda e, o=out, i=in_, k=kw: e.dma_start(o, i, **k), 16, is_async=True,
            reads=(in_,), writes=(out,)
        )

    def dmat(self, out, in_, eng="scalar"):
        return self.op(
            eng, lambda e, o=out, i=in_: e.dma_start_transpose(o, i), 16, is_async=True,
            reads=(in_,), writes=(out,)
        )

    def mm(self, out, lhsT, rhs, start=True, stop=True):
        return self.op(
            "tensor",
            lambda e, o=out, l=lhsT, r=rhs, s0=start, s1=stop: e.matmul(o, l, r, start=s0, stop=s1),
            reads=(lhsT, rhs), writes=(out,)
        )

    def transpose(self, out, in_, ident):
        return self.op(
            "tensor", lambda e, o=out, i=in_, d=ident: e.transpose(o, i, d),
            reads=(in_, ident), writes=(out,)
        )

    def act(self, out, in_, func, bias=0.0, scale=1.0, accum_out=None, eng="scalar"):
        extra = () if isinstance(bias, (int, float)) and isinstance(scale, (int, float)) else tuple(
            x for x in (bias, scale) if not isinstance(x, (int, float))
        )
        cp = () if self._const_tile is None else (self._const_tile[:],)
        return self.op(
            eng,
            lambda e, o=out, i=in_, f=func, b=bias, s=scale, a=accum_out: e.activation(
                o, i, f, bias=b, scale=s, accum_out=a
            ),
            reads=(in_,) + extra + cp + (() if accum_out is None else (accum_out,)),
            writes=(out,) + (() if accum_out is None else (accum_out,)),
        )

    def tt(self, out, a, b, alu, eng="vector", chain_ok=False):
        return self.op(
            eng, lambda e, o=out, x=a, y=b, p=alu: e.tensor_tensor(o, x, y, p),
            reads=(a, b), writes=(out,)
        )

    def ts(self, out, in_, s1, s2, op0, op1=ALU.bypass, accum_out=None, eng="vector", chain_ok=False):
        return self.op(
            eng,
            lambda e, o=out, i=in_, a=s1, b=s2, p0=op0, p1=op1, ac=accum_out: e.tensor_scalar(
                o, i, a, b, p0, p1, accum_out=ac
            ),
            reads=(in_, s1, s2) + (() if accum_out is None else (accum_out,)),
            writes=(out,) + (() if accum_out is None else (accum_out,)),
        )

    def tss(self, out, in_, s, alu, eng="vector", chain_ok=False):
        return self.op(
            eng, lambda e, o=out, i=in_, a=s, p=alu: e.tensor_single_scalar(o, i, a, p),
            reads=(in_, s), writes=(out,)
        )

    def stt(self, out, in0, scalar, in1, op0, op1, eng="vector", chain_ok=False):
        return self.op(
            eng,
            lambda e, o=out, i0=in0, sc=scalar, i1=in1, p0=op0, p1=op1: e.scalar_tensor_tensor(
                o, i0, sc, i1, p0, p1
            ),
            reads=(in0, scalar, in1), writes=(out,)
        )

    def red(self, out, in_, alu, absval=False, eng="vector"):
        return self.op(
            eng,
            lambda e, o=out, i=in_, p=alu, a=absval: e.tensor_reduce(
                o, i, AX.X, p, apply_absolute_value=a
            ),
            reads=(in_, out), writes=(out,)
        )

    def scan(self, out, d0, d1, init, op0, op1, eng="vector"):
        return self.op(
            eng,
            lambda e, o=out, a=d0, b=d1, ini=init, p0=op0, p1=op1: e.tensor_tensor_scan(
                o, a, b, ini, p0, p1
            ),
            reads=(d0, d1, init), writes=(out,)
        )

    def copy(self, out, in_, eng="vector"):
        return self.op(
            eng, lambda e, o=out, i=in_: e.tensor_copy(o, i), reads=(in_,), writes=(out,)
        )

    def memset(self, ap, v, eng="vector"):
        return self.op(eng, lambda e, o=ap, x=v: e.memset(o, x), writes=(ap,))

    def recip(self, out, in_):
        return self.op(
            "vector", lambda e, o=out, i=in_: e.reciprocal(o, i), reads=(in_,), writes=(out,)
        )

    def iota(self, out, pattern, base=0, channel_multiplier=0):
        return self.op(
            "gpsimd",
            lambda e, o=out, p=pattern, b=base, c=channel_multiplier: e.iota(
                o, p, base=b, channel_multiplier=c, allow_small_or_imprecise_dtypes=True
            ),
            writes=(out,),
        )

    def affsel(self, out, in_, pattern, compare_op, fill, base=0, channel_multiplier=1):
        return self.op(
            "gpsimd",
            lambda e, o=out, i=in_, p=pattern, c=compare_op, f=fill, b=base, m=channel_multiplier: e.affine_select(
                out=o, in_=i, pattern=p, compare_op=c, fill=f, base=b, channel_multiplier=m
            ),
            reads=(in_,), writes=(out,)
        )

    def barrier(self, tag=0):
        import os as _os
        sel = _os.environ.get("BARRIERS", "")
        if sel != "all" and str(tag) not in sel.split(","):
            return
        self.op("vector", lambda e, o=self._barrier_tile[0:1, :]: e.memset(o, 0.0), barrier=True)

    def collective(self, kind, alu, groups, ins, outs):
        return self.op(
            "gpsimd",
            lambda e, k=kind, o=alu, g=groups, i=ins, u=outs: e.collective_compute(
                k, o, replica_groups=g, ins=i, outs=u
            ),
            1,
            is_async=True,
            reads=tuple(ins), writes=tuple(outs),
        )

    def finish(self):
        NPOOL = 60
        eng_names = ("sync", "scalar", "vector", "tensor", "gpsimd")
        eng_sems = {e: self.semaphore(f"se_{e}") for e in eng_names}
        pool = [self.semaphore(f"sp{i}") for i in range(NPOOL)]
        pool_uses = [0] * NPOOL

        import os as _os
        if _os.environ.get("SCHED_SERIAL"):
            for i in range(len(self.items)):
                self.items[i][4] = set(range(i))
        n = len(self.items)
        token = [None] * n
        eng_seq = {e: 0 for e in eng_names}
        next_pool = 0
        for i, (eng, emit, inc, is_async, deps) in enumerate(self.items):
            if is_async:
                j = next_pool
                next_pool = (next_pool + 1) % NPOOL
                pool_uses[j] += inc
                token[i] = ("p", j, pool_uses[j])
            else:
                eng_seq[eng] += 1
                token[i] = ("e", eng, eng_seq[eng])

        by_eng = {e: [] for e in eng_names}
        clocks = {e: {} for e in eng_names}
        eng_pos = {e: 0 for e in eng_names}
        for i, (eng, emit, inc, is_async, deps) in enumerate(self.items):
            waits = {}
            for d in sorted(deps):
                de, demit, dinc, dasync, _ = self.items[d]
                tk = token[d]
                if tk[0] == "e":
                    if de == "tensor" and eng == "tensor" and not is_async:
                        continue
                    key = ("e", tk[1])
                    waits[key] = max(waits.get(key, 0), tk[2])
                else:
                    key = ("p", tk[1])
                    waits[key] = max(waits.get(key, 0), tk[2])
            ck = clocks[eng]
            wl = []
            for key, v in waits.items():
                if ck.get(key, -1) >= v:
                    continue
                wl.append((key, v))
                ck[key] = max(ck.get(key, -1), v)
            by_eng[eng].append((wl, emit, inc, token[i]))

        final_tokens = []
        for e in eng_names:
            if eng_seq[e] > 0:
                final_tokens.append((("e", e), eng_seq[e]))
        for j in range(NPOOL):
            if pool_uses[j] > 0:
                final_tokens.append((("p", j), pool_uses[j]))

        def semof(key):
            return eng_sems[key[1]] if key[0] == "e" else pool[key[1]]

        block = self._stack.enter_context(self.nc.Block())

        def emit_engine(engobj, eng):
            for wl, emit, inc, tk in by_eng[eng]:
                for key, v in wl:
                    engobj.wait_ge(semof(key), v)
                inst = emit(engobj)
                inst.then_inc(semof(("e", tk[1]) if tk[0] == "e" else ("p", tk[1])), inc if tk[0] == "p" else 1)

        def final_wait(engobj):
            for key, v in final_tokens:
                engobj.wait_ge(semof(key), v)

        @block.sync
        def _(e):
            emit_engine(e, "sync")
            final_wait(e)

        @block.scalar
        def _(e):
            emit_engine(e, "scalar")

        @block.vector
        def _(e):
            emit_engine(e, "vector")

        @block.tensor
        def _(e):
            emit_engine(e, "tensor")

        @block.gpsimd
        def _(e):
            emit_engine(e, "gpsimd")

        self._stack.close()
        return self.nc




def emit_plain_ln(p, x_tile, out_tile, scratch, eps=1e-5, out_is_bf16=False):
    st = scratch["st"]
    mu, ms, var, rstd = st[:, 0:1], st[:, 1:2], st[:, 2:3], st[:, 3:4]
    n = x_tile.shape[1]
    p.memset(st[:, 0:4], 0.0)
    p.act(scratch["sq"][:, 0:n], x_tile, AF.Square, accum_out=ms)
    p.red(mu, x_tile, ALU.add)
    p.ts(mu, mu, 1.0 / n, None, ALU.mult)
    p.ts(ms, ms, 1.0 / n, None, ALU.mult)
    p.act(var, mu, AF.Square)
    p.tt(var, ms, var, ALU.subtract)
    p.act(var, var, AF.Sqrt, bias=p.constv(eps))
    p.recip(rstd, var)
    p.ts(out_tile, x_tile, mu, rstd, ALU.subtract, ALU.mult)


def emit_fp4_quant(p, x_tile, out_tile, scratch, ncol):
    nb = ncol // 16
    a, b2 = scratch["a"][:, 0:ncol], scratch["b"][:, 0:ncol]
    bm, br = scratch["bm"][:, 0:nb], scratch["br"][:, 0:nb]
    x3 = x_tile.rearrange("p (a b) -> p a b", b=16)
    sgn = scratch["sg4"][:, 0:ncol]
    p.op("scalar", lambda e, o=sgn, i=x_tile: e.sign(o, i), reads=(x_tile,), writes=(sgn,))
    p.red(bm, x3, ALU.max, absval=True)
    p.ts(bm, bm, 1.0 / 12.0, 5e-13, ALU.mult, ALU.max, chain_ok=True)
    p.recip(br, bm)
    p.ts(a.bitcast(I32), x_tile.bitcast(I32), 0x7FFFFFFF, None, ALU.bitwise_and)
    a3 = a.rearrange("p (a b) -> p a b", b=16)
    p.tt(a3, a3, br.unsqueeze(2).broadcast_to([128, nb, 16]), ALU.mult, chain_ok=True)
    c1, c2 = b2, scratch["c2"][:, 0:ncol]
    p.tss(c1, a, 0.5, ALU.is_gt, chain_ok=True)
    p.stt(c1, a, 1.5, c1, ALU.is_gt, ALU.add, chain_ok=True)
    p.stt(c1, a, 2.5, c1, ALU.is_gt, ALU.add, chain_ok=True)
    p.stt(c1, a, 3.5, c1, ALU.is_gt, ALU.add, chain_ok=True)
    p.tss(c2, a, 5.0, ALU.is_gt, chain_ok=True)
    p.stt(c2, a, 7.0, c2, ALU.is_gt, ALU.add, chain_ok=True)
    p.stt(c2, a, 10.0, c2, ALU.is_gt, ALU.add, chain_ok=True)
    p.stt(c2, a, 10.0, c2, ALU.is_gt, ALU.add, chain_ok=True)
    p.stt(c1, c2, 2.0, c1, ALU.mult, ALU.add, chain_ok=True)
    p.tt(c1, c1, sgn, ALU.mult)
    c13 = c1.rearrange("p (a b) -> p a b", b=16)
    o3 = out_tile.rearrange("p (a b) -> p a b", b=16)
    p.tt(o3, c13, bm.unsqueeze(2).broadcast_to([128, nb, 16]), ALU.mult, chain_ok=True)



I_FF = 2816
NI = I_FF // 128


def emit_l1_ffn(p, X, wsl_d, wdsl_d, tag="f1"):
    qx_dr = p.dram(f"{tag}_qx", [S, D], BF16)
    qh_dr = p.dram(f"{tag}_qh", [S, I_FF], BF16)
    wq_dr = p.dram(f"{tag}_wq", [8 * 2 * I_FF, 128], BF16)
    wqin_dr = p.dram(f"{tag}_wqi", [2 * I_FF, 128], BF16)
    wdq_dr = p.dram(f"{tag}_wdq", [8 * 128, I_FF], BF16)
    wdqin_dr = p.dram(f"{tag}_wdqi", [128, I_FF], BF16)
    DBG.update(qx=qx_dr, qh=qh_dr, wq=wq_dr, wdq=wdq_dr)

    with p.scope() as outer:
        scr = {
            "sq": p.sbuf("sq", [128, D], F32, scope=outer),
            "st": p.sbuf("st", [128, 8], F32, scope=outer),
            "a": p.sbuf("qa", [128, D], F32, scope=outer),
            "b": p.sbuf("qb", [128, D], F32, scope=outer),
            "c2": p.sbuf("qc", [128, D], F32, scope=outer),
            "sg4": p.sbuf("qsg", [128, D], F32, scope=outer),
            "bm": p.sbuf("qbm", [128, D // 16], F32, scope=outer),
            "br": p.sbuf("qbr", [128, D // 16], F32, scope=outer),
        }
        emit_l1_ffn_body(p, X, wsl_d, wdsl_d, scr, qx_dr, qh_dr, wq_dr, wqin_dr, wdq_dr, wdqin_dr)


def emit_l1_ffn_body(p, X, wsl_d, wdsl_d, scr, qx_dr, qh_dr, wq_dr, wqin_dr, wdq_dr, wdqin_dr):
    with p.scope() as sc:
        wtile = [p.sbuf(f"wt{i}", [128, 1024], F32, scope=sc) for i in range(2)]
        wqt = [p.sbuf(f"wqt{i}", [128, 1024], BF16, scope=sc) for i in range(2)]
        jobs = []
        for t in range(2 * NI):
            r0 = 128 * t
            jobs.append((wsl_d[r0 : r0 + 128, :], wqin_dr[r0 : r0 + 128, :], 128))
        for j in range(4):
            c0 = 704 * j
            jobs.append((wdsl_d[:, c0 : c0 + 704], wdqin_dr[:, c0 : c0 + 704], 704))
        with p.par("wq"):
            p.dma(wtile[0][:, 0 : jobs[0][2]], jobs[0][0])
        for i, (srcap, dstap, ncol) in enumerate(jobs):
            b = i % 2
            with p.par("wq"):
                if i + 1 < len(jobs):
                    p.dma(wtile[1 - b][:, 0 : jobs[i + 1][2]], jobs[i + 1][0])
                if i > 0:
                    p.dma(dst_prev, wqt_prev, eng="gpsimd")
            emit_fp4_quant(p, wtile[b][:, 0:ncol], wqt[b][:, 0:ncol], scr, ncol)
            dst_prev, wqt_prev = dstap, wqt[b][:, 0:ncol]
        with p.par("wq"):
            p.dma(dst_prev, wqt_prev, eng="gpsimd")
        p.collective("AllGather", ALU.bypass, [list(range(8))], [wqin_dr.ap()], [wq_dr.ap()])
        p.collective("AllGather", ALU.bypass, [list(range(8))], [wdqin_dr.ap()], [wdq_dr.ap()])

        xn = p.sbuf("xn", [128, D], F32, scope=sc)
        qx = [p.sbuf(f"qx{i}", [128, D], BF16, scope=sc) for i in range(2)]
        for t in range(NT):
            emit_plain_ln(p, X[t][:], xn[:], scr)
            emit_fp4_quant(p, xn[:], qx[t % 2][:], scr, D)
            with p.par("lnq"):
                p.dma(qx_dr[128 * t : 128 * (t + 1), :], qx[t % 2][:])

    with p.scope() as sc:
        qxg = p.sbuf("qxg", [128, DCH * 512], BF16, scope=sc)
        wgt = [p.sbuf(f"wgt{i}", [128, 512], BF16, scope=sc) for i in range(2)]
        wut = [p.sbuf(f"wut{i}", [128, 512], BF16, scope=sc) for i in range(2)]
        sg = p.sbuf("sg", [128, 512], F32, scope=sc)
        hb = p.sbuf("hb", [128, 512], F32, scope=sc)
        qh = p.sbuf("qh", [128, 512], BF16, scope=sc)
        psg = [p.psum(f"psg{i}", [128, 512], scope=sc) for i in range(4)]
        psu = [p.psum(f"psu{i}", [128, 512], scope=sc) for i in range(4)]
        ICH = [512, 512, 512, 512, 512, 256]

        def ldw(ic, k, b):
            base = 2 * I_FF * k
            i0 = 512 * ic
            iw = ICH[ic]
            p.dmat(wgt[b][:, 0:iw], wq_dr[base + i0 : base + i0 + iw, :])
            p.dmat(wut[b][:, 0:iw], wq_dr[base + I_FF + i0 : base + I_FF + i0 + iw, :])

        for g in range(4):
            g0 = 512 * g
            with p.par("gu"):
                for k in range(DCH):
                    p.dmat(qxg[:, 512 * k : 512 * (k + 1)], qx_dr[g0 : g0 + 512, 128 * k : 128 * (k + 1)])
                ldw(0, 0, 0)
            it = 0
            for ic in range(6):
                i0 = 512 * ic
                iw = ICH[ic]
                for k in range(DCH):
                    b = it % 2
                    with p.par("gu"):
                        if not (ic == 5 and k == DCH - 1):
                            nic, nk = (ic, k + 1) if k < DCH - 1 else (ic + 1, 0)
                            ldw(nic, nk, 1 - b)
                        for tt in range(4):
                            lhsT = qxg[:, 512 * k + 128 * tt : 512 * k + 128 * (tt + 1)]
                            p.mm(psg[tt][:, 0:iw], lhsT, wgt[b][:, 0:iw], start=(k == 0), stop=(k == DCH - 1))
                            p.mm(psu[tt][:, 0:iw], lhsT, wut[b][:, 0:iw], start=(k == 0), stop=(k == DCH - 1))
                    it += 1
                for tt in range(4):
                    p.act(sg[:, 0:iw], psg[tt][:, 0:iw], AF.Silu)
                    p.tt(hb[:, 0:iw], sg[:, 0:iw], psu[tt][:, 0:iw], ALU.mult)
                    emit_fp4_quant(p, hb[:, 0:iw], qh[:, 0:iw], scr, iw)
                    p.dma(qh_dr[g0 + 128 * tt : g0 + 128 * (tt + 1), i0 : i0 + iw], qh[:, 0:iw])

    with p.scope() as sc:
        qhg = p.sbuf("qhg", [128, NI * 512], BF16, scope=sc)
        wdt = [p.sbuf(f"wdt{i}", [128, D], BF16, scope=sc) for i in range(2)]
        psd = [p.psum(f"psd{i}", [128, 512], scope=sc) for i in range(8)]

        def ldd(k, b):
            p.dmat(wdt[b][:, 0:512], wdq_dr[0:512, 128 * k : 128 * (k + 1)])
            p.dmat(wdt[b][:, 512:1024], wdq_dr[512:1024, 128 * k : 128 * (k + 1)])

        for g in range(4):
            g0 = 512 * g
            with p.par("dn"):
                for k in range(NI):
                    p.dmat(qhg[:, 512 * k : 512 * (k + 1)], qh_dr[g0 : g0 + 512, 128 * k : 128 * (k + 1)])
                ldd(0, 0)
            for k in range(NI):
                b = k % 2
                with p.par("dn"):
                    if k < NI - 1:
                        ldd(k + 1, 1 - b)
                    for tt in range(4):
                        lhsT = qhg[:, 512 * k + 128 * tt : 512 * k + 128 * (tt + 1)]
                        for hc in range(2):
                            p.mm(
                                psd[2 * tt + hc][:],
                                lhsT,
                                wdt[b][:, 512 * hc : 512 * (hc + 1)],
                                start=(k == 0),
                                stop=(k == NI - 1),
                            )
            for tt in range(4):
                t = 4 * g + tt
                for hc in range(2):
                    h0 = 512 * hc
                    p.tt(X[t][:, h0 : h0 + 512], X[t][:, h0 : h0 + 512], psd[2 * tt + hc][:], ALU.add)




def emit_int8_wquant(p, w_tile, out_tile, sc_col, scratch, nrow, ncol):
    a = scratch["a"]
    r = scratch["st"][:, 4:5]
    p.red(sc_col, w_tile, ALU.max, absval=True)
    p.ts(sc_col, sc_col, 1.0 / 127.0, 1e-30, ALU.mult, ALU.max)
    p.recip(r[0:nrow, :], sc_col)
    p.ts(a[0:nrow, 0:ncol], w_tile, r[0:nrow, :], None, ALU.mult)
    p.ts(a[0:nrow, 0:ncol], a[0:nrow, 0:ncol], MAGIC, -MAGIC, ALU.add, ALU.add)
    p.ts(out_tile, a[0:nrow, 0:ncol], 127.0, -128.0, ALU.min, ALU.max)


def emit_partition_gmax(p, mxcol, mx_dr, ar_in, ar_out, row16, tag):
    p.dma(mx_dr.ap().rearrange("a b -> b a"), mxcol)
    p.memset(row16[0:1, :], 0.0)
    p.dma(row16[0:1, 8:9], mx_dr[0:1, 0:1])
    pass


def emit_l0_ffn(p, X, wisl_d, wdsl_d, clipx_d, clipd_d, tag="g0"):
    qx_dr = p.dram(f"{tag}_qx", [S, D], BF16)
    h_dr = p.dram(f"{tag}_h", [S, I_FF], BF16)
    wq_dr = p.dram(f"{tag}_wq", [2 * I_FF, D], BF16)
    win_dr = p.dram(f"{tag}_wi", [704, D], BF16)
    wdq_dr = p.dram(f"{tag}_wdq", [D, I_FF], BF16)
    wdin_dr = p.dram(f"{tag}_wdi", [128, I_FF], BF16)
    sc_dr = p.dram(f"{tag}_sc", [2 * I_FF, 1], F32)
    scin_dr = p.dram(f"{tag}_sci", [704, 1], F32)
    scd_dr = p.dram(f"{tag}_scd", [D, 1], F32)
    scdin_dr = p.dram(f"{tag}_scdi", [128, 1], F32)
    mx_dr = p.dram(f"{tag}_mx", [1, 128], F32)
    arx_in = p.dram(f"{tag}_axi", [1, 16], F32)
    arx_out = p.dram(f"{tag}_axo", [1, 16], F32)
    arh_in = p.dram(f"{tag}_ahi", [1, 16], F32)
    arh_out = p.dram(f"{tag}_aho", [1, 16], F32)
    scal_dr = p.dram(f"{tag}_scal", [1, 8], F32)
    G8 = [list(range(8))]

    with p.scope() as outer:
        scr = {
            "sq": p.sbuf("sq", [128, D], F32, scope=outer),
            "st": p.sbuf("st", [128, 8], F32, scope=outer),
            "a": p.sbuf("qa", [128, I_FF], F32, scope=outer),
        }
        row16 = p.sbuf("row16", [1, 16], F32, scope=outer)
        scalrow = p.sbuf("scalrow", [1, 8], F32, scope=outer)
        colbc = p.sbuf("colbc", [128, 4], F32, scope=outer)

        with p.scope() as sc:
            wt = [p.sbuf(f"wt{i}", [128, I_FF], F32, scope=sc) for i in range(2)]
            wq = [p.sbuf(f"wq{i}", [128, I_FF], BF16, scope=sc) for i in range(2)]
            scol = [p.sbuf(f"scol{i}", [128, 1], F32, scope=sc) for i in range(2)]
            rows = [128, 128, 128, 128, 128, 64]
            jobs = []
            r0 = 0
            for nr in rows:
                jobs.append((wisl_d[r0 : r0 + nr, :], win_dr[r0 : r0 + nr, :], scin_dr[r0 : r0 + nr, :], nr, D))
                r0 += nr
            jobs.append((wdsl_d[:, :], wdin_dr[:, :], scdin_dr[:, :], 128, I_FF))
            with p.par():
                p.dma(wt[0][0 : jobs[0][3], 0 : jobs[0][4]], jobs[0][0])
            prev = None
            for i, (srcap, dstap, scap, nr, ncol) in enumerate(jobs):
                b = i % 2
                with p.par():
                    if i + 1 < len(jobs):
                        p.dma(wt[1 - b][0 : jobs[i + 1][3], 0 : jobs[i + 1][4]], jobs[i + 1][0])
                    if prev is not None:
                        p.dma(prev[0], prev[1], eng="gpsimd")
                        p.dma(prev[2], prev[3], eng="gpsimd")
                emit_int8_wquant(p, wt[b][0:nr, 0:ncol], wq[b][0:nr, 0:ncol], scol[b][0:nr, :], scr, nr, ncol)
                prev = (dstap, wq[b][0:nr, 0:ncol], scap, scol[b][0:nr, :])
            with p.par():
                p.dma(prev[0], prev[1], eng="gpsimd")
                p.dma(prev[2], prev[3], eng="gpsimd")
            p.collective("AllGather", ALU.bypass, G8, [win_dr.ap()], [wq_dr.ap()])
            p.collective("AllGather", ALU.bypass, G8, [scin_dr.ap()], [sc_dr.ap()])
            p.collective("AllGather", ALU.bypass, G8, [wdin_dr.ap()], [wdq_dr.ap()])
            p.collective("AllGather", ALU.bypass, G8, [scdin_dr.ap()], [scd_dr.ap()])

        XN = [p.sbuf(f"xn{t}", [128, D], F32, scope=outer) for t in range(NT)]
        mxcol = p.sbuf("mxcol", [128, 1], F32, scope=outer)
        tmpc = p.sbuf("tmpc", [128, 1], F32, scope=outer)
        p.memset(mxcol[:], 0.0)
        for t in range(NT):
            emit_plain_ln(p, X[t][:], XN[t][:], scr)
            p.red(tmpc[:], XN[t][:], ALU.max, absval=True)
            p.tt(mxcol[:], mxcol[:], tmpc[:], ALU.max)
        p.dma(mx_dr.ap().rearrange("a b -> b a"), mxcol[:])
        p.memset(row16[0:1, :], 0.0)
        rowbuf = p.sbuf("rowbuf", [1, 128], F32, scope=outer)
        p.dma(rowbuf[0:1, :], mx_dr[0:1, :])
        p.red(row16[0:1, 0:1], rowbuf[0:1, :], ALU.max)
        p.dma(arx_in[0:1, :], row16[0:1, :])
        p.collective("AllReduce", ALU.max, G8, [arx_in.ap()], [arx_out.ap()])
        p.dma(row16[0:1, 0:16], arx_out[0:1, :])
        sig = scalrow[0:1, 4:5]
        p.dma(sig, clipx_d[0:1, 0:1])
        p.act(sig, sig, AF.Sigmoid)
        sxv, rsxv = scalrow[0:1, 0:1], scalrow[0:1, 1:2]
        p.tt(sxv, sig, row16[0:1, 0:1], ALU.mult)
        p.ts(sxv, sxv, 1.0 / 127.0, None, ALU.mult)
        p.recip(rsxv, sxv)
        p.dma(scal_dr[0:1, 0:2], scalrow[0:1, 0:2])
        p.dma(colbc[:, 0:1], scal_dr[0:1, 1:2].partition_broadcast(128))
        p.dma(colbc[:, 1:2], scal_dr[0:1, 0:1].partition_broadcast(128))

        with p.scope() as sc:
            q32 = p.sbuf("q32", [128, D], F32, scope=sc)
            qx = [p.sbuf(f"qx{i}", [128, D], BF16, scope=sc) for i in range(2)]
            for t in range(NT):
                p.ts(q32[:], XN[t][:], colbc[:, 0:1], None, ALU.mult)
                p.ts(q32[:], q32[:], MAGIC, -MAGIC, ALU.add, ALU.add, chain_ok=True)
                p.ts(qx[t % 2][:], q32[:], 127.0, -128.0, ALU.min, ALU.max, chain_ok=True)
                with p.par():
                    p.dma(qx_dr[128 * t : 128 * (t + 1), :], qx[t % 2][:])

        sgu_bc = p.sbuf("sgubc", [128, 2 * I_FF], F32, scope=outer)
        p.dma(sgu_bc[:], sc_dr.ap().rearrange("a b -> b a").partition_broadcast(128))
        p.ts(sgu_bc[:], sgu_bc[:], colbc[:, 1:2], None, ALU.mult)

        hmx = p.sbuf("hmx", [128, 1], F32, scope=outer)
        p.memset(hmx[:], 0.0)
        with p.scope() as sc:
            qxg = p.sbuf("qxg", [128, DCH * 512], BF16, scope=sc)
            wgt = [p.sbuf(f"wgt{i}", [128, 512], BF16, scope=sc) for i in range(2)]
            wut = [p.sbuf(f"wut{i}", [128, 512], BF16, scope=sc) for i in range(2)]
            gs = p.sbuf("gs", [128, 512], F32, scope=sc)
            us = p.sbuf("us", [128, 512], F32, scope=sc)
            hb = p.sbuf("hb", [128, 512], BF16, scope=sc)
            psg = [p.psum(f"psg{i}", [128, 512], scope=sc) for i in range(4)]
            psu = [p.psum(f"psu{i}", [128, 512], scope=sc) for i in range(4)]
            ICH = [512, 512, 512, 512, 512, 256]

            def ldw0(ic, k, b):
                i0 = 512 * ic
                iw = ICH[ic]
                p.dmat(wgt[b][:, 0:iw], wq_dr[i0 : i0 + iw, 128 * k : 128 * (k + 1)])
                p.dmat(wut[b][:, 0:iw], wq_dr[I_FF + i0 : I_FF + i0 + iw, 128 * k : 128 * (k + 1)])

            for g in range(4):
                g0 = 512 * g
                with p.par():
                    for k in range(DCH):
                        p.dmat(qxg[:, 512 * k : 512 * (k + 1)], qx_dr[g0 : g0 + 512, 128 * k : 128 * (k + 1)])
                    ldw0(0, 0, 0)
                it = 0
                for ic in range(6):
                    i0 = 512 * ic
                    iw = ICH[ic]
                    for k in range(DCH):
                        b = it % 2
                        with p.par():
                            if not (ic == 5 and k == DCH - 1):
                                nic, nk = (ic, k + 1) if k < DCH - 1 else (ic + 1, 0)
                                ldw0(nic, nk, 1 - b)
                            for tt in range(4):
                                lhsT = qxg[:, 512 * k + 128 * tt : 512 * k + 128 * (tt + 1)]
                                p.mm(psg[tt][:, 0:iw], lhsT, wgt[b][:, 0:iw], start=(k == 0), stop=(k == DCH - 1))
                                p.mm(psu[tt][:, 0:iw], lhsT, wut[b][:, 0:iw], start=(k == 0), stop=(k == DCH - 1))
                        it += 1
                    for tt in range(4):
                        p.tt(gs[:, 0:iw], psg[tt][:, 0:iw], sgu_bc[:, i0 : i0 + iw], ALU.mult)
                        p.act(gs[:, 0:iw], gs[:, 0:iw], AF.Silu)
                        p.tt(us[:, 0:iw], psu[tt][:, 0:iw], sgu_bc[:, I_FF + i0 : I_FF + i0 + iw], ALU.mult)
                        p.tt(us[:, 0:iw], gs[:, 0:iw], us[:, 0:iw], ALU.mult)
                        p.red(tmpc[:], us[:, 0:iw], ALU.max, absval=True)
                        p.tt(hmx[:], hmx[:], tmpc[:], ALU.max)
                        p.copy(hb[:, 0:iw], us[:, 0:iw])
                        p.dma(h_dr[g0 + 128 * tt : g0 + 128 * (tt + 1), i0 : i0 + iw], hb[:, 0:iw])

        p.dma(mx_dr.ap().rearrange("a b -> b a"), hmx[:])
        p.memset(row16[0:1, :], 0.0)
        p.dma(rowbuf[0:1, :], mx_dr[0:1, :])
        p.red(row16[0:1, 0:1], rowbuf[0:1, :], ALU.max)
        p.dma(arh_in[0:1, :], row16[0:1, :])
        p.collective("AllReduce", ALU.max, G8, [arh_in.ap()], [arh_out.ap()])
        p.dma(row16[0:1, 0:16], arh_out[0:1, :])
        p.dma(sig, clipd_d[0:1, 0:1])
        p.act(sig, sig, AF.Sigmoid)
        shv, rshv = scalrow[0:1, 2:3], scalrow[0:1, 3:4]
        p.tt(shv, sig, row16[0:1, 0:1], ALU.mult)
        p.ts(shv, shv, 1.0 / 127.0, None, ALU.mult)
        p.recip(rshv, shv)
        p.dma(scal_dr[0:1, 2:4], scalrow[0:1, 2:4])
        p.dma(colbc[:, 2:3], scal_dr[0:1, 3:4].partition_broadcast(128))
        p.dma(colbc[:, 3:4], scal_dr[0:1, 2:3].partition_broadcast(128))

        swd_bc = p.sbuf("swdbc", [128, D], F32, scope=outer)
        p.dma(swd_bc[:], scd_dr.ap().rearrange("a b -> b a").partition_broadcast(128))
        p.ts(swd_bc[:], swd_bc[:], colbc[:, 3:4], None, ALU.mult)

        with p.scope() as sc:
            qhg = p.sbuf("qhg", [128, NI * 512], BF16, scope=sc)
            hstg = [p.sbuf(f"hstg{i}", [128, 512], BF16, scope=sc) for i in range(2)]
            hq32 = p.sbuf("hq32", [128, 512], F32, scope=sc)
            wdt = [p.sbuf(f"wdt{i}", [128, D], BF16, scope=sc) for i in range(2)]
            psd = [p.psum(f"psd{i}", [128, 512], scope=sc) for i in range(8)]

            def ldd0(k, b):
                p.dmat(wdt[b][:, 0:512], wdq_dr[0:512, 128 * k : 128 * (k + 1)])
                p.dmat(wdt[b][:, 512:1024], wdq_dr[512:1024, 128 * k : 128 * (k + 1)])

            for g in range(4):
                g0 = 512 * g
                with p.par():
                    p.dmat(hstg[0][:], h_dr[g0 : g0 + 512, 0:128])
                for k in range(NI):
                    b = k % 2
                    with p.par():
                        if k < NI - 1:
                            p.dmat(hstg[1 - b][:], h_dr[g0 : g0 + 512, 128 * (k + 1) : 128 * (k + 2)])
                        p.ts(hq32[:], hstg[b][:], colbc[:, 2:3], None, ALU.mult)
                        p.ts(hq32[:], hq32[:], MAGIC, -MAGIC, ALU.add, ALU.add)
                        p.ts(qhg[:, 512 * k : 512 * (k + 1)], hq32[:], 127.0, -128.0, ALU.min, ALU.max)
                    if k == NI - 1:
                        with p.par():
                            ldd0(0, 0)
                for k in range(NI):
                    b = k % 2
                    with p.par():
                        if k < NI - 1:
                            ldd0(k + 1, 1 - b)
                        for tt in range(4):
                            lhsT = qhg[:, 512 * k + 128 * tt : 512 * k + 128 * (tt + 1)]
                            for hc in range(2):
                                p.mm(
                                    psd[2 * tt + hc][:],
                                    lhsT,
                                    wdt[b][:, 512 * hc : 512 * (hc + 1)],
                                    start=(k == 0),
                                    stop=(k == NI - 1),
                                )
                for tt in range(4):
                    t = 4 * g + tt
                    for hc in range(2):
                        h0 = 512 * hc
                        p.tt(hq32[:], psd[2 * tt + hc][:], swd_bc[:, h0 : h0 + 512], ALU.mult)
                        p.tt(X[t][:, h0 : h0 + 512], X[t][:, h0 : h0 + 512], hq32[:], ALU.add)



NH = 32
HP = 64
DIN = 2048
DSTATE = 128
CONVD = 2304
NCH = 16
DFF = 4096


def emit_affine_ln(p, x_tile, out_tile, scratch, lnw_bc, lnb_bc, u_scratch):
    aff = scratch["aff"][:, 0 : x_tile.shape[1]]
    emit_plain_ln(p, x_tile, u_scratch, scratch)
    emit_plain_ln(p, u_scratch, aff, scratch)
    p.tt(aff, aff, lnw_bc, ALU.mult)
    p.tt(out_tile, aff, lnb_bc, ALU.add)


def emit_l0_attn(
    p, X, inw_d, convw_d, convb_d, dtb_d, alog_d, dskip_d, rmsw_d, outw_d,
    w1_d, b1_d, w2_d, b2_d, lnw_d, lnb_d, xhalo_d, hmask_d, omask_d, tag="m0"
):
    xn_dr = p.dram(f"{tag}_xn", [S, D], BF16)
    z_dr = p.dram(f"{tag}_z", [S, DIN], BF16)
    xss_dr = p.dram(f"{tag}_xss", [DIN, S], BF16)
    caf_dr = p.dram(f"{tag}_caf", [NCH, NH * 128], F32)
    ncl_dr = p.dram(f"{tag}_ncl", [NCH, NH * 128], F32)
    lam_dr = p.dram(f"{tag}_lam", [NH, S], BF16)
    lamtot_dr = p.dram(f"{tag}_lamt", [NH, S], BF16)
    etm_dr = p.dram(f"{tag}_etm", [NH, S], BF16)
    easum_dr = p.dram(f"{tag}_eas", [NCH, NH], F32)
    yraw_dr = p.dram(f"{tag}_yraw", [S, DIN], BF16)
    y4_dr = p.dram(f"{tag}_y4", [S, DIN], BF16)
    sfin_dr = p.dram(f"{tag}_sfin", [128, DIN], F32)
    sag_dr = p.dram(f"{tag}_sag", [256, DIN], F32)
    xn2_dr = p.dram(f"{tag}_xn2", [S, D], BF16)
    h1_dr = p.dram(f"{tag}_h1", [S, DFF], BF16)
    PAIRS = [[0, 1], [2, 3], [4, 5], [6, 7]]

    with p.scope() as outer:
        scr = {
            "sq": p.sbuf("sq", [128, D], F32, scope=outer),
            "st": p.sbuf("st", [128, 8], F32, scope=outer),
            "aff": p.sbuf("aff", [128, D], F32, scope=outer),
        }
        uscr = p.sbuf("uscr", [128, D], F32, scope=outer)
        lnw_bc = p.sbuf("lnwbc", [128, D], F32, scope=outer)
        lnb_bc = p.sbuf("lnbbc", [128, D], F32, scope=outer)
        ident = p.sbuf("ident", [128, 128], F32, scope=outer)
        p.dma(lnw_bc[:], lnw_d[0:1, :].partition_broadcast(128))
        p.dma(lnb_bc[:], lnb_d[0:1, :].partition_broadcast(128))
        p.memset(ident[:], 0.0)
        p.affsel(ident[:], ident[:], [[-1, 128]], ALU.not_equal, 1.0, 0, 1)

        with p.scope() as sc:
            xnb = p.sbuf("xnb", [128, D], BF16, scope=sc)
            for t in range(NT):
                emit_affine_ln(p, X[t][:], xnb[:], scr, lnw_bc[:], lnb_bc[:], uscr[:])
                p.dma(xn_dr[128 * t : 128 * (t + 1), :], xnb[:])
        xh3f = p.sbuf("xh3f", [128, DCH * 4], BF16, scope=outer)
        with p.scope() as sc:
            xh = p.sbuf("xh", [128, D], F32, scope=sc)
            xnh = p.sbuf("xnh", [128, D], F32, scope=sc)
            pt = p.psum("pt", [128, 128], scope=sc)
            p.memset(xh[:], 0.0)
            p.dma(xh[0:3, :], xhalo_d[0:3, :])
            emit_affine_ln(p, xh[:], xnh[:], scr, lnw_bc[:], lnb_bc[:], uscr[:])
            for k in range(DCH):
                p.transpose(pt[:, 0:128], xnh[:, 128 * k : 128 * (k + 1)], ident[:])
                p.copy(xh3f[:, 4 * k : 4 * k + 3], pt[:, 0:3])

        BM = p.sbuf("Bfm", [128, S], BF16, scope=outer)
        CM = p.sbuf("Cfm", [128, S], BF16, scope=outer)
        dttm = p.sbuf("dttm", [128, NCH * NH], F32, scope=outer)
        haloxbc = p.sbuf("haloxbc", [128, 18 * 3], BF16, scope=outer)
        with p.scope() as sc:
            xnf = p.sbuf("xnf", [128, DCH * S], BF16, scope=sc)
            for k in range(DCH):
                p.dmat(xnf[:, S * k : S * (k + 1)], xn_dr[:, 128 * k : 128 * (k + 1)])
            sc2 = p.scope()
            sc2.__enter__()
            wt = [p.sbuf(f"wt{i}", [128, 512], BF16, scope=sc2) for i in range(2)]
            zb = [p.sbuf(f"zb{i}", [128, 512], BF16, scope=sc2) for i in range(4)]
            psz = [p.psum(f"psz{i}", [128, 512], scope=sc2) for i in range(4)]
            def ldz(zc, k, b):
                p.dma(wt[b][:], inw_d[128 * k : 128 * (k + 1), 512 * zc : 512 * (zc + 1)], eng="gpsimd")

            for g in range(4):
                with p.par():
                    ldz(0, 0, 0)
                it = 0
                for zc in range(4):
                    for k in range(DCH):
                        b = it % 2
                        with p.par():
                            if not (zc == 3 and k == DCH - 1):
                                nzc, nk = (zc, k + 1) if k < DCH - 1 else (zc + 1, 0)
                                ldz(nzc, nk, 1 - b)
                            for tt in range(4):
                                lhsT = xnf[:, S * k + 512 * g + 128 * tt : S * k + 512 * g + 128 * (tt + 1)]
                                p.mm(psz[tt][:], lhsT, wt[b][:], start=(k == 0), stop=(k == DCH - 1))
                        it += 1
                    with p.par():
                        for tt in range(4):
                            p.copy(zb[tt][:], psz[tt][:])
                    with p.par():
                        for tt in range(4):
                            p.dma(z_dr[512 * g + 128 * tt : 512 * g + 128 * (tt + 1), 512 * zc : 512 * (zc + 1)], zb[tt][:])
            sc2.__exit__(None, None, None)
            sc2 = p.scope()
            sc2.__enter__()
            dtbias_bc = p.sbuf("dtbbc", [128, NH], F32, scope=sc2)
            p.dma(dtbias_bc[:], dtb_d[0:1, :].partition_broadcast(128))
            wdt = p.sbuf("wdt", [128, DCH * NH], BF16, scope=sc2)
            psd = p.psum("psdt", [128, NH], scope=sc2)
            d1 = p.sbuf("d1", [128, NH], F32, scope=sc2)
            d2 = p.sbuf("d2", [128, NH], F32, scope=sc2)
            for k in range(DCH):
                p.dma(wdt[:, NH * k : NH * (k + 1)], inw_d[128 * k : 128 * (k + 1), 4352:4384], eng="gpsimd")
            for tc in range(NT):
                for k in range(DCH):
                    lhsT = xnf[:, S * k + 128 * tc : S * k + 128 * (tc + 1)]
                    p.mm(psd[:, 0:NH], lhsT, wdt[:, NH * k : NH * (k + 1)], start=(k == 0), stop=(k == DCH - 1))
                p.tt(d1[:], psd[:, 0:NH], dtbias_bc[:], ALU.add)
                p.act(d2[:], d1[:], AF.Abs)
                p.act(d2[:], d2[:], AF.Exp, scale=-1.0)
                p.act(d2[:], d2[:], AF.Ln, bias=1.0)
                p.ts(d1[:], d1[:], 0.0, None, ALU.max)
                p.tt(dttm[:, NH * tc : NH * (tc + 1)], d1[:], d2[:], ALU.add)
            sc2.__exit__(None, None, None)
            convw = p.sbuf("convw", [128, 18 * 4], F32, scope=sc)
            convb = p.sbuf("convb", [128, 18], F32, scope=sc)
            p.dma(convw[:].rearrange("p (f k) -> p f k", k=4), convw_d.ap().rearrange("(f p) k -> p f k", p=128))
            p.dma(convb[:].rearrange("p (a f) -> p a f", a=1), convb_d.ap().rearrange("a (f p) -> p a f", p=128), allow_slow_non_contiguous=True)
            hmcol = p.sbuf("hmcol", [128, 1], F32, scope=sc)
            p.dma(hmcol[:], hmask_d[0:1, 0:1].partition_broadcast(128))
            wtile = [p.sbuf(f"wtile{i}", [128, 128], BF16, scope=sc) for i in range(2)]
            xbch = p.sbuf("xbch", [128, S + 4], F32, scope=sc)
            cacc = p.sbuf("cacc", [128, S], F32, scope=sc)
            sil = p.sbuf("sil", [128, S], BF16, scope=sc)
            psx = [p.psum(f"psx{i}", [128, 512], scope=sc) for i in range(4)]
            psh = p.psum("psh", [128, 16], scope=sc)
            def ldxbc(fc, k, b):
                col0 = 2048 + 128 * fc
                p.dma(wtile[b][:], inw_d[128 * k : 128 * (k + 1), col0 : col0 + 128], eng="gpsimd")

            with p.par():
                ldxbc(0, 0, 0)
            itx = 0
            for fc in range(18):
                for k in range(DCH):
                    b = itx % 2
                    with p.par():
                        if not (fc == 17 and k == DCH - 1):
                            nfc, nk = (fc, k + 1) if k < DCH - 1 else (fc + 1, 0)
                            ldxbc(nfc, nk, 1 - b)
                        for g in range(4):
                            p.mm(psx[g][:], wtile[b][:], xnf[:, S * k + 512 * g : S * k + 512 * (g + 1)], start=(k == 0), stop=(k == DCH - 1))
                        p.mm(psh[:, 0:3], wtile[b][:], xh3f[:, 4 * k : 4 * k + 3], start=(k == 0), stop=(k == DCH - 1))
                    itx += 1
                p.ts(xbch[:, 1:4], psh[:, 0:3], hmcol[:], None, ALU.mult)
                for g in range(4):
                    p.copy(xbch[:, 4 + 512 * g : 4 + 512 * (g + 1)], psx[g][:])
                p.copy(haloxbc[:, 3 * fc : 3 * fc + 3], xbch[:, 1:4])
                p.ts(cacc[:], xbch[:, 1 : 1 + S], convw[:, 4 * fc : 4 * fc + 1], None, ALU.mult)
                for k in range(1, 4):
                    p.stt(cacc[:], xbch[:, 1 + k : 1 + k + S], convw[:, 4 * fc + k : 4 * fc + k + 1], cacc[:], ALU.mult, ALU.add)
                p.act(sil[:], cacc[:], AF.Silu, bias=convb[:, fc : fc + 1])
                if fc < 16:
                    p.dma(xss_dr[128 * fc : 128 * (fc + 1), :], sil[:])
                elif fc == 16:
                    p.copy(BM[:], sil[:])
                else:
                    p.copy(CM[:], sil[:])

        with p.scope() as sc:
            dtfm = p.sbuf("dtfm", [NH, S], F32, scope=sc)
            pt = p.psum("ptd", [128, 128], scope=sc)
            for tc in range(NT):
                p.transpose(pt[0:NH, 0:128], dttm[:, NH * tc : NH * (tc + 1)], ident[:])
                p.copy(dtfm[0:NH, 128 * tc : 128 * (tc + 1)], pt[0:NH, 0:128])
            acol = p.sbuf("acol", [NH, 1], F32, scope=sc)
            p.dma(acol[0:NH, :], alog_d.ap().rearrange("a b -> b a"), allow_slow_non_contiguous=True)
            p.act(acol[0:NH, :], acol[0:NH, :], AF.Exp)
            p.ts(acol[0:NH, :], acol[0:NH, :], -1.0, None, ALU.mult)
            afm = p.sbuf("afm", [NH, S], F32, scope=sc)
            p.ts(afm[0:NH, :], dtfm[0:NH, :], acol[0:NH, :], None, ALU.mult)
            zrow = p.sbuf("zrow", [NH, 128], F32, scope=sc)
            p.memset(zrow[0:NH, :], 0.0)
            cafm = p.sbuf("cafm", [NH, S], F32, scope=sc)
            for c in range(NCH):
                p.scan(cafm[0:NH, 128 * c : 128 * (c + 1)], afm[0:NH, 128 * c : 128 * (c + 1)], zrow[0:NH, :], 0.0, ALU.add, ALU.add)
            zrow2 = p.sbuf("zrow2", [NH, S], F32, scope=sc)
            p.memset(zrow2[0:NH, :], 0.0)
            catot = p.sbuf("catot", [NH, S], F32, scope=sc)
            p.scan(catot[0:NH, :], afm[0:NH, :], zrow2[0:NH, :], 0.0, ALU.add, ALU.add)
            brow = p.sbuf("brow", [NH, S], BF16, scope=sc)
            p.act(brow[0:NH, :], cafm[0:NH, :], AF.Exp)
            p.dma(lam_dr[0:NH, :], brow[0:NH, :])
            p.act(brow[0:NH, :], catot[0:NH, :], AF.Exp)
            p.dma(lamtot_dr[0:NH, :], brow[0:NH, :])
            asumb = p.sbuf("asumb", [NH, NCH], F32, scope=sc)
            for c in range(NCH):
                p.copy(asumb[0:NH, c : c + 1], cafm[0:NH, 128 * c + 127 : 128 * (c + 1)])
            efm = p.sbuf("efm", [NH, S], F32, scope=sc)
            p.tt(
                efm[0:NH, :].rearrange("h (c t) -> h c t", t=128),
                asumb[0:NH, :].unsqueeze(2).broadcast_to([NH, NCH, 128]),
                cafm[0:NH, :].rearrange("h (c t) -> h c t", t=128),
                ALU.subtract,
            )
            p.act(brow[0:NH, :], efm[0:NH, :], AF.Exp)
            p.dma(etm_dr[0:NH, :], brow[0:NH, :])
            p.act(asumb[0:NH, :], asumb[0:NH, :], AF.Exp)
            p.dma(easum_dr.ap().rearrange("c h -> h c"), asumb[0:NH, :], allow_slow_non_contiguous=True)
            nclfm = p.sbuf("nclfm", [NH, S], F32, scope=sc)
            p.ts(nclfm[0:NH, :], cafm[0:NH, :], -1.0, None, ALU.mult)
            for c in range(NCH):
                p.dma(caf_dr[c : c + 1, :].rearrange("a (h t) -> (a h) t", t=128), cafm[0:NH, 128 * c : 128 * (c + 1)])
                p.dma(ncl_dr[c : c + 1, :].rearrange("a (h t) -> (a h) t", t=128), nclfm[0:NH, 128 * c : 128 * (c + 1)])

        maskDT = p.sbuf("maskDT", [128, 128], F32, scope=outer)
        p.memset(maskDT[:], 0.0)
        p.affsel(maskDT[:], maskDT[:], [[1, 128]], ALU.is_ge, -1e30, 0, -1)
        dskip_bc = p.sbuf("dskipbc", [128, NH], F32, scope=outer)
        p.dma(dskip_bc[:], dskip_d[0:1, :].partition_broadcast(128))
        onesr = p.sbuf("onesr", [1, 128], F32, scope=outer)
        p.memset(onesr[0:1, :], 1.0)
        SST = p.sbuf("SST", [128, DIN], F32, scope=outer)
        p.memset(SST[:], 0.0)
        with p.scope() as sc:
            Sb = p.sbuf("Sb", [128, DIN], BF16, scope=sc)
            Xtm = p.sbuf("Xtm", [128, DIN], BF16, scope=sc)
            Xd = p.sbuf("Xd", [128, DIN], BF16, scope=sc)
            XdE = p.sbuf("XdE", [128, DIN], BF16, scope=sc)
            lamt = p.sbuf("lamt", [128, NH], BF16, scope=sc)
            etmt = p.sbuf("etmt", [128, NH], BF16, scope=sc)
            lambc = p.sbuf("lambc", [128, NH], F32, scope=sc)
            Gt = p.sbuf("Gt", [128, 128], BF16, scope=sc)
            Btm = p.sbuf("Btm", [128, 128], BF16, scope=sc)
            Bf32 = p.sbuf("Bf32", [128, 128], F32, scope=sc)
            carow = p.sbuf("carow", [1, NH * 128], F32, scope=sc)
            nclrow = p.sbuf("nclrow", [1, NH * 128], F32, scope=sc)
            Ws = p.sbuf("Ws", [128, DIN], F32, scope=sc)
            Wp = p.sbuf("Wp", [128, DIN], BF16, scope=sc)
            ybuf = p.sbuf("ybuf", [128, DIN], F32, scope=sc)
            ybuf2 = p.sbuf("ybuf2", [128, DIN], F32, scope=sc)
            yb16 = p.sbuf("yb16", [128, DIN], BF16, scope=sc)
            pd = p.psum("pd", [128, DIN], scope=sc)
            py = p.psum("py", [128, DIN], scope=sc)
            for c in range(NCH):
                t0 = 128 * c
                p.dmat(Xtm[:], xss_dr[:, t0 : t0 + 128])
                p.dmat(lamt[:, 0:NH], lam_dr[0:NH, t0 : t0 + 128])
                p.dmat(etmt[:, 0:NH], etm_dr[0:NH, t0 : t0 + 128])
                p.dma(lambc[:, 0:NH], easum_dr[c : c + 1, :].partition_broadcast(128))
                p.dma(carow[0:1, :], caf_dr[c : c + 1, :])
                p.dma(nclrow[0:1, :], ncl_dr[c : c + 1, :])
                x3 = Xtm[:].rearrange("p (h e) -> p h e", e=HP)
                p.tt(Xd[:].rearrange("p (h e) -> p h e", e=HP), x3, dttm[:, NH * c : NH * (c + 1)].unsqueeze(2).broadcast_to([128, NH, HP]), ALU.mult)
                p.tt(XdE[:].rearrange("p (h e) -> p h e", e=HP), Xd[:].rearrange("p (h e) -> p h e", e=HP), etmt[:, 0:NH].unsqueeze(2).broadcast_to([128, NH, HP]), ALU.mult)
                p.mm(pd[:, 0:128], BM[:, t0 : t0 + 128], CM[:, t0 : t0 + 128])
                p.copy(Gt[:], pd[:, 0:128])
                p.copy(Bf32[:], BM[:, t0 : t0 + 128])
                p.transpose(pd[:, 128:256], Bf32[:], ident[:])
                p.copy(Btm[:], pd[:, 128:256])
                for hh in range(2):
                    o0 = DIN * hh
                    for s4 in range(4):
                        p.mm(pd[:, 512 * s4 : 512 * (s4 + 1)], onesr[0:1, :], carow[0:1, 2048 * hh + 512 * s4 : 2048 * hh + 512 * (s4 + 1)], start=True, stop=False)
                    for j in range(16):
                        h = 16 * hh + j
                        p.mm(pd[:, 128 * j : 128 * (j + 1)], nclrow[0:1, 128 * h : 128 * (h + 1)], onesr[0:1, :], start=False, stop=True)
                    p.tt(
                        Ws[:, o0 : o0 + 2048].rearrange("p (j t) -> p j t", t=128) if False else Ws[:, 0:2048].rearrange("p (j t) -> p j t", t=128),
                        pd[:].rearrange("p (j t) -> p j t", t=128),
                        maskDT[:].unsqueeze(1).broadcast_to([128, 16, 128]),
                        ALU.add,
                    )
                    p.act(Wp[:, 0:2048], Ws[:, 0:2048], AF.Exp)
                    p.tt(
                        Wp[:, 0:2048].rearrange("p (j t) -> p j t", t=128),
                        Wp[:, 0:2048].rearrange("p (j t) -> p j t", t=128),
                        Gt[:].unsqueeze(1).broadcast_to([128, 16, 128]),
                        ALU.mult,
                    )
                    for j in range(16):
                        h = 16 * hh + j
                        p.mm(py[:, HP * h : HP * (h + 1)], Wp[:, 128 * j : 128 * (j + 1)], Xd[:, HP * h : HP * (h + 1)], start=True, stop=True)
                p.copy(Sb[:], SST[:])
                for h in range(NH):
                    p.mm(pd[:, HP * h : HP * (h + 1)], CM[:, t0 : t0 + 128], Sb[:, HP * h : HP * (h + 1)], start=True, stop=True)
                p.tt(ybuf[:].rearrange("p (h e) -> p h e", e=HP), pd[:].rearrange("p (h e) -> p h e", e=HP), lamt[:, 0:NH].unsqueeze(2).broadcast_to([128, NH, HP]), ALU.mult)
                p.tt(ybuf[:], ybuf[:], py[:], ALU.add)
                p.tt(ybuf2[:].rearrange("p (h e) -> p h e", e=HP), x3, dskip_bc[:].unsqueeze(2).broadcast_to([128, NH, HP]), ALU.mult)
                p.tt(yb16[:], ybuf[:], ybuf2[:], ALU.add)
                p.dma(yraw_dr[t0 : t0 + 128, :], yb16[:])
                p.ts(SST[:].rearrange("p (h e) -> p h e", e=HP), SST[:].rearrange("p (h e) -> p h e", e=HP), None, None, ALU.mult) if False else None
                p.tt(SST[:].rearrange("p (h e) -> p h e", e=HP), SST[:].rearrange("p (h e) -> p h e", e=HP), lambc[:, 0:NH].unsqueeze(2).broadcast_to([128, NH, HP]), ALU.mult)
                for h in range(NH):
                    p.mm(py[:, HP * h : HP * (h + 1)], Btm[:], XdE[:, HP * h : HP * (h + 1)], start=True, stop=True)
                p.tt(SST[:], SST[:], py[:], ALU.add)
            p.dma(sfin_dr[:, :], SST[:])
        p.collective("AllGather", ALU.bypass, PAIRS, [sfin_dr.ap()], [sag_dr.ap()])

        rms_bc = p.sbuf("rmsbc", [128, DIN], F32, scope=outer)
        p.dma(rms_bc[:], rmsw_d[0:1, :].partition_broadcast(128))
        with p.scope() as sc:
            omcol = p.sbuf("omcol", [128, 1], F32, scope=sc)
            p.dma(omcol[:], omask_d[0:1, 0:1].partition_broadcast(128))
            Sin = p.sbuf("Sin", [128, DIN], F32, scope=sc)
            Sinb = p.sbuf("Sinb", [128, DIN], BF16, scope=sc)
            p.dma(Sin[:], sag_dr[0:128, :])
            p.ts(Sin[:], Sin[:], omcol[:], None, ALU.mult)
            p.copy(Sinb[:], Sin[:])
            lamtt = p.sbuf("lamtt", [128, NH], BF16, scope=sc)
            yr = p.sbuf("yr", [128, DIN], BF16, scope=sc)
            zt = p.sbuf("zt", [128, DIN], BF16, scope=sc)
            zs = p.sbuf("zs", [128, DIN], F32, scope=sc)
            yv = p.sbuf("yv", [128, DIN], F32, scope=sc)
            yo = p.sbuf("yo", [128, DIN], BF16, scope=sc)
            rcol = p.sbuf("rcol", [128, 1], F32, scope=sc)
            sqs = p.sbuf("sqs", [128, DIN], F32, scope=sc)
            pz = p.psum("pz", [128, DIN], scope=sc)
            for c in range(NCH):
                t0 = 128 * c
                p.dmat(lamtt[:, 0:NH], lamtot_dr[0:NH, t0 : t0 + 128])
                p.dma(yr[:], yraw_dr[t0 : t0 + 128, :])
                p.dma(zt[:], z_dr[t0 : t0 + 128, :], eng="gpsimd")
                for h in range(NH):
                    p.mm(pz[:, HP * h : HP * (h + 1)], CM[:, t0 : t0 + 128], Sinb[:, HP * h : HP * (h + 1)], start=True, stop=True)
                p.tt(yv[:].rearrange("p (h e) -> p h e", e=HP), pz[:].rearrange("p (h e) -> p h e", e=HP), lamtt[:, 0:NH].unsqueeze(2).broadcast_to([128, NH, HP]), ALU.mult)
                p.tt(yv[:], yv[:], yr[:], ALU.add)
                p.act(zs[:], zt[:], AF.Silu)
                p.tt(yv[:], yv[:], zs[:], ALU.mult)
                p.memset(rcol[:], 0.0)
                p.act(sqs[:], yv[:], AF.Square, accum_out=rcol[:])
                p.ts(rcol[:], rcol[:], 1.0 / DIN, None, ALU.mult)
                p.act(rcol[:], rcol[:], AF.Sqrt, bias=p.constv(1e-5))
                p.recip(rcol[:], rcol[:])
                p.ts(yv[:], yv[:], rcol[:], None, ALU.mult)
                p.tt(yo[:], yv[:], rms_bc[:], ALU.mult)
                p.dma(y4_dr[t0 : t0 + 128, :], yo[:])

        out1_dr = p.dram(f"{tag}_out1", [S, D], F32)
        with p.scope() as sc:
            yf = p.sbuf("yf", [128, 16 * 512], BF16, scope=sc)
            wot = [p.sbuf(f"wot{i}", [128, D], BF16, scope=sc) for i in range(2)]
            o1t = p.sbuf("o1t", [128, D], F32, scope=sc)
            pso = [p.psum(f"pso{i}", [128, 512], scope=sc) for i in range(8)]

            def ldo(k, b):
                p.dma(wot[b][:], outw_d[128 * k : 128 * (k + 1), 0:1024], eng="gpsimd")

            for g in range(4):
                g0 = 512 * g
                with p.par():
                    for k in range(16):
                        p.dmat(yf[:, 512 * k : 512 * (k + 1)], y4_dr[g0 : g0 + 512, 128 * k : 128 * (k + 1)])
                    ldo(0, 0)
                for k in range(16):
                    b = k % 2
                    with p.par():
                        if k < 15:
                            ldo(k + 1, 1 - b)
                        for tt in range(4):
                            lhsT = yf[:, 512 * k + 128 * tt : 512 * k + 128 * (tt + 1)]
                            p.mm(pso[2 * tt][:], lhsT, wot[b][:, 0:512], start=(k == 0), stop=(k == 15))
                            p.mm(pso[2 * tt + 1][:], lhsT, wot[b][:, 512:1024], start=(k == 0), stop=(k == 15))
                for tt in range(4):
                    t = 4 * g + tt
                    emit_plain_ln(p, X[t][:], uscr[:], scr)
                    for hc in range(2):
                        p.tt(o1t[:, 512 * hc : 512 * (hc + 1)], pso[2 * tt + hc][:], uscr[:, 512 * hc : 512 * (hc + 1)], ALU.add)
                    with p.par():
                        p.dma(out1_dr[128 * t : 128 * (t + 1), :], o1t[:])
                        p.tt(X[t][:], X[t][:], o1t[:], ALU.add)

        with p.scope() as sc:
            xn2b = p.sbuf("xn2b", [128, D], BF16, scope=sc)
            o1l = p.sbuf("o1l", [128, D], F32, scope=sc)
            for t in range(NT):
                p.dma(o1l[:], out1_dr[128 * t : 128 * (t + 1), :])
                emit_affine_ln(p, o1l[:], xn2b[:], scr, lnw_bc[:], lnb_bc[:], uscr[:])
                p.dma(xn2_dr[128 * t : 128 * (t + 1), :], xn2b[:])
        with p.scope() as sc:
            xnf = p.sbuf("xnf2", [128, DCH * 512], BF16, scope=sc)
            wt = [p.sbuf(f"wt2{i}", [128, 512], BF16, scope=sc) for i in range(2)]
            b1bc = p.sbuf("b1bc", [128, DFF], F32, scope=sc)
            p.dma(b1bc[:], b1_d[0:1, :].partition_broadcast(128))
            hb = [p.sbuf(f"hb1{i}", [128, 512], F32, scope=sc) for i in range(4)]
            hbb = [p.sbuf(f"hbb1{i}", [128, 512], BF16, scope=sc) for i in range(4)]
            ps1 = [p.psum(f"ps1{i}", [128, 512], scope=sc) for i in range(4)]
            def ldw1(jc, k, b):
                p.dma(wt[b][:], w1_d[128 * k : 128 * (k + 1), 512 * jc : 512 * (jc + 1)], eng="gpsimd")

            for g in range(4):
                g0 = 512 * g
                with p.par():
                    for k in range(DCH):
                        p.dmat(xnf[:, 512 * k : 512 * (k + 1)], xn2_dr[g0 : g0 + 512, 128 * k : 128 * (k + 1)])
                    ldw1(0, 0, 0)
                it = 0
                for jc in range(8):
                    j0 = 512 * jc
                    for k in range(DCH):
                        b = it % 2
                        with p.par():
                            if not (jc == 7 and k == DCH - 1):
                                njc, nk = (jc, k + 1) if k < DCH - 1 else (jc + 1, 0)
                                ldw1(njc, nk, 1 - b)
                            for tt in range(4):
                                lhsT = xnf[:, 512 * k + 128 * tt : 512 * k + 128 * (tt + 1)]
                                p.mm(ps1[tt][:], lhsT, wt[b][:], start=(k == 0), stop=(k == DCH - 1))
                        it += 1
                    for tt in range(4):
                        p.tt(hb[tt][:], ps1[tt][:], b1bc[:, j0 : j0 + 512], ALU.add)
                        p.act(hbb[tt][:], hb[tt][:], AF.Relu, eng="scalar")
                    with p.par():
                        for tt in range(4):
                            p.dma(h1_dr[g0 + 128 * tt : g0 + 128 * (tt + 1), j0 : j0 + 512], hbb[tt][:])
        with p.scope() as sc:
            h1f = p.sbuf("h1f", [128, 32 * 512], BF16, scope=sc)
            wt = [p.sbuf(f"wt3{i}", [128, D], BF16, scope=sc) for i in range(2)]
            b2bc = p.sbuf("b2bc", [128, D], F32, scope=sc)
            p.dma(b2bc[:], b2_d[0:1, :].partition_broadcast(128))
            mo = p.sbuf("mo", [128, 512], F32, scope=sc)
            ps2 = [p.psum(f"ps2{i}", [128, 512], scope=sc) for i in range(8)]
            def ldw2(k, b):
                p.dma(wt[b][:], w2_d[128 * k : 128 * (k + 1), 0:1024], eng="gpsimd")

            for g in range(4):
                g0 = 512 * g
                with p.par():
                    for k in range(32):
                        p.dmat(h1f[:, 512 * k : 512 * (k + 1)], h1_dr[g0 : g0 + 512, 128 * k : 128 * (k + 1)])
                    ldw2(0, 0)
                for k in range(32):
                    b = k % 2
                    with p.par():
                        if k < 31:
                            ldw2(k + 1, 1 - b)
                        for tt in range(4):
                            lhsT = h1f[:, 512 * k + 128 * tt : 512 * k + 128 * (tt + 1)]
                            for hc in range(2):
                                p.mm(ps2[2 * tt + hc][:], lhsT, wt[b][:, 512 * hc : 512 * (hc + 1)], start=(k == 0), stop=(k == 31))
                for tt in range(4):
                    t = 4 * g + tt
                    for hc in range(2):
                        h0 = 512 * hc
                        p.tt(mo[:], ps2[2 * tt + hc][:], b2bc[:, h0 : h0 + 512], ALU.add)
                        p.tt(X[t][:, h0 : h0 + 512], X[t][:, h0 : h0 + 512], mo[:], ALU.add, chain_ok=True)


def _l0attn_host_inputs(x_full):
    outs = []
    for c in range(8):
        b, half = c // 2, c % 2
        if half == 1:
            halo = np.ascontiguousarray(x_full[b, 2045:2048])
            hm = 1.0
        else:
            halo = np.ascontiguousarray(x_full[b, 0:3])
            hm = 0.0
        outs.append((halo, np.array([[hm]], np.float32), np.array([[1.0 if half == 1 else 0.0]], np.float32)))
    return outs


def build_l0_attn_launch():
    p = Prog("l0attn")
    nc = p.nc
    x_d = nc.dram_tensor("x", [S, D], F32, kind="ExternalInput")
    inw_d = nc.dram_tensor("inw", [D, 4384], F32, kind="ExternalInput")
    convw_d = nc.dram_tensor("convw", [CONVD, 4], F32, kind="ExternalInput")
    convb_d = nc.dram_tensor("convb", [1, CONVD], F32, kind="ExternalInput")
    dtb_d = nc.dram_tensor("dtb", [1, NH], F32, kind="ExternalInput")
    alog_d = nc.dram_tensor("alog", [1, NH], F32, kind="ExternalInput")
    dskip_d = nc.dram_tensor("dskip", [1, NH], F32, kind="ExternalInput")
    rmsw_d = nc.dram_tensor("rmsw", [1, DIN], F32, kind="ExternalInput")
    outw_d = nc.dram_tensor("outw", [DIN, D], F32, kind="ExternalInput")
    w1_d = nc.dram_tensor("w1", [D, DFF], F32, kind="ExternalInput")
    b1_d = nc.dram_tensor("b1", [1, DFF], F32, kind="ExternalInput")
    w2_d = nc.dram_tensor("w2", [DFF, D], F32, kind="ExternalInput")
    b2_d = nc.dram_tensor("b2", [1, D], F32, kind="ExternalInput")
    lnw_d = nc.dram_tensor("lnw", [1, D], F32, kind="ExternalInput")
    lnb_d = nc.dram_tensor("lnb", [1, D], F32, kind="ExternalInput")
    xhalo_d = nc.dram_tensor("xhalo", [3, D], F32, kind="ExternalInput")
    hmask_d = nc.dram_tensor("hmask", [1, 1], F32, kind="ExternalInput")
    omask_d = nc.dram_tensor("omask", [1, 1], F32, kind="ExternalInput")
    y_d = nc.dram_tensor("y", [S, D], F32, kind="ExternalOutput")
    X = [p.sbuf(f"x{t}", [128, D], F32) for t in range(NT)]
    for t in range(NT):
        p.dma(X[t][:], x_d[128 * t : 128 * (t + 1), :])
    emit_l0_attn(
        p, X, inw_d, convw_d, convb_d, dtb_d, alog_d, dskip_d, rmsw_d, outw_d,
        w1_d, b1_d, w2_d, b2_d, lnw_d, lnb_d, xhalo_d, hmask_d, omask_d
    )
    for t in range(NT):
        p.dma(y_d[128 * t : 128 * (t + 1), :], X[t][:])
    return p.finish()



SEQ = 4096
NJC = SEQ // 128


def emit_l1_attn(p, X, wq_d, wk_d, wv_d, bq_d, bk_d, bv_d, thr_d, msel_d, tag="a1"):
    xn_dr = p.dram(f"{tag}_xn", [S, D], BF16)
    q_dr = p.dram(f"{tag}_q", [S, D], BF16)
    kv_in = p.dram(f"{tag}_kvi", [S, 2 * D], BF16)
    kv_dr = p.dram(f"{tag}_kv", [SEQ, 2 * D], BF16)
    e_dr = p.dram(f"{tag}_e", [SEQ, S], BF16)
    st_in = p.dram(f"{tag}_sti", [2 * 128, NJC], F32)
    st_out = p.dram(f"{tag}_sto", [4 * 128, NJC], F32)
    v2_dr = p.dram(f"{tag}_v2", [SEQ, D], BF16)
    PAIRS = [[0, 1], [2, 3], [4, 5], [6, 7]]

    with p.scope() as outer:
        scr = {
            "sq": p.sbuf("sq", [128, D], F32, scope=outer),
            "st": p.sbuf("st", [128, 8], F32, scope=outer),
        }
        with p.scope() as sc:
            xn = p.sbuf("xn", [128, D], F32, scope=sc)
            xnb = p.sbuf("xnb", [128, D], BF16, scope=sc)
            for t in range(NT):
                emit_plain_ln(p, X[t][:], xn[:], scr)
                p.tt(X[t][:], X[t][:], xn[:], ALU.add)
                p.copy(xnb[:], xn[:])
                p.dma(xn_dr[128 * t : 128 * (t + 1), :], xnb[:])

        p.barrier(1)
        with p.scope() as sc:
            xnf = p.sbuf("xnf", [128, DCH * 512], BF16, scope=sc)
            wt = [p.sbuf(f"wt{i}", [128, 512], BF16, scope=sc) for i in range(2)]
            bbc = p.sbuf("bbc", [128, 3 * D], F32, scope=sc)
            qstg = p.sbuf("qstg", [128, 4 * D], BF16, scope=sc)
            kvstg = p.sbuf("kvstg", [128, 4 * 2 * D], BF16, scope=sc)
            ps = [p.psum(f"pqkv{i}", [128, 512], scope=sc) for i in range(4)]
            p.dma(bbc[:, 0:D], bq_d[0:1, :].partition_broadcast(128))
            p.dma(bbc[:, D : 2 * D], bk_d[0:1, :].partition_broadcast(128))
            p.dma(bbc[:, 2 * D : 3 * D], bv_d[0:1, :].partition_broadcast(128))
            WL = [(wi, w_d, jh) for wi, w_d in enumerate((wq_d, wk_d, wv_d)) for jh in range(2)]

            def ldqkv(i, k, b):
                wi, w_d, jh = WL[i]
                p.dma(wt[b][:], w_d[128 * k : 128 * (k + 1), 512 * jh : 512 * jh + 512], eng="gpsimd")

            for g in range(4):
                g0 = 512 * g
                with p.par():
                    for k in range(DCH):
                        p.dmat(xnf[:, 512 * k : 512 * (k + 1)], xn_dr[g0 : g0 + 512, 128 * k : 128 * (k + 1)])
                    ldqkv(0, 0, 0)
                it = 0
                for i in range(6):
                    wi, w_d, jh = WL[i]
                    j0 = 512 * jh
                    for k in range(DCH):
                        b = it % 2
                        with p.par():
                            if not (i == 5 and k == DCH - 1):
                                ni, nk = (i, k + 1) if k < DCH - 1 else (i + 1, 0)
                                ldqkv(ni, nk, 1 - b)
                            for tt in range(4):
                                lhsT = xnf[:, 512 * k + 128 * tt : 512 * k + 128 * (tt + 1)]
                                p.mm(ps[tt][:], lhsT, wt[b][:], start=(k == 0), stop=(k == DCH - 1))
                        it += 1
                    with p.par():
                        for tt in range(4):
                            dst = (
                                qstg[:, D * tt + j0 : D * tt + j0 + 512]
                                if wi == 0
                                else kvstg[:, 2 * D * tt + (wi - 1) * D + j0 : 2 * D * tt + (wi - 1) * D + j0 + 512]
                            )
                            p.tt(dst, ps[tt][:], bbc[:, wi * D + j0 : wi * D + j0 + 512], ALU.add)
                with p.par():
                    for tt in range(4):
                        r0 = g0 + 128 * tt
                        p.dma(q_dr[r0 : r0 + 128, :], qstg[:, D * tt : D * (tt + 1)])
                        p.dma(kv_in[r0 : r0 + 128, :], kvstg[:, 2 * D * tt : 2 * D * (tt + 1)])
        p.collective("AllGather", ALU.bypass, PAIRS, [kv_in.ap()], [kv_dr.ap()])

        p.barrier(2)
        iotaT = p.sbuf("iotaT", [128, S], F32, scope=outer)
        p.iota(iotaT[:], [[1, S]], base=0, channel_multiplier=-1)
        thrbc = p.sbuf("thrbc", [128, NJC], F32, scope=outer)
        p.dma(thrbc[:], thr_d[0:1, :].partition_broadcast(128))
        with p.scope() as sc:
            qf = p.sbuf("qf", [128, DCH * S], BF16, scope=sc)
            kf = p.sbuf("kf", [128, DCH * SEQ], BF16, scope=sc)
            for dc in range(DCH):
                p.dmat(qf[:, S * dc : S * (dc + 1)], q_dr[:, 128 * dc : 128 * (dc + 1)])
                p.dmat(kf[:, SEQ * dc : SEQ * (dc + 1)], kv_dr[:, 128 * dc : 128 * (dc + 1)])
            sm = p.sbuf("sm", [128, S], F32, scope=sc)
            eb = p.sbuf("eb", [128, S], BF16, scope=sc)
            maskA = p.sbuf("maskA", [128, S], F32, scope=sc)
            mcol = p.sbuf("mcol", [128, 1], F32, scope=sc)
            nmcol = p.sbuf("nmcol", [128, 1], F32, scope=sc)
            dcol = p.sbuf("dcol", [128, 1], F32, scope=sc)
            pss = [p.psum(f"pss{i}", [128, 512], scope=sc) for i in range(4)]
            for jc in range(NJC):
                for it in range(4):
                    i0 = 512 * it
                    for dc in range(DCH):
                        lhsT = kf[:, SEQ * dc + 128 * jc : SEQ * dc + 128 * (jc + 1)]
                        rhs = qf[:, S * dc + i0 : S * dc + i0 + 512]
                        p.mm(pss[it][:], lhsT, rhs, start=(dc == 0), stop=(dc == DCH - 1))
                p.ts(maskA[:], iotaT[:, 0:S], thrbc[:, jc : jc + 1], -1e30, ALU.is_lt, ALU.mult)
                for it in range(4):
                    p.tt(sm[:, 512 * it : 512 * (it + 1)], pss[it][:], maskA[:, 512 * it : 512 * (it + 1)], ALU.add)
                p.red(mcol[:], sm[:], ALU.max)
                p.ts(nmcol[:], mcol[:], -1.0, None, ALU.mult)
                p.memset(dcol[:], 0.0)
                p.act(eb[:], sm[:], AF.Exp, bias=nmcol[:], accum_out=dcol[:])
                with p.par():
                    p.dma(e_dr[128 * jc : 128 * (jc + 1), :], eb[:])
                    p.dma(st_in[0:128, jc : jc + 1], mcol[:], allow_slow_non_contiguous=True, eng="gpsimd")
                    p.dma(st_in[128:256, jc : jc + 1], dcol[:], allow_slow_non_contiguous=True, eng="gpsimd")
        p.collective("AllGather", ALU.bypass, PAIRS, [st_in.ap()], [st_out.ap()])

        p.barrier(3)
        fmat = p.sbuf("fmat", [128, NJC], F32, scope=outer)
        with p.scope() as sc:
            me = p.sbuf("cme", [128, NJC], F32, scope=sc)
            de = p.sbuf("cde", [128, NJC], F32, scope=sc)
            mo = p.sbuf("cmo", [128, NJC], F32, scope=sc)
            do_ = p.sbuf("cdo", [128, NJC], F32, scope=sc)
            mm_ = p.sbuf("cmm", [128, NJC], F32, scope=sc)
            tmp = p.sbuf("ctmp", [128, NJC], F32, scope=sc)
            eme = p.sbuf("ceme", [128, NJC], F32, scope=sc)
            emo = p.sbuf("cemo", [128, NJC], F32, scope=sc)
            mselc = p.sbuf("mselc", [128, 1], F32, scope=sc)
            p.dma(me[:], st_out[0:128, :])
            p.dma(de[:], st_out[128:256, :])
            p.dma(mo[:], st_out[256:384, :])
            p.dma(do_[:], st_out[384:512, :])
            p.dma(mselc[:], msel_d[0:1, 0:1].partition_broadcast(128))
            p.barrier(5)
            p.tt(mm_[:], me[:], mo[:], ALU.max)
            p.tt(tmp[:], me[:], mm_[:], ALU.subtract)
            p.act(eme[:], tmp[:], AF.Exp)
            p.tt(tmp[:], mo[:], mm_[:], ALU.subtract)
            p.act(emo[:], tmp[:], AF.Exp)
            p.tt(de[:], de[:], eme[:], ALU.mult)
            p.tt(do_[:], do_[:], emo[:], ALU.mult)
            p.tt(de[:], de[:], do_[:], ALU.add)
            p.ts(tmp[:], eme[:], mselc[:], None, ALU.mult)
            p.ts(eme[:], emo[:], mselc[:], None, ALU.mult)
            p.tt(emo[:], emo[:], eme[:], ALU.subtract)
            p.tt(tmp[:], tmp[:], emo[:], ALU.add)
            p.ts(de[:], de[:], 32.0, None, ALU.mult)
            p.recip(de[:], de[:])
            p.tt(fmat[:], tmp[:], de[:], ALU.mult)
            import os as _os
            if _os.environ.get("DBGA"):
                fdbg = p.dram(f"{tag}_fdbg", [128, 4 * NJC], F32)
                DBG["a1_fdbg"] = fdbg
                p.dma(fdbg[:, 0:NJC], fmat[:])
                p.dma(fdbg[:, NJC : 2 * NJC], me[:])
                p.dma(fdbg[:, 2 * NJC : 3 * NJC], de[:])
                p.dma(fdbg[:, 3 * NJC : 4 * NJC], mselc[:].broadcast_to([128, NJC]) if False else eme[:])
        p.barrier(6)
        with p.scope() as sc:
            vt = p.sbuf("vt", [128, D], BF16, scope=sc)
            for jc in range(NJC):
                p.barrier(7)
                p.dma(vt[:], kv_dr[128 * jc : 128 * (jc + 1), D : 2 * D])
                p.barrier(7)
                p.ts(vt[:], vt[:], fmat[:, jc : jc + 1], None, ALU.mult)
                p.barrier(7)
                p.dma(v2_dr[128 * jc : 128 * (jc + 1), :], vt[:])

        p.barrier(4)
        with p.scope() as sc:
            ers = [p.sbuf(f"ers{i}", [128, 512], BF16, scope=sc) for i in range(2)]
            vt = [p.sbuf(f"vt2{i}", [128, D], BF16, scope=sc) for i in range(2)]
            psr = [p.psum(f"psr{i}", [128, 512], scope=sc) for i in range(8)]

            def ldpv(ig, jc, b):
                p.dma(ers[b][:], e_dr[128 * jc : 128 * (jc + 1), 512 * ig : 512 * ig + 512])
                p.dma(vt[b][:], v2_dr[128 * jc : 128 * (jc + 1), :], eng="gpsimd")

            with p.par():
                ldpv(0, 0, 0)
            for ig in range(4):
                for jc in range(NJC):
                    b = jc % 2
                    with p.par():
                        if not (ig == 3 and jc == NJC - 1):
                            nig, njc = (ig, jc + 1) if jc < NJC - 1 else (ig + 1, 0)
                            ldpv(nig, njc, 1 - b)
                        for isub in range(4):
                            lhsT = ers[b][:, 128 * isub : 128 * (isub + 1)]
                            for vfh in range(2):
                                p.mm(
                                    psr[2 * isub + vfh][:],
                                    lhsT,
                                    vt[b][:, 512 * vfh : 512 * (vfh + 1)],
                                    start=(jc == 0),
                                    stop=(jc == NJC - 1),
                                )
                with p.par():
                    for isub in range(4):
                        t = 4 * ig + isub
                        for vfh in range(2):
                            h0 = 512 * vfh
                            p.tt(X[t][:, h0 : h0 + 512], X[t][:, h0 : h0 + 512], psr[2 * isub + vfh][:], ALU.add)


def _l1attn_host_inputs():
    outs = []
    for c in range(8):
        base = 2048 * (c % 2)
        thr = np.array([[128 * jc - base for jc in range(NJC)]], np.float32)
        msel = np.array([[1.0 if c % 2 == 0 else 0.0]], np.float32)
        outs.append((thr, msel))
    return outs


def build_l1_attn_launch():
    p = Prog("l1attn")
    nc = p.nc
    x_d = nc.dram_tensor("x", [S, D], F32, kind="ExternalInput")
    wq_d = nc.dram_tensor("wq", [D, D], F32, kind="ExternalInput")
    wk_d = nc.dram_tensor("wk", [D, D], F32, kind="ExternalInput")
    wv_d = nc.dram_tensor("wv", [D, D], F32, kind="ExternalInput")
    bq_d = nc.dram_tensor("bq", [1, D], F32, kind="ExternalInput")
    bk_d = nc.dram_tensor("bk", [1, D], F32, kind="ExternalInput")
    bv_d = nc.dram_tensor("bv", [1, D], F32, kind="ExternalInput")
    thr_d = nc.dram_tensor("thr", [1, NJC], F32, kind="ExternalInput")
    msel_d = nc.dram_tensor("msel", [1, 1], F32, kind="ExternalInput")
    y_d = nc.dram_tensor("y", [S, D], F32, kind="ExternalOutput")
    X = [p.sbuf(f"x{t}", [128, D], F32) for t in range(NT)]
    for t in range(NT):
        p.dma(X[t][:], x_d[128 * t : 128 * (t + 1), :])
    emit_l1_attn(p, X, wq_d, wk_d, wv_d, bq_d, bk_d, bv_d, thr_d, msel_d)
    for t in range(NT):
        p.dma(y_d[128 * t : 128 * (t + 1), :], X[t][:])
    return p.finish()


def _l0ffn_host_inputs(wg, wu, wd):
    outs = []
    stacked = np.concatenate([wg, wu], axis=0)
    for c in range(8):
        wisl = stacked[704 * c : 704 * (c + 1), :]
        wdsl = wd[128 * c : 128 * (c + 1), :]
        outs.append((np.ascontiguousarray(wisl), np.ascontiguousarray(wdsl)))
    return outs


def build_l0_ffn_launch():
    p = Prog("l0ffn")
    nc = p.nc
    x_d = nc.dram_tensor("x", [S, D], F32, kind="ExternalInput")
    wisl_d = nc.dram_tensor("wisl", [704, D], F32, kind="ExternalInput")
    wdsl_d = nc.dram_tensor("wdsl", [128, I_FF], F32, kind="ExternalInput")
    clipx_d = nc.dram_tensor("clipx", [1, 1], F32, kind="ExternalInput")
    clipd_d = nc.dram_tensor("clipd", [1, 1], F32, kind="ExternalInput")
    y_d = nc.dram_tensor("y", [S, D], F32, kind="ExternalOutput")
    X = [p.sbuf(f"x{t}", [128, D], F32) for t in range(NT)]
    for t in range(NT):
        p.dma(X[t][:], x_d[128 * t : 128 * (t + 1), :])
    emit_l0_ffn(p, X, wisl_d, wdsl_d, clipx_d, clipd_d)
    for t in range(NT):
        p.dma(y_d[128 * t : 128 * (t + 1), :], X[t][:])
    return p.finish()


def _l1ffn_host_inputs(wg, wu, wd):
    outs = []
    for c in range(8):
        wsl = np.concatenate(
            [wg[:, 128 * c : 128 * (c + 1)], wu[:, 128 * c : 128 * (c + 1)]], axis=0
        )
        wdsl = wd[128 * c : 128 * (c + 1), :]
        outs.append((np.ascontiguousarray(wsl), np.ascontiguousarray(wdsl)))
    return outs




def build_l1_ffn_launch(debug=False):
    p = Prog("l1ffn")
    nc = p.nc
    x_d = nc.dram_tensor("x", [S, D], F32, kind="ExternalInput")
    wsl_d = nc.dram_tensor("wsl", [2 * I_FF, 128], F32, kind="ExternalInput")
    wdsl_d = nc.dram_tensor("wdsl", [128, I_FF], F32, kind="ExternalInput")
    y_d = nc.dram_tensor("y", [S, D], F32, kind="ExternalOutput")

    X = [p.sbuf(f"x{t}", [128, D], F32) for t in range(NT)]
    for t in range(NT):
        p.dma(X[t][:], x_d[128 * t : 128 * (t + 1), :])
    emit_l1_ffn(p, X, wsl_d, wdsl_d)
    for t in range(NT):
        p.dma(y_d[128 * t : 128 * (t + 1), :], X[t][:])
    if debug:
        for nm, h in DBG.items():
            out = nc.dram_tensor("dbg_" + nm, list(h.ap().shape), h.ap().dtype, kind="ExternalOutput")
            p.dma(out.ap(), h.ap())
    return p.finish()


DBG = {}


def build_fused():
    p = Prog("mk")
    nc = p.nc
    x_d = nc.dram_tensor("x", [S, D], F32, kind="ExternalInput")
    inw_d = nc.dram_tensor("inw", [D, 4384], F32, kind="ExternalInput")
    convw_d = nc.dram_tensor("convw", [CONVD, 4], F32, kind="ExternalInput")
    convb_d = nc.dram_tensor("convb", [1, CONVD], F32, kind="ExternalInput")
    dtb_d = nc.dram_tensor("dtb", [1, NH], F32, kind="ExternalInput")
    alog_d = nc.dram_tensor("alog", [1, NH], F32, kind="ExternalInput")
    dskip_d = nc.dram_tensor("dskip", [1, NH], F32, kind="ExternalInput")
    rmsw_d = nc.dram_tensor("rmsw", [1, DIN], F32, kind="ExternalInput")
    outw_d = nc.dram_tensor("outw", [DIN, D], F32, kind="ExternalInput")
    w1_d = nc.dram_tensor("w1", [D, DFF], F32, kind="ExternalInput")
    b1_d = nc.dram_tensor("b1", [1, DFF], F32, kind="ExternalInput")
    w2_d = nc.dram_tensor("w2", [DFF, D], F32, kind="ExternalInput")
    b2_d = nc.dram_tensor("b2", [1, D], F32, kind="ExternalInput")
    lnw_d = nc.dram_tensor("lnw", [1, D], F32, kind="ExternalInput")
    lnb_d = nc.dram_tensor("lnb", [1, D], F32, kind="ExternalInput")
    xhalo_d = nc.dram_tensor("xhalo", [3, D], F32, kind="ExternalInput")
    hmask_d = nc.dram_tensor("hmask", [1, 1], F32, kind="ExternalInput")
    omask_d = nc.dram_tensor("omask", [1, 1], F32, kind="ExternalInput")
    wisl_d = nc.dram_tensor("wisl", [704, D], F32, kind="ExternalInput")
    wdsl0_d = nc.dram_tensor("wdsl0", [128, I_FF], F32, kind="ExternalInput")
    clipx_d = nc.dram_tensor("clipx", [1, 1], F32, kind="ExternalInput")
    clipd_d = nc.dram_tensor("clipd", [1, 1], F32, kind="ExternalInput")
    wq_d = nc.dram_tensor("wq", [D, D], F32, kind="ExternalInput")
    wk_d = nc.dram_tensor("wk", [D, D], F32, kind="ExternalInput")
    wv_d = nc.dram_tensor("wv", [D, D], F32, kind="ExternalInput")
    bq_d = nc.dram_tensor("bq", [1, D], F32, kind="ExternalInput")
    bk_d = nc.dram_tensor("bk", [1, D], F32, kind="ExternalInput")
    bv_d = nc.dram_tensor("bv", [1, D], F32, kind="ExternalInput")
    thr_d = nc.dram_tensor("thr", [1, NJC], F32, kind="ExternalInput")
    msel_d = nc.dram_tensor("msel", [1, 1], F32, kind="ExternalInput")
    wsl1_d = nc.dram_tensor("wsl1", [2 * I_FF, 128], F32, kind="ExternalInput")
    wdsl1_d = nc.dram_tensor("wdsl1", [128, I_FF], F32, kind="ExternalInput")
    y_d = nc.dram_tensor("y", [S, D], F32, kind="ExternalOutput")

    X = [p.sbuf(f"x{t}", [128, D], F32) for t in range(NT)]
    for t in range(NT):
        p.dma(X[t][:], x_d[128 * t : 128 * (t + 1), :])
    emit_l0_attn(
        p, X, inw_d, convw_d, convb_d, dtb_d, alog_d, dskip_d, rmsw_d, outw_d,
        w1_d, b1_d, w2_d, b2_d, lnw_d, lnb_d, xhalo_d, hmask_d, omask_d
    )
    emit_l0_ffn(p, X, wisl_d, wdsl0_d, clipx_d, clipd_d)
    emit_l1_attn(p, X, wq_d, wk_d, wv_d, bq_d, bk_d, bv_d, thr_d, msel_d)
    emit_l1_ffn(p, X, wsl1_d, wdsl1_d)
    for t in range(NT):
        p.dma(y_d[128 * t : 128 * (t + 1), :], X[t][:])
    return p.finish()


def _fused_in_maps(inputs):
    x = np.asarray(inputs["L0_attn_x"], np.float32)
    xs = _shard_x(x)
    hi0 = _l0attn_host_inputs(x)
    wsl0 = _l0ffn_host_inputs(
        np.asarray(inputs["L0_ffn_w_gate"], np.float32),
        np.asarray(inputs["L0_ffn_w_up"], np.float32),
        np.asarray(inputs["L0_ffn_w_down"], np.float32),
    )
    hi1 = _l1attn_host_inputs()
    wsl1 = _l1ffn_host_inputs(
        np.asarray(inputs["L1_ffn_w_gate"], np.float32),
        np.asarray(inputs["L1_ffn_w_up"], np.float32),
        np.asarray(inputs["L1_ffn_w_down"], np.float32),
    )
    base = {
        "inw": np.asarray(inputs["L0_attn_in_proj_w"], np.float32),
        "convw": np.asarray(inputs["L0_attn_conv_w"], np.float32),
        "convb": np.asarray(inputs["L0_attn_conv_b"], np.float32).reshape(1, -1),
        "dtb": np.asarray(inputs["L0_attn_dt_bias"], np.float32).reshape(1, -1),
        "alog": np.asarray(inputs["L0_attn_A_log"], np.float32).reshape(1, -1),
        "dskip": np.asarray(inputs["L0_attn_D_skip"], np.float32).reshape(1, -1),
        "rmsw": np.asarray(inputs["L0_attn_rms_w"], np.float32).reshape(1, -1),
        "outw": np.asarray(inputs["L0_attn_out_proj_w"], np.float32),
        "w1": np.asarray(inputs["L0_attn_mlp_w1"], np.float32),
        "b1": np.asarray(inputs["L0_attn_mlp_b1"], np.float32).reshape(1, -1),
        "w2": np.asarray(inputs["L0_attn_mlp_w2"], np.float32),
        "b2": np.asarray(inputs["L0_attn_mlp_b2"], np.float32).reshape(1, -1),
        "lnw": np.asarray(inputs["L0_attn_ln_w"], np.float32).reshape(1, -1),
        "lnb": np.asarray(inputs["L0_attn_ln_b"], np.float32).reshape(1, -1),
        "clipx": np.asarray(inputs["L0_ffn_clip_x"], np.float32).reshape(1, 1),
        "clipd": np.asarray(inputs["L0_ffn_clip_down"], np.float32).reshape(1, 1),
        "wq": np.asarray(inputs["L1_attn_Wq"], np.float32),
        "wk": np.asarray(inputs["L1_attn_Wk"], np.float32),
        "wv": np.asarray(inputs["L1_attn_Wv"], np.float32),
        "bq": np.asarray(inputs["L1_attn_bq"], np.float32).reshape(1, -1),
        "bk": np.asarray(inputs["L1_attn_bk"], np.float32).reshape(1, -1),
        "bv": np.asarray(inputs["L1_attn_bv"], np.float32).reshape(1, -1),
    }
    in_maps = []
    for c in range(8):
        m = dict(base)
        m["x"] = xs[c]
        m["xhalo"], m["hmask"], m["omask"] = hi0[c]
        m["wisl"], m["wdsl0"] = wsl0[c]
        m["thr"], m["msel"] = hi1[c]
        m["wsl1"], m["wdsl1"] = wsl1[c]
        in_maps.append(m)
    return in_maps


def kernel(**inputs):
    nc = build_fused()
    in_maps = _fused_in_maps(inputs)
    res = run_bass_kernel_spmd(nc, in_maps, core_ids=list(range(8)))
    out = np.zeros((4, 4096, 1024), np.float32)
    for c in range(8):
        out[c // 2, 2048 * (c % 2) : 2048 * (c % 2 + 1)] = np.asarray(res.results[c]["y"])
    return out


def _shard_x(x):
    return [np.ascontiguousarray(x[c // 2, 2048 * (c % 2) : 2048 * (c % 2 + 1)]) for c in range(8)]


def kernel_phase_a(**inputs):
    x = np.asarray(inputs["L0_attn_x"], np.float32)
    xs = _shard_x(x)

    nc = build_l0_attn_launch()
    hi = _l0attn_host_inputs(x)
    base = {
        "inw": np.asarray(inputs["L0_attn_in_proj_w"], np.float32),
        "convw": np.asarray(inputs["L0_attn_conv_w"], np.float32),
        "convb": np.asarray(inputs["L0_attn_conv_b"], np.float32).reshape(1, -1),
        "dtb": np.asarray(inputs["L0_attn_dt_bias"], np.float32).reshape(1, -1),
        "alog": np.asarray(inputs["L0_attn_A_log"], np.float32).reshape(1, -1),
        "dskip": np.asarray(inputs["L0_attn_D_skip"], np.float32).reshape(1, -1),
        "rmsw": np.asarray(inputs["L0_attn_rms_w"], np.float32).reshape(1, -1),
        "outw": np.asarray(inputs["L0_attn_out_proj_w"], np.float32),
        "w1": np.asarray(inputs["L0_attn_mlp_w1"], np.float32),
        "b1": np.asarray(inputs["L0_attn_mlp_b1"], np.float32).reshape(1, -1),
        "w2": np.asarray(inputs["L0_attn_mlp_w2"], np.float32),
        "b2": np.asarray(inputs["L0_attn_mlp_b2"], np.float32).reshape(1, -1),
        "lnw": np.asarray(inputs["L0_attn_ln_w"], np.float32).reshape(1, -1),
        "lnb": np.asarray(inputs["L0_attn_ln_b"], np.float32).reshape(1, -1),
    }
    in_maps = []
    for c in range(8):
        m = dict(base)
        m["x"] = xs[c]
        m["xhalo"], m["hmask"], m["omask"] = hi[c]
        in_maps.append(m)
    res = run_bass_kernel_spmd(nc, in_maps, core_ids=list(range(8)))
    xs = [np.asarray(res.results[c]["y"]) for c in range(8)]

    nc = build_l0_ffn_launch()
    wsl = _l0ffn_host_inputs(
        np.asarray(inputs["L0_ffn_w_gate"], np.float32),
        np.asarray(inputs["L0_ffn_w_up"], np.float32),
        np.asarray(inputs["L0_ffn_w_down"], np.float32),
    )
    cx = np.asarray(inputs["L0_ffn_clip_x"], np.float32).reshape(1, 1)
    cd = np.asarray(inputs["L0_ffn_clip_down"], np.float32).reshape(1, 1)
    in_maps = [
        {"x": xs[c], "wisl": wsl[c][0], "wdsl": wsl[c][1], "clipx": cx, "clipd": cd}
        for c in range(8)
    ]
    res = run_bass_kernel_spmd(nc, in_maps, core_ids=list(range(8)))
    xs = [np.asarray(res.results[c]["y"]) for c in range(8)]

    nc = build_l1_attn_launch()
    hi = _l1attn_host_inputs()
    in_maps = [
        {
            "x": xs[c],
            "wq": np.asarray(inputs["L1_attn_Wq"], np.float32),
            "wk": np.asarray(inputs["L1_attn_Wk"], np.float32),
            "wv": np.asarray(inputs["L1_attn_Wv"], np.float32),
            "bq": np.asarray(inputs["L1_attn_bq"], np.float32).reshape(1, -1),
            "bk": np.asarray(inputs["L1_attn_bk"], np.float32).reshape(1, -1),
            "bv": np.asarray(inputs["L1_attn_bv"], np.float32).reshape(1, -1),
            "thr": hi[c][0],
            "msel": hi[c][1],
        }
        for c in range(8)
    ]
    res = run_bass_kernel_spmd(nc, in_maps, core_ids=list(range(8)))
    xs = [np.asarray(res.results[c]["y"]) for c in range(8)]

    nc = build_l1_ffn_launch()
    wsl = _l1ffn_host_inputs(
        np.asarray(inputs["L1_ffn_w_gate"], np.float32),
        np.asarray(inputs["L1_ffn_w_up"], np.float32),
        np.asarray(inputs["L1_ffn_w_down"], np.float32),
    )
    in_maps = [{"x": xs[c], "wsl": wsl[c][0], "wdsl": wsl[c][1]} for c in range(8)]
    res = run_bass_kernel_spmd(nc, in_maps, core_ids=list(range(8)))
    xs = [np.asarray(res.results[c]["y"]) for c in range(8)]

    out = np.zeros((4, 4096, 1024), np.float32)
    for c in range(8):
        out[c // 2, 2048 * (c % 2) : 2048 * (c % 2 + 1)] = xs[c]
    return out


# version 08
# speedup vs baseline: 2.1558x; 1.0265x over previous
import numpy as np
from contextlib import ExitStack

import concourse.bass as bass
import concourse.mybir as mybir
from concourse.bass_utils import run_bass_kernel_spmd

F32 = mybir.dt.float32
I32 = mybir.dt.int32
BF16 = mybir.dt.bfloat16
AF = mybir.ActivationFunctionType
ALU = mybir.AluOpType
AX = mybir.AxisListType

MAGIC = 12582912.0
SEM_LIM = 24000

S = 2048
NT = S // 128
D = 1024
DCH = D // 128

COMPUTE = ("scalar", "vector", "tensor", "gpsimd")


class Prog:

    def __init__(self, name="prog"):
        self.nc = bass.Bass("TRN2", target_bir_lowering=False)
        self._stack = ExitStack()
        self._stack.__enter__()
        self.cnt = 0
        self.items = []
        self._tens = {}
        self._dep_floor = None
        self._last = None
        self._uid = 0
        self._consts = {}
        self._const_tile = self.sbuf("constpool", [128, 32], F32)
        self._barrier_tile = self.sbuf("barrier", [1, 8], F32)

    def uid(self):
        self._uid += 1
        return self._uid

    def sbuf(self, name, shape, dtype, scope=None):
        st = scope if scope is not None else self._stack
        return st.enter_context(self.nc.sbuf_tensor(f"{name}_{self.uid()}", shape, dtype))

    def psum(self, name, shape, dtype=F32, scope=None):
        bank_elems = 512 if dtype == F32 else 1024
        shape = [shape[0], ((shape[1] + bank_elems - 1) // bank_elems) * bank_elems]
        st = scope if scope is not None else self._stack
        return st.enter_context(self.nc.psum_tensor(f"{name}_{self.uid()}", shape, dtype))

    def scope(self):
        prog = self

        class _Scope(ExitStack):
            def __exit__(self_, *a):
                prog.op(
                    "vector",
                    lambda e, o=prog._barrier_tile[0:1, :]: e.memset(o, 0.0),
                    barrier=True,
                )
                prog._dep_floor = len(prog.items) - 1
                return ExitStack.__exit__(self_, *a)

        return _Scope()

    def semaphore(self, name):
        return self._stack.enter_context(self.nc.semaphore(f"{name}_{self.uid()}"))

    def dram(self, name, shape, dtype, kind="Internal"):
        return self.nc.dram_tensor(name, shape, dtype, kind=kind)

    def constv(self, v):
        v = float(v)
        if v in (0.0, 1.0):
            return v
        if v not in self._consts:
            j = len(self._consts)
            assert j < 32
            self._consts[v] = j
            col = self._const_tile[:, j : j + 1]
            self.memset(col, v)
            self.nc.const_aps.register(F32, v, col)
        return v

    def par(self, key=None):

        class _Par:
            def __enter__(self_):
                return self_

            def __exit__(self_, *a):
                return False

        return _Par()

    @staticmethod
    def _apkeys(x):
        if x is None or isinstance(x, (int, float)):
            return ()
        try:
            return (x.name,)
        except Exception:
            return ()

    def op(self, eng, emit, inc=1, is_async=False, barrier=False, chain_ok=False,
           reads=(), writes=()):
        idx = len(self.items)
        rks = set()
        wks = set()
        for x in reads:
            rks.update(self._apkeys(x))
        for x in writes:
            wks.update(self._apkeys(x))
        deps = set()
        if self._dep_floor is not None:
            deps.add(self._dep_floor)
        if barrier:
            deps.update(range(idx))
        else:
            for k in rks:
                t = self._tens.get(k)
                if t is not None:
                    deps.update(t[0])
            for k in wks:
                t = self._tens.get(k)
                if t is not None:
                    deps.update(t[1])
        for k in rks:
            self._tens.setdefault(k, [set(), set()])[1].add(idx)
        for k in wks:
            t = self._tens.setdefault(k, [set(), set()])
            t[0].add(idx)
            t[1] = set()
        self.items.append([eng, emit, inc, is_async, deps])
        self.cnt += inc
        return self.cnt

    def dma(self, out, in_, eng="sync", **kw):
        return self.op(
            eng, lambda e, o=out, i=in_, k=kw: e.dma_start(o, i, **k), 16, is_async=True,
            reads=(in_,), writes=(out,)
        )

    def dmat(self, out, in_, eng="scalar"):
        return self.op(
            eng, lambda e, o=out, i=in_: e.dma_start_transpose(o, i), 16, is_async=True,
            reads=(in_,), writes=(out,)
        )

    def mm(self, out, lhsT, rhs, start=True, stop=True):
        return self.op(
            "tensor",
            lambda e, o=out, l=lhsT, r=rhs, s0=start, s1=stop: e.matmul(o, l, r, start=s0, stop=s1),
            reads=(lhsT, rhs), writes=(out,)
        )

    def transpose(self, out, in_, ident):
        return self.op(
            "tensor", lambda e, o=out, i=in_, d=ident: e.transpose(o, i, d),
            reads=(in_, ident), writes=(out,)
        )

    def act(self, out, in_, func, bias=0.0, scale=1.0, accum_out=None, eng="scalar"):
        extra = () if isinstance(bias, (int, float)) and isinstance(scale, (int, float)) else tuple(
            x for x in (bias, scale) if not isinstance(x, (int, float))
        )
        cp = () if self._const_tile is None else (self._const_tile[:],)
        return self.op(
            eng,
            lambda e, o=out, i=in_, f=func, b=bias, s=scale, a=accum_out: e.activation(
                o, i, f, bias=b, scale=s, accum_out=a
            ),
            reads=(in_,) + extra + cp + (() if accum_out is None else (accum_out,)),
            writes=(out,) + (() if accum_out is None else (accum_out,)),
        )

    def tt(self, out, a, b, alu, eng="vector", chain_ok=False):
        return self.op(
            eng, lambda e, o=out, x=a, y=b, p=alu: e.tensor_tensor(o, x, y, p),
            reads=(a, b), writes=(out,)
        )

    def ts(self, out, in_, s1, s2, op0, op1=ALU.bypass, accum_out=None, eng="vector", chain_ok=False):
        return self.op(
            eng,
            lambda e, o=out, i=in_, a=s1, b=s2, p0=op0, p1=op1, ac=accum_out: e.tensor_scalar(
                o, i, a, b, p0, p1, accum_out=ac
            ),
            reads=(in_, s1, s2) + (() if accum_out is None else (accum_out,)),
            writes=(out,) + (() if accum_out is None else (accum_out,)),
        )

    def tss(self, out, in_, s, alu, eng="vector", chain_ok=False):
        return self.op(
            eng, lambda e, o=out, i=in_, a=s, p=alu: e.tensor_single_scalar(o, i, a, p),
            reads=(in_, s), writes=(out,)
        )

    def stt(self, out, in0, scalar, in1, op0, op1, eng="vector", chain_ok=False):
        return self.op(
            eng,
            lambda e, o=out, i0=in0, sc=scalar, i1=in1, p0=op0, p1=op1: e.scalar_tensor_tensor(
                o, i0, sc, i1, p0, p1
            ),
            reads=(in0, scalar, in1), writes=(out,)
        )

    def red(self, out, in_, alu, absval=False, eng="vector"):
        return self.op(
            eng,
            lambda e, o=out, i=in_, p=alu, a=absval: e.tensor_reduce(
                o, i, AX.X, p, apply_absolute_value=a
            ),
            reads=(in_, out), writes=(out,)
        )

    def scan(self, out, d0, d1, init, op0, op1, eng="vector"):
        return self.op(
            eng,
            lambda e, o=out, a=d0, b=d1, ini=init, p0=op0, p1=op1: e.tensor_tensor_scan(
                o, a, b, ini, p0, p1
            ),
            reads=(d0, d1, init), writes=(out,)
        )

    def copy(self, out, in_, eng="vector"):
        return self.op(
            eng, lambda e, o=out, i=in_: e.tensor_copy(o, i), reads=(in_,), writes=(out,)
        )

    def memset(self, ap, v, eng="vector"):
        return self.op(eng, lambda e, o=ap, x=v: e.memset(o, x), writes=(ap,))

    def recip(self, out, in_):
        return self.op(
            "vector", lambda e, o=out, i=in_: e.reciprocal(o, i), reads=(in_,), writes=(out,)
        )

    def iota(self, out, pattern, base=0, channel_multiplier=0):
        return self.op(
            "gpsimd",
            lambda e, o=out, p=pattern, b=base, c=channel_multiplier: e.iota(
                o, p, base=b, channel_multiplier=c, allow_small_or_imprecise_dtypes=True
            ),
            writes=(out,),
        )

    def affsel(self, out, in_, pattern, compare_op, fill, base=0, channel_multiplier=1):
        return self.op(
            "gpsimd",
            lambda e, o=out, i=in_, p=pattern, c=compare_op, f=fill, b=base, m=channel_multiplier: e.affine_select(
                out=o, in_=i, pattern=p, compare_op=c, fill=f, base=b, channel_multiplier=m
            ),
            reads=(in_,), writes=(out,)
        )

    def barrier(self, tag=0):
        import os as _os
        sel = _os.environ.get("BARRIERS", "")
        if sel != "all" and str(tag) not in sel.split(","):
            return
        self.op("vector", lambda e, o=self._barrier_tile[0:1, :]: e.memset(o, 0.0), barrier=True)

    def collective(self, kind, alu, groups, ins, outs):
        return self.op(
            "gpsimd",
            lambda e, k=kind, o=alu, g=groups, i=ins, u=outs: e.collective_compute(
                k, o, replica_groups=g, ins=i, outs=u
            ),
            1,
            is_async=True,
            reads=tuple(ins), writes=tuple(outs),
        )

    def finish(self):
        NPOOL = 60
        eng_names = ("sync", "scalar", "vector", "tensor", "gpsimd")
        eng_sems = {e: self.semaphore(f"se_{e}") for e in eng_names}
        pool = [self.semaphore(f"sp{i}") for i in range(NPOOL)]
        pool_uses = [0] * NPOOL

        import os as _os
        if _os.environ.get("SCHED_SERIAL"):
            for i in range(len(self.items)):
                self.items[i][4] = set(range(i))
        n = len(self.items)
        token = [None] * n
        eng_seq = {e: 0 for e in eng_names}
        next_pool = 0
        for i, (eng, emit, inc, is_async, deps) in enumerate(self.items):
            if is_async:
                j = next_pool
                next_pool = (next_pool + 1) % NPOOL
                pool_uses[j] += inc
                token[i] = ("p", j, pool_uses[j])
            else:
                eng_seq[eng] += 1
                token[i] = ("e", eng, eng_seq[eng])

        by_eng = {e: [] for e in eng_names}
        clocks = {e: {} for e in eng_names}
        eng_pos = {e: 0 for e in eng_names}
        for i, (eng, emit, inc, is_async, deps) in enumerate(self.items):
            waits = {}
            for d in sorted(deps):
                de, demit, dinc, dasync, _ = self.items[d]
                tk = token[d]
                if tk[0] == "e":
                    if de == "tensor" and eng == "tensor" and not is_async:
                        continue
                    key = ("e", tk[1])
                    waits[key] = max(waits.get(key, 0), tk[2])
                else:
                    key = ("p", tk[1])
                    waits[key] = max(waits.get(key, 0), tk[2])
            ck = clocks[eng]
            wl = []
            for key, v in waits.items():
                if ck.get(key, -1) >= v:
                    continue
                wl.append((key, v))
                ck[key] = max(ck.get(key, -1), v)
            by_eng[eng].append((wl, emit, inc, token[i]))

        final_tokens = []
        for e in eng_names:
            if eng_seq[e] > 0:
                final_tokens.append((("e", e), eng_seq[e]))
        for j in range(NPOOL):
            if pool_uses[j] > 0:
                final_tokens.append((("p", j), pool_uses[j]))

        def semof(key):
            return eng_sems[key[1]] if key[0] == "e" else pool[key[1]]

        block = self._stack.enter_context(self.nc.Block())

        def emit_engine(engobj, eng):
            for wl, emit, inc, tk in by_eng[eng]:
                for key, v in wl:
                    engobj.wait_ge(semof(key), v)
                inst = emit(engobj)
                inst.then_inc(semof(("e", tk[1]) if tk[0] == "e" else ("p", tk[1])), inc if tk[0] == "p" else 1)

        def final_wait(engobj):
            for key, v in final_tokens:
                engobj.wait_ge(semof(key), v)

        @block.sync
        def _(e):
            emit_engine(e, "sync")
            final_wait(e)

        @block.scalar
        def _(e):
            emit_engine(e, "scalar")

        @block.vector
        def _(e):
            emit_engine(e, "vector")

        @block.tensor
        def _(e):
            emit_engine(e, "tensor")

        @block.gpsimd
        def _(e):
            emit_engine(e, "gpsimd")

        self._stack.close()
        return self.nc




def emit_plain_ln(p, x_tile, out_tile, scratch, eps=1e-5, slot=0):
    stl = scratch["st"]
    st = stl[slot % len(stl)] if isinstance(stl, list) else stl
    s0 = 0
    mu, ms, var, rstd = st[:, s0 : s0 + 1], st[:, s0 + 1 : s0 + 2], st[:, s0 + 2 : s0 + 3], st[:, s0 + 3 : s0 + 4]
    n = x_tile.shape[1]
    p.memset(st[:, s0 : s0 + 4], 0.0)
    p.act(scratch["sq"][:, 0:n], x_tile, AF.Square, accum_out=ms)
    p.red(mu, x_tile, ALU.add)
    p.ts(mu, mu, 1.0 / n, None, ALU.mult)
    p.ts(ms, ms, 1.0 / n, None, ALU.mult)
    p.act(var, mu, AF.Square)
    p.tt(var, ms, var, ALU.subtract)
    p.act(var, var, AF.Sqrt, bias=p.constv(eps))
    p.recip(rstd, var)
    p.ts(out_tile, x_tile, mu, rstd, ALU.subtract, ALU.mult)


def emit_fp4_quant(p, x_tile, out_tile, scratch, ncol):
    nb = ncol // 16
    a, b2 = scratch["a"][:, 0:ncol], scratch["b"][:, 0:ncol]
    bm, br = scratch["bm"][:, 0:nb], scratch["br"][:, 0:nb]
    x3 = x_tile.rearrange("p (a b) -> p a b", b=16)
    sgn = scratch["sg4"][:, 0:ncol]
    p.op("scalar", lambda e, o=sgn, i=x_tile: e.sign(o, i), reads=(x_tile,), writes=(sgn,))
    p.red(bm, x3, ALU.max, absval=True)
    p.ts(bm, bm, 1.0 / 12.0, 5e-13, ALU.mult, ALU.max)
    p.recip(br, bm)
    p.ts(a.bitcast(I32), x_tile.bitcast(I32), 0x7FFFFFFF, None, ALU.bitwise_and)
    a3 = a.rearrange("p (a b) -> p a b", b=16)
    p.tt(a3, a3, br.unsqueeze(2).broadcast_to([128, nb, 16]), ALU.mult)
    c1, c2 = b2, scratch["c2"][:, 0:ncol]
    p.tss(c2, a, 5.0, ALU.is_gt)
    p.stt(c2, a, 7.0, c2, ALU.is_gt, ALU.add)
    p.stt(c2, a, 10.0, c2, ALU.is_gt, ALU.add)
    p.stt(c2, a, 10.0, c2, ALU.is_gt, ALU.add)
    p.tss(c1, a, 0.5, ALU.is_gt)
    p.stt(c1, a, 1.5, c1, ALU.is_gt, ALU.add)
    p.stt(c1, a, 2.5, c1, ALU.is_gt, ALU.add)
    p.stt(c1, a, 3.5, c1, ALU.is_gt, ALU.add)
    p.stt(c1, c2, 2.0, c1, ALU.mult, ALU.add)
    p.tt(c1, c1, sgn, ALU.mult)
    c13 = c1.rearrange("p (a b) -> p a b", b=16)
    o3 = out_tile.rearrange("p (a b) -> p a b", b=16)
    p.tt(o3, c13, bm.unsqueeze(2).broadcast_to([128, nb, 16]), ALU.mult)



I_FF = 2816
NI = I_FF // 128


def emit_l1_ffn(p, X, wsl_d, wdsl_d, tag="f1"):
    qx_dr = p.dram(f"{tag}_qx", [S, D], BF16)
    qh_dr = p.dram(f"{tag}_qh", [S, I_FF], BF16)
    wq_dr = p.dram(f"{tag}_wq", [8 * 2 * I_FF, 128], BF16)
    wqin_dr = p.dram(f"{tag}_wqi", [2 * I_FF, 128], BF16)
    wdq_dr = p.dram(f"{tag}_wdq", [8 * 128, I_FF], BF16)
    wdqin_dr = p.dram(f"{tag}_wdqi", [128, I_FF], BF16)
    DBG.update(qx=qx_dr, qh=qh_dr, wq=wq_dr, wdq=wdq_dr)

    with p.scope() as outer:
        scr = {
            "sq": p.sbuf("sq", [128, D], F32, scope=outer),
            "st": [p.sbuf(f"st{i}", [128, 8], F32, scope=outer) for i in range(4)],
            "a": p.sbuf("qa", [128, I_FF], F32, scope=outer),
            "b": p.sbuf("qb", [128, I_FF], F32, scope=outer),
            "c2": p.sbuf("qc", [128, I_FF], F32, scope=outer),
            "sg4": p.sbuf("qsg", [128, I_FF], F32, scope=outer),
            "bm": p.sbuf("qbm", [128, I_FF // 16], F32, scope=outer),
            "br": p.sbuf("qbr", [128, I_FF // 16], F32, scope=outer),
        }
        emit_l1_ffn_body(p, X, wsl_d, wdsl_d, scr, qx_dr, qh_dr, wq_dr, wqin_dr, wdq_dr, wdqin_dr)


def emit_l1_ffn_body(p, X, wsl_d, wdsl_d, scr, qx_dr, qh_dr, wq_dr, wqin_dr, wdq_dr, wdqin_dr):
    with p.scope() as sc:
        wtile = [p.sbuf(f"wt{i}", [128, 1024], F32, scope=sc) for i in range(2)]
        wqt = [p.sbuf(f"wqt{i}", [128, 1024], BF16, scope=sc) for i in range(2)]
        jobs = []
        for t in range(2 * NI):
            r0 = 128 * t
            jobs.append((wsl_d[r0 : r0 + 128, :], wqin_dr[r0 : r0 + 128, :], 128))
        for j in range(4):
            c0 = 704 * j
            jobs.append((wdsl_d[:, c0 : c0 + 704], wdqin_dr[:, c0 : c0 + 704], 704))
        with p.par("wq"):
            p.dma(wtile[0][:, 0 : jobs[0][2]], jobs[0][0])
        for i, (srcap, dstap, ncol) in enumerate(jobs):
            b = i % 2
            with p.par("wq"):
                if i + 1 < len(jobs):
                    p.dma(wtile[1 - b][:, 0 : jobs[i + 1][2]], jobs[i + 1][0])
                if i > 0:
                    p.dma(dst_prev, wqt_prev, eng="gpsimd")
            emit_fp4_quant(p, wtile[b][:, 0:ncol], wqt[b][:, 0:ncol], scr, ncol)
            dst_prev, wqt_prev = dstap, wqt[b][:, 0:ncol]
        with p.par("wq"):
            p.dma(dst_prev, wqt_prev, eng="gpsimd")
        p.collective("AllGather", ALU.bypass, [list(range(8))], [wqin_dr.ap()], [wq_dr.ap()])
        p.collective("AllGather", ALU.bypass, [list(range(8))], [wdqin_dr.ap()], [wdq_dr.ap()])

        xn = [p.sbuf(f"xn{i}", [128, D], F32, scope=sc) for i in range(2)]
        qx = [p.sbuf(f"qx{i}", [128, D], BF16, scope=sc) for i in range(2)]
        for t in range(NT):
            emit_plain_ln(p, X[t][:], xn[t % 2][:], scr, slot=t)
            emit_fp4_quant(p, xn[t % 2][:], qx[t % 2][:], scr, D)
            p.dma(qx_dr[128 * t : 128 * (t + 1), :], qx[t % 2][:])

    with p.scope() as sc:
        qxg = p.sbuf("qxg", [128, DCH * 512], BF16, scope=sc)
        wgt = [p.sbuf(f"wgt{i}", [128, 512], BF16, scope=sc) for i in range(2)]
        wut = [p.sbuf(f"wut{i}", [128, 512], BF16, scope=sc) for i in range(2)]
        sg = p.sbuf("sg", [128, 512], F32, scope=sc)
        hb = [p.sbuf(f"hb{i}", [128, I_FF], F32, scope=sc) for i in range(4)]
        qh = p.sbuf("qh", [128, I_FF], BF16, scope=sc)
        psg = [p.psum(f"psg{i}", [128, 512], scope=sc) for i in range(4)]
        psu = [p.psum(f"psu{i}", [128, 512], scope=sc) for i in range(4)]
        ICH = [512, 512, 512, 512, 512, 256]

        def ldw(ic, k, b):
            base = 2 * I_FF * k
            i0 = 512 * ic
            iw = ICH[ic]
            p.dmat(wgt[b][:, 0:iw], wq_dr[base + i0 : base + i0 + iw, :])
            p.dmat(wut[b][:, 0:iw], wq_dr[base + I_FF + i0 : base + I_FF + i0 + iw, :])

        for g in range(4):
            g0 = 512 * g
            with p.par("gu"):
                for k in range(DCH):
                    p.dmat(qxg[:, 512 * k : 512 * (k + 1)], qx_dr[g0 : g0 + 512, 128 * k : 128 * (k + 1)])
                ldw(0, 0, 0)
            it = 0
            for ic in range(6):
                i0 = 512 * ic
                iw = ICH[ic]
                for k in range(DCH):
                    b = it % 2
                    with p.par("gu"):
                        if not (ic == 5 and k == DCH - 1):
                            nic, nk = (ic, k + 1) if k < DCH - 1 else (ic + 1, 0)
                            ldw(nic, nk, 1 - b)
                        for tt in range(4):
                            lhsT = qxg[:, 512 * k + 128 * tt : 512 * k + 128 * (tt + 1)]
                            p.mm(psg[tt][:, 0:iw], lhsT, wgt[b][:, 0:iw], start=(k == 0), stop=(k == DCH - 1))
                            p.mm(psu[tt][:, 0:iw], lhsT, wut[b][:, 0:iw], start=(k == 0), stop=(k == DCH - 1))
                    it += 1
                for tt in range(4):
                    p.act(sg[:, 0:iw], psg[tt][:, 0:iw], AF.Silu)
                    p.tt(hb[tt][:, i0 : i0 + iw], sg[:, 0:iw], psu[tt][:, 0:iw], ALU.mult)
            for tt in range(4):
                emit_fp4_quant(p, hb[tt][:], qh[:], scr, I_FF)
                p.dma(qh_dr[g0 + 128 * tt : g0 + 128 * (tt + 1), :], qh[:])

    with p.scope() as sc:
        qhg = p.sbuf("qhg", [128, NI * 512], BF16, scope=sc)
        wdt = [p.sbuf(f"wdt{i}", [128, D], BF16, scope=sc) for i in range(2)]
        psd = [p.psum(f"psd{i}", [128, 512], scope=sc) for i in range(8)]

        def ldd(k, b):
            p.dmat(wdt[b][:, 0:512], wdq_dr[0:512, 128 * k : 128 * (k + 1)])
            p.dmat(wdt[b][:, 512:1024], wdq_dr[512:1024, 128 * k : 128 * (k + 1)])

        for g in range(4):
            g0 = 512 * g
            with p.par("dn"):
                for k in range(NI):
                    p.dmat(qhg[:, 512 * k : 512 * (k + 1)], qh_dr[g0 : g0 + 512, 128 * k : 128 * (k + 1)])
                ldd(0, 0)
            for k in range(NI):
                b = k % 2
                with p.par("dn"):
                    if k < NI - 1:
                        ldd(k + 1, 1 - b)
                    for tt in range(4):
                        lhsT = qhg[:, 512 * k + 128 * tt : 512 * k + 128 * (tt + 1)]
                        for hc in range(2):
                            p.mm(
                                psd[2 * tt + hc][:],
                                lhsT,
                                wdt[b][:, 512 * hc : 512 * (hc + 1)],
                                start=(k == 0),
                                stop=(k == NI - 1),
                            )
            for tt in range(4):
                t = 4 * g + tt
                for hc in range(2):
                    h0 = 512 * hc
                    p.tt(X[t][:, h0 : h0 + 512], X[t][:, h0 : h0 + 512], psd[2 * tt + hc][:], ALU.add)




def emit_int8_wquant(p, w_tile, out_tile, sc_col, scratch, nrow, ncol):
    a = scratch["a"]
    _st0 = scratch["st"][0] if isinstance(scratch["st"], list) else scratch["st"]
    r = _st0[:, 4:5]
    p.red(sc_col, w_tile, ALU.max, absval=True)
    p.ts(sc_col, sc_col, 1.0 / 127.0, 1e-30, ALU.mult, ALU.max)
    p.recip(r[0:nrow, :], sc_col)
    p.ts(a[0:nrow, 0:ncol], w_tile, r[0:nrow, :], None, ALU.mult)
    p.ts(a[0:nrow, 0:ncol], a[0:nrow, 0:ncol], MAGIC, -MAGIC, ALU.add, ALU.add)
    p.ts(out_tile, a[0:nrow, 0:ncol], 127.0, -128.0, ALU.min, ALU.max)


def emit_partition_gmax(p, mxcol, mx_dr, ar_in, ar_out, row16, tag):
    p.dma(mx_dr.ap().rearrange("a b -> b a"), mxcol)
    p.memset(row16[0:1, :], 0.0)
    p.dma(row16[0:1, 8:9], mx_dr[0:1, 0:1])
    pass


def emit_l0_ffn(p, X, wisl_d, wdsl_d, clipx_d, clipd_d, tag="g0"):
    qx_dr = p.dram(f"{tag}_qx", [S, D], BF16)
    h_dr = p.dram(f"{tag}_h", [S, I_FF], BF16)
    wq_dr = p.dram(f"{tag}_wq", [2 * I_FF, D], BF16)
    win_dr = p.dram(f"{tag}_wi", [704, D], BF16)
    wdq_dr = p.dram(f"{tag}_wdq", [D, I_FF], BF16)
    wdin_dr = p.dram(f"{tag}_wdi", [128, I_FF], BF16)
    sc_dr = p.dram(f"{tag}_sc", [2 * I_FF, 1], F32)
    scin_dr = p.dram(f"{tag}_sci", [704, 1], F32)
    scd_dr = p.dram(f"{tag}_scd", [D, 1], F32)
    scdin_dr = p.dram(f"{tag}_scdi", [128, 1], F32)
    mx_dr = p.dram(f"{tag}_mx", [1, 128], F32)
    arx_in = p.dram(f"{tag}_axi", [1, 16], F32)
    arx_out = p.dram(f"{tag}_axo", [1, 16], F32)
    arh_in = p.dram(f"{tag}_ahi", [1, 16], F32)
    arh_out = p.dram(f"{tag}_aho", [1, 16], F32)
    scal_dr = p.dram(f"{tag}_scal", [1, 8], F32)
    G8 = [list(range(8))]

    with p.scope() as outer:
        scr = {
            "sq": p.sbuf("sq", [128, D], F32, scope=outer),
            "st": [p.sbuf(f"st{i}", [128, 8], F32, scope=outer) for i in range(4)],
            "a": p.sbuf("qa", [128, I_FF], F32, scope=outer),
        }
        row16 = p.sbuf("row16", [1, 16], F32, scope=outer)
        scalrow = p.sbuf("scalrow", [1, 8], F32, scope=outer)
        colbc = p.sbuf("colbc", [128, 4], F32, scope=outer)

        with p.scope() as sc:
            wt = [p.sbuf(f"wt{i}", [128, I_FF], F32, scope=sc) for i in range(2)]
            wq = [p.sbuf(f"wq{i}", [128, I_FF], BF16, scope=sc) for i in range(2)]
            scol = [p.sbuf(f"scol{i}", [128, 1], F32, scope=sc) for i in range(2)]
            rows = [128, 128, 128, 128, 128, 64]
            jobs = []
            r0 = 0
            for nr in rows:
                jobs.append((wisl_d[r0 : r0 + nr, :], win_dr[r0 : r0 + nr, :], scin_dr[r0 : r0 + nr, :], nr, D))
                r0 += nr
            jobs.append((wdsl_d[:, :], wdin_dr[:, :], scdin_dr[:, :], 128, I_FF))
            with p.par():
                p.dma(wt[0][0 : jobs[0][3], 0 : jobs[0][4]], jobs[0][0])
            prev = None
            for i, (srcap, dstap, scap, nr, ncol) in enumerate(jobs):
                b = i % 2
                with p.par():
                    if i + 1 < len(jobs):
                        p.dma(wt[1 - b][0 : jobs[i + 1][3], 0 : jobs[i + 1][4]], jobs[i + 1][0])
                    if prev is not None:
                        p.dma(prev[0], prev[1], eng="gpsimd")
                        p.dma(prev[2], prev[3], eng="gpsimd")
                emit_int8_wquant(p, wt[b][0:nr, 0:ncol], wq[b][0:nr, 0:ncol], scol[b][0:nr, :], scr, nr, ncol)
                prev = (dstap, wq[b][0:nr, 0:ncol], scap, scol[b][0:nr, :])
            with p.par():
                p.dma(prev[0], prev[1], eng="gpsimd")
                p.dma(prev[2], prev[3], eng="gpsimd")
            p.collective("AllGather", ALU.bypass, G8, [win_dr.ap()], [wq_dr.ap()])
            p.collective("AllGather", ALU.bypass, G8, [scin_dr.ap()], [sc_dr.ap()])
            p.collective("AllGather", ALU.bypass, G8, [wdin_dr.ap()], [wdq_dr.ap()])
            p.collective("AllGather", ALU.bypass, G8, [scdin_dr.ap()], [scd_dr.ap()])

        XN = [p.sbuf(f"xn{t}", [128, D], F32, scope=outer) for t in range(NT)]
        mxcol = p.sbuf("mxcol", [128, 1], F32, scope=outer)
        tmpc = p.sbuf("tmpc", [128, 1], F32, scope=outer)
        p.memset(mxcol[:], 0.0)
        tmpc2 = p.sbuf("tmpc2", [128, NT], F32, scope=outer)
        for t in range(NT):
            emit_plain_ln(p, X[t][:], XN[t][:], scr, slot=t)
            p.red(tmpc2[:, t : t + 1], XN[t][:], ALU.max, absval=True)
        p.red(mxcol[:], tmpc2[:], ALU.max)
        p.dma(mx_dr.ap().rearrange("a b -> b a"), mxcol[:])
        p.memset(row16[0:1, :], 0.0)
        rowbuf = p.sbuf("rowbuf", [1, 128], F32, scope=outer)
        p.dma(rowbuf[0:1, :], mx_dr[0:1, :])
        p.red(row16[0:1, 0:1], rowbuf[0:1, :], ALU.max)
        p.dma(arx_in[0:1, :], row16[0:1, :])
        p.collective("AllReduce", ALU.max, G8, [arx_in.ap()], [arx_out.ap()])
        p.dma(row16[0:1, 0:16], arx_out[0:1, :])
        sig = scalrow[0:1, 4:5]
        p.dma(sig, clipx_d[0:1, 0:1])
        p.act(sig, sig, AF.Sigmoid)
        sxv, rsxv = scalrow[0:1, 0:1], scalrow[0:1, 1:2]
        p.tt(sxv, sig, row16[0:1, 0:1], ALU.mult)
        p.ts(sxv, sxv, 1.0 / 127.0, None, ALU.mult)
        p.recip(rsxv, sxv)
        p.dma(scal_dr[0:1, 0:2], scalrow[0:1, 0:2])
        p.dma(colbc[:, 0:1], scal_dr[0:1, 1:2].partition_broadcast(128))
        p.dma(colbc[:, 1:2], scal_dr[0:1, 0:1].partition_broadcast(128))

        with p.scope() as sc:
            q32 = p.sbuf("q32", [128, D], F32, scope=sc)
            qx = [p.sbuf(f"qx{i}", [128, D], BF16, scope=sc) for i in range(2)]
            for t in range(NT):
                p.ts(q32[:], XN[t][:], colbc[:, 0:1], None, ALU.mult)
                p.ts(q32[:], q32[:], MAGIC, -MAGIC, ALU.add, ALU.add, chain_ok=True)
                p.ts(qx[t % 2][:], q32[:], 127.0, -128.0, ALU.min, ALU.max, chain_ok=True)
                with p.par():
                    p.dma(qx_dr[128 * t : 128 * (t + 1), :], qx[t % 2][:])

        sgu_bc = p.sbuf("sgubc", [128, 2 * I_FF], F32, scope=outer)
        p.dma(sgu_bc[:], sc_dr.ap().rearrange("a b -> b a").partition_broadcast(128))
        p.ts(sgu_bc[:], sgu_bc[:], colbc[:, 1:2], None, ALU.mult)

        hmx = p.sbuf("hmx", [128, 1], F32, scope=outer)
        p.memset(hmx[:], 0.0)
        with p.scope() as sc:
            qxg = p.sbuf("qxg", [128, DCH * 512], BF16, scope=sc)
            wgt = [p.sbuf(f"wgt{i}", [128, 512], BF16, scope=sc) for i in range(2)]
            wut = [p.sbuf(f"wut{i}", [128, 512], BF16, scope=sc) for i in range(2)]
            gs = p.sbuf("gs", [128, 512], F32, scope=sc)
            us = p.sbuf("us", [128, 512], F32, scope=sc)
            hb = p.sbuf("hb", [128, 512], BF16, scope=sc)
            psg = [p.psum(f"psg{i}", [128, 512], scope=sc) for i in range(4)]
            psu = [p.psum(f"psu{i}", [128, 512], scope=sc) for i in range(4)]
            ICH = [512, 512, 512, 512, 512, 256]

            def ldw0(ic, k, b):
                i0 = 512 * ic
                iw = ICH[ic]
                p.dmat(wgt[b][:, 0:iw], wq_dr[i0 : i0 + iw, 128 * k : 128 * (k + 1)])
                p.dmat(wut[b][:, 0:iw], wq_dr[I_FF + i0 : I_FF + i0 + iw, 128 * k : 128 * (k + 1)])

            for g in range(4):
                g0 = 512 * g
                with p.par():
                    for k in range(DCH):
                        p.dmat(qxg[:, 512 * k : 512 * (k + 1)], qx_dr[g0 : g0 + 512, 128 * k : 128 * (k + 1)])
                    ldw0(0, 0, 0)
                it = 0
                for ic in range(6):
                    i0 = 512 * ic
                    iw = ICH[ic]
                    for k in range(DCH):
                        b = it % 2
                        with p.par():
                            if not (ic == 5 and k == DCH - 1):
                                nic, nk = (ic, k + 1) if k < DCH - 1 else (ic + 1, 0)
                                ldw0(nic, nk, 1 - b)
                            for tt in range(4):
                                lhsT = qxg[:, 512 * k + 128 * tt : 512 * k + 128 * (tt + 1)]
                                p.mm(psg[tt][:, 0:iw], lhsT, wgt[b][:, 0:iw], start=(k == 0), stop=(k == DCH - 1))
                                p.mm(psu[tt][:, 0:iw], lhsT, wut[b][:, 0:iw], start=(k == 0), stop=(k == DCH - 1))
                        it += 1
                    for tt in range(4):
                        p.tt(gs[:, 0:iw], psg[tt][:, 0:iw], sgu_bc[:, i0 : i0 + iw], ALU.mult)
                        p.act(gs[:, 0:iw], gs[:, 0:iw], AF.Silu)
                        p.tt(us[:, 0:iw], psu[tt][:, 0:iw], sgu_bc[:, I_FF + i0 : I_FF + i0 + iw], ALU.mult)
                        p.tt(us[:, 0:iw], gs[:, 0:iw], us[:, 0:iw], ALU.mult)
                        p.red(tmpc[:], us[:, 0:iw], ALU.max, absval=True)
                        p.tt(hmx[:], hmx[:], tmpc[:], ALU.max)
                        p.copy(hb[:, 0:iw], us[:, 0:iw])
                        p.dma(h_dr[g0 + 128 * tt : g0 + 128 * (tt + 1), i0 : i0 + iw], hb[:, 0:iw])

        p.dma(mx_dr.ap().rearrange("a b -> b a"), hmx[:])
        p.memset(row16[0:1, :], 0.0)
        p.dma(rowbuf[0:1, :], mx_dr[0:1, :])
        p.red(row16[0:1, 0:1], rowbuf[0:1, :], ALU.max)
        p.dma(arh_in[0:1, :], row16[0:1, :])
        p.collective("AllReduce", ALU.max, G8, [arh_in.ap()], [arh_out.ap()])
        p.dma(row16[0:1, 0:16], arh_out[0:1, :])
        p.dma(sig, clipd_d[0:1, 0:1])
        p.act(sig, sig, AF.Sigmoid)
        shv, rshv = scalrow[0:1, 2:3], scalrow[0:1, 3:4]
        p.tt(shv, sig, row16[0:1, 0:1], ALU.mult)
        p.ts(shv, shv, 1.0 / 127.0, None, ALU.mult)
        p.recip(rshv, shv)
        p.dma(scal_dr[0:1, 2:4], scalrow[0:1, 2:4])
        p.dma(colbc[:, 2:3], scal_dr[0:1, 3:4].partition_broadcast(128))
        p.dma(colbc[:, 3:4], scal_dr[0:1, 2:3].partition_broadcast(128))

        swd_bc = p.sbuf("swdbc", [128, D], F32, scope=outer)
        p.dma(swd_bc[:], scd_dr.ap().rearrange("a b -> b a").partition_broadcast(128))
        p.ts(swd_bc[:], swd_bc[:], colbc[:, 3:4], None, ALU.mult)

        with p.scope() as sc:
            qhg = p.sbuf("qhg", [128, NI * 512], BF16, scope=sc)
            hstg = [p.sbuf(f"hstg{i}", [128, 512], BF16, scope=sc) for i in range(2)]
            hq32 = p.sbuf("hq32", [128, 512], F32, scope=sc)
            wdt = [p.sbuf(f"wdt{i}", [128, D], BF16, scope=sc) for i in range(2)]
            psd = [p.psum(f"psd{i}", [128, 512], scope=sc) for i in range(8)]

            def ldd0(k, b):
                p.dmat(wdt[b][:, 0:512], wdq_dr[0:512, 128 * k : 128 * (k + 1)])
                p.dmat(wdt[b][:, 512:1024], wdq_dr[512:1024, 128 * k : 128 * (k + 1)])

            for g in range(4):
                g0 = 512 * g
                with p.par():
                    p.dmat(hstg[0][:], h_dr[g0 : g0 + 512, 0:128])
                for k in range(NI):
                    b = k % 2
                    with p.par():
                        if k < NI - 1:
                            p.dmat(hstg[1 - b][:], h_dr[g0 : g0 + 512, 128 * (k + 1) : 128 * (k + 2)])
                        p.ts(hq32[:], hstg[b][:], colbc[:, 2:3], None, ALU.mult)
                        p.ts(hq32[:], hq32[:], MAGIC, -MAGIC, ALU.add, ALU.add)
                        p.ts(qhg[:, 512 * k : 512 * (k + 1)], hq32[:], 127.0, -128.0, ALU.min, ALU.max)
                    if k == NI - 1:
                        with p.par():
                            ldd0(0, 0)
                for k in range(NI):
                    b = k % 2
                    with p.par():
                        if k < NI - 1:
                            ldd0(k + 1, 1 - b)
                        for tt in range(4):
                            lhsT = qhg[:, 512 * k + 128 * tt : 512 * k + 128 * (tt + 1)]
                            for hc in range(2):
                                p.mm(
                                    psd[2 * tt + hc][:],
                                    lhsT,
                                    wdt[b][:, 512 * hc : 512 * (hc + 1)],
                                    start=(k == 0),
                                    stop=(k == NI - 1),
                                )
                for tt in range(4):
                    t = 4 * g + tt
                    for hc in range(2):
                        h0 = 512 * hc
                        p.tt(hq32[:], psd[2 * tt + hc][:], swd_bc[:, h0 : h0 + 512], ALU.mult)
                        p.tt(X[t][:, h0 : h0 + 512], X[t][:, h0 : h0 + 512], hq32[:], ALU.add)



NH = 32
HP = 64
DIN = 2048
DSTATE = 128
CONVD = 2304
NCH = 16
DFF = 4096


def emit_affine_ln(p, x_tile, out_tile, scratch, lnw_bc, lnb_bc, u_scratch, slot=0):
    afl = scratch["aff"]
    aff_t = afl[slot % len(afl)] if isinstance(afl, list) else afl
    aff = aff_t[:, 0 : x_tile.shape[1]]
    emit_plain_ln(p, x_tile, u_scratch, scratch, slot=slot)
    emit_plain_ln(p, u_scratch, aff, scratch, slot=slot)
    p.tt(aff, aff, lnw_bc, ALU.mult)
    p.tt(out_tile, aff, lnb_bc, ALU.add)


def emit_l0_attn(
    p, X, inw_d, convw_d, convb_d, dtb_d, alog_d, dskip_d, rmsw_d, outw_d,
    w1_d, b1_d, w2_d, b2_d, lnw_d, lnb_d, xhalo_d, hmask_d, omask_d, tag="m0"
):
    xn_dr = p.dram(f"{tag}_xn", [S, D], BF16)
    z_dr = p.dram(f"{tag}_z", [S, DIN], BF16)
    xss_dr = p.dram(f"{tag}_xss", [DIN, S], BF16)
    caf_dr = p.dram(f"{tag}_caf", [NCH, NH * 128], F32)
    ncl_dr = p.dram(f"{tag}_ncl", [NCH, NH * 128], F32)
    lam_dr = p.dram(f"{tag}_lam", [NH, S], BF16)
    lamtot_dr = p.dram(f"{tag}_lamt", [NH, S], BF16)
    etm_dr = p.dram(f"{tag}_etm", [NH, S], BF16)
    easum_dr = p.dram(f"{tag}_eas", [NCH, NH], F32)
    yraw_dr = p.dram(f"{tag}_yraw", [S, DIN], BF16)
    y4_dr = p.dram(f"{tag}_y4", [S, DIN], BF16)
    sfin_dr = p.dram(f"{tag}_sfin", [128, DIN], F32)
    sag_dr = p.dram(f"{tag}_sag", [256, DIN], F32)
    xn2_dr = p.dram(f"{tag}_xn2", [S, D], BF16)
    h1_dr = p.dram(f"{tag}_h1", [S, DFF], BF16)
    PAIRS = [[0, 1], [2, 3], [4, 5], [6, 7]]

    with p.scope() as outer:
        scr = {
            "sq": p.sbuf("sq", [128, D], F32, scope=outer),
            "st": [p.sbuf(f"st{i}", [128, 8], F32, scope=outer) for i in range(4)],
            "aff": [p.sbuf(f"aff{i}", [128, D], F32, scope=outer) for i in range(2)],
        }
        uscr = [p.sbuf(f"uscr{i}", [128, D], F32, scope=outer) for i in range(2)]
        lnw_bc = p.sbuf("lnwbc", [128, D], F32, scope=outer)
        lnb_bc = p.sbuf("lnbbc", [128, D], F32, scope=outer)
        ident = p.sbuf("ident", [128, 128], F32, scope=outer)
        p.dma(lnw_bc[:], lnw_d[0:1, :].partition_broadcast(128))
        p.dma(lnb_bc[:], lnb_d[0:1, :].partition_broadcast(128))
        p.memset(ident[:], 0.0)
        p.affsel(ident[:], ident[:], [[-1, 128]], ALU.not_equal, 1.0, 0, 1)

        with p.scope() as sc:
            xnb = [p.sbuf(f"xnb{i}", [128, D], BF16, scope=sc) for i in range(2)]
            for t in range(NT):
                b = t % 2
                emit_affine_ln(p, X[t][:], xnb[b][:], scr, lnw_bc[:], lnb_bc[:], uscr[b][:], slot=t)
                p.dma(xn_dr[128 * t : 128 * (t + 1), :], xnb[b][:])
        xh3f = p.sbuf("xh3f", [128, DCH * 4], BF16, scope=outer)
        with p.scope() as sc:
            xh = p.sbuf("xh", [128, D], F32, scope=sc)
            xnh = p.sbuf("xnh", [128, D], F32, scope=sc)
            pt = p.psum("pt", [128, 128], scope=sc)
            p.memset(xh[:], 0.0)
            p.dma(xh[0:3, :], xhalo_d[0:3, :])
            emit_affine_ln(p, xh[:], xnh[:], scr, lnw_bc[:], lnb_bc[:], uscr[0][:])
            for k in range(DCH):
                p.transpose(pt[:, 0:128], xnh[:, 128 * k : 128 * (k + 1)], ident[:])
                p.copy(xh3f[:, 4 * k : 4 * k + 3], pt[:, 0:3])

        BM = p.sbuf("Bfm", [128, S], BF16, scope=outer)
        CM = p.sbuf("Cfm", [128, S], BF16, scope=outer)
        dttm = p.sbuf("dttm", [128, NCH * NH], F32, scope=outer)
        haloxbc = p.sbuf("haloxbc", [128, 18 * 3], BF16, scope=outer)
        with p.scope() as sc:
            xnf = p.sbuf("xnf", [128, DCH * S], BF16, scope=sc)
            for k in range(DCH):
                p.dmat(xnf[:, S * k : S * (k + 1)], xn_dr[:, 128 * k : 128 * (k + 1)])
            sc2 = p.scope()
            sc2.__enter__()
            wt = [p.sbuf(f"wt{i}", [128, 512], BF16, scope=sc2) for i in range(2)]
            zb = [p.sbuf(f"zb{i}", [128, 512], BF16, scope=sc2) for i in range(4)]
            psz = [p.psum(f"psz{i}", [128, 512], scope=sc2) for i in range(4)]
            def ldz(zc, k, b):
                p.dma(wt[b][:], inw_d[128 * k : 128 * (k + 1), 512 * zc : 512 * (zc + 1)], eng="gpsimd")

            for g in range(4):
                with p.par():
                    ldz(0, 0, 0)
                it = 0
                for zc in range(4):
                    for k in range(DCH):
                        b = it % 2
                        with p.par():
                            if not (zc == 3 and k == DCH - 1):
                                nzc, nk = (zc, k + 1) if k < DCH - 1 else (zc + 1, 0)
                                ldz(nzc, nk, 1 - b)
                            for tt in range(4):
                                lhsT = xnf[:, S * k + 512 * g + 128 * tt : S * k + 512 * g + 128 * (tt + 1)]
                                p.mm(psz[tt][:], lhsT, wt[b][:], start=(k == 0), stop=(k == DCH - 1))
                        it += 1
                    with p.par():
                        for tt in range(4):
                            p.copy(zb[tt][:], psz[tt][:])
                    with p.par():
                        for tt in range(4):
                            p.dma(z_dr[512 * g + 128 * tt : 512 * g + 128 * (tt + 1), 512 * zc : 512 * (zc + 1)], zb[tt][:])
            sc2.__exit__(None, None, None)
            sc2 = p.scope()
            sc2.__enter__()
            dtbias_bc = p.sbuf("dtbbc", [128, NH], F32, scope=sc2)
            p.dma(dtbias_bc[:], dtb_d[0:1, :].partition_broadcast(128))
            wdt = p.sbuf("wdt", [128, DCH * NH], BF16, scope=sc2)
            psd = p.psum("psdt", [128, NH], scope=sc2)
            d1 = p.sbuf("d1", [128, NH], F32, scope=sc2)
            d2 = p.sbuf("d2", [128, NH], F32, scope=sc2)
            for k in range(DCH):
                p.dma(wdt[:, NH * k : NH * (k + 1)], inw_d[128 * k : 128 * (k + 1), 4352:4384], eng="gpsimd")
            for tc in range(NT):
                for k in range(DCH):
                    lhsT = xnf[:, S * k + 128 * tc : S * k + 128 * (tc + 1)]
                    p.mm(psd[:, 0:NH], lhsT, wdt[:, NH * k : NH * (k + 1)], start=(k == 0), stop=(k == DCH - 1))
                p.tt(d1[:], psd[:, 0:NH], dtbias_bc[:], ALU.add)
                p.act(d2[:], d1[:], AF.Abs)
                p.act(d2[:], d2[:], AF.Exp, scale=-1.0)
                p.act(d2[:], d2[:], AF.Ln, bias=1.0)
                p.ts(d1[:], d1[:], 0.0, None, ALU.max)
                p.tt(dttm[:, NH * tc : NH * (tc + 1)], d1[:], d2[:], ALU.add)
            sc2.__exit__(None, None, None)
            convw = p.sbuf("convw", [128, 18 * 4], F32, scope=sc)
            convb = p.sbuf("convb", [128, 18], F32, scope=sc)
            p.dma(convw[:].rearrange("p (f k) -> p f k", k=4), convw_d.ap().rearrange("(f p) k -> p f k", p=128))
            p.dma(convb[:].rearrange("p (a f) -> p a f", a=1), convb_d.ap().rearrange("a (f p) -> p a f", p=128), allow_slow_non_contiguous=True)
            hmcol = p.sbuf("hmcol", [128, 1], F32, scope=sc)
            p.dma(hmcol[:], hmask_d[0:1, 0:1].partition_broadcast(128))
            wtile = [p.sbuf(f"wtile{i}", [128, 128], BF16, scope=sc) for i in range(2)]
            xbch = p.sbuf("xbch", [128, S + 4], F32, scope=sc)
            cacc = p.sbuf("cacc", [128, S], F32, scope=sc)
            sil = p.sbuf("sil", [128, S], BF16, scope=sc)
            psx = [p.psum(f"psx{i}", [128, 512], scope=sc) for i in range(4)]
            psh = p.psum("psh", [128, 16], scope=sc)
            def ldxbc(fc, k, b):
                col0 = 2048 + 128 * fc
                p.dma(wtile[b][:], inw_d[128 * k : 128 * (k + 1), col0 : col0 + 128], eng="gpsimd")

            with p.par():
                ldxbc(0, 0, 0)
            itx = 0
            for fc in range(18):
                for k in range(DCH):
                    b = itx % 2
                    with p.par():
                        if not (fc == 17 and k == DCH - 1):
                            nfc, nk = (fc, k + 1) if k < DCH - 1 else (fc + 1, 0)
                            ldxbc(nfc, nk, 1 - b)
                        for g in range(4):
                            p.mm(psx[g][:], wtile[b][:], xnf[:, S * k + 512 * g : S * k + 512 * (g + 1)], start=(k == 0), stop=(k == DCH - 1))
                        p.mm(psh[:, 0:3], wtile[b][:], xh3f[:, 4 * k : 4 * k + 3], start=(k == 0), stop=(k == DCH - 1))
                    itx += 1
                p.ts(xbch[:, 1:4], psh[:, 0:3], hmcol[:], None, ALU.mult)
                for g in range(4):
                    p.copy(xbch[:, 4 + 512 * g : 4 + 512 * (g + 1)], psx[g][:])
                p.copy(haloxbc[:, 3 * fc : 3 * fc + 3], xbch[:, 1:4])
                p.ts(cacc[:], xbch[:, 1 : 1 + S], convw[:, 4 * fc : 4 * fc + 1], None, ALU.mult)
                for k in range(1, 4):
                    p.stt(cacc[:], xbch[:, 1 + k : 1 + k + S], convw[:, 4 * fc + k : 4 * fc + k + 1], cacc[:], ALU.mult, ALU.add)
                p.act(sil[:], cacc[:], AF.Silu, bias=convb[:, fc : fc + 1])
                if fc < 16:
                    p.dma(xss_dr[128 * fc : 128 * (fc + 1), :], sil[:])
                elif fc == 16:
                    p.copy(BM[:], sil[:])
                else:
                    p.copy(CM[:], sil[:])

        with p.scope() as sc:
            dtfm = p.sbuf("dtfm", [NH, S], F32, scope=sc)
            pt = p.psum("ptd", [128, 128], scope=sc)
            for tc in range(NT):
                p.transpose(pt[0:NH, 0:128], dttm[:, NH * tc : NH * (tc + 1)], ident[:])
                p.copy(dtfm[0:NH, 128 * tc : 128 * (tc + 1)], pt[0:NH, 0:128])
            acol = p.sbuf("acol", [NH, 1], F32, scope=sc)
            p.dma(acol[0:NH, :], alog_d.ap().rearrange("a b -> b a"), allow_slow_non_contiguous=True)
            p.act(acol[0:NH, :], acol[0:NH, :], AF.Exp)
            p.ts(acol[0:NH, :], acol[0:NH, :], -1.0, None, ALU.mult)
            afm = p.sbuf("afm", [NH, S], F32, scope=sc)
            p.ts(afm[0:NH, :], dtfm[0:NH, :], acol[0:NH, :], None, ALU.mult)
            zrow = p.sbuf("zrow", [NH, 128], F32, scope=sc)
            p.memset(zrow[0:NH, :], 0.0)
            cafm = p.sbuf("cafm", [NH, S], F32, scope=sc)
            for c in range(NCH):
                p.scan(cafm[0:NH, 128 * c : 128 * (c + 1)], afm[0:NH, 128 * c : 128 * (c + 1)], zrow[0:NH, :], 0.0, ALU.add, ALU.add)
            zrow2 = p.sbuf("zrow2", [NH, S], F32, scope=sc)
            p.memset(zrow2[0:NH, :], 0.0)
            catot = p.sbuf("catot", [NH, S], F32, scope=sc)
            p.scan(catot[0:NH, :], afm[0:NH, :], zrow2[0:NH, :], 0.0, ALU.add, ALU.add)
            brow = p.sbuf("brow", [NH, S], BF16, scope=sc)
            p.act(brow[0:NH, :], cafm[0:NH, :], AF.Exp)
            p.dma(lam_dr[0:NH, :], brow[0:NH, :])
            p.act(brow[0:NH, :], catot[0:NH, :], AF.Exp)
            p.dma(lamtot_dr[0:NH, :], brow[0:NH, :])
            asumb = p.sbuf("asumb", [NH, NCH], F32, scope=sc)
            for c in range(NCH):
                p.copy(asumb[0:NH, c : c + 1], cafm[0:NH, 128 * c + 127 : 128 * (c + 1)])
            efm = p.sbuf("efm", [NH, S], F32, scope=sc)
            p.tt(
                efm[0:NH, :].rearrange("h (c t) -> h c t", t=128),
                asumb[0:NH, :].unsqueeze(2).broadcast_to([NH, NCH, 128]),
                cafm[0:NH, :].rearrange("h (c t) -> h c t", t=128),
                ALU.subtract,
            )
            p.act(brow[0:NH, :], efm[0:NH, :], AF.Exp)
            p.dma(etm_dr[0:NH, :], brow[0:NH, :])
            p.act(asumb[0:NH, :], asumb[0:NH, :], AF.Exp)
            p.dma(easum_dr.ap().rearrange("c h -> h c"), asumb[0:NH, :], allow_slow_non_contiguous=True)
            nclfm = p.sbuf("nclfm", [NH, S], F32, scope=sc)
            p.ts(nclfm[0:NH, :], cafm[0:NH, :], -1.0, None, ALU.mult)
            for c in range(NCH):
                p.dma(caf_dr[c : c + 1, :].rearrange("a (h t) -> (a h) t", t=128), cafm[0:NH, 128 * c : 128 * (c + 1)])
                p.dma(ncl_dr[c : c + 1, :].rearrange("a (h t) -> (a h) t", t=128), nclfm[0:NH, 128 * c : 128 * (c + 1)])

        maskDT = p.sbuf("maskDT", [128, 128], F32, scope=outer)
        p.memset(maskDT[:], 0.0)
        p.affsel(maskDT[:], maskDT[:], [[1, 128]], ALU.is_ge, -1e30, 0, -1)
        dskip_bc = p.sbuf("dskipbc", [128, NH], F32, scope=outer)
        p.dma(dskip_bc[:], dskip_d[0:1, :].partition_broadcast(128))
        onesr = p.sbuf("onesr", [1, 128], F32, scope=outer)
        p.memset(onesr[0:1, :], 1.0)
        SST = p.sbuf("SST", [128, DIN], F32, scope=outer)
        p.memset(SST[:], 0.0)
        with p.scope() as sc:
            Sb = p.sbuf("Sb", [128, DIN], BF16, scope=sc)
            Xtm = p.sbuf("Xtm", [128, DIN], BF16, scope=sc)
            Xd = p.sbuf("Xd", [128, DIN], BF16, scope=sc)
            XdE = p.sbuf("XdE", [128, DIN], BF16, scope=sc)
            lamt = p.sbuf("lamt", [128, NH], BF16, scope=sc)
            etmt = p.sbuf("etmt", [128, NH], BF16, scope=sc)
            lambc = p.sbuf("lambc", [128, NH], F32, scope=sc)
            Gt = p.sbuf("Gt", [128, 128], BF16, scope=sc)
            Btm = p.sbuf("Btm", [128, 128], BF16, scope=sc)
            Bf32 = p.sbuf("Bf32", [128, 128], F32, scope=sc)
            carow = p.sbuf("carow", [1, NH * 128], F32, scope=sc)
            nclrow = p.sbuf("nclrow", [1, NH * 128], F32, scope=sc)
            Ws = p.sbuf("Ws", [128, DIN], F32, scope=sc)
            Wp = p.sbuf("Wp", [128, DIN], BF16, scope=sc)
            ybuf = p.sbuf("ybuf", [128, DIN], F32, scope=sc)
            ybuf2 = p.sbuf("ybuf2", [128, DIN], F32, scope=sc)
            yb16 = p.sbuf("yb16", [128, DIN], BF16, scope=sc)
            pd = p.psum("pd", [128, DIN], scope=sc)
            py = p.psum("py", [128, DIN], scope=sc)
            for c in range(NCH):
                t0 = 128 * c
                p.dmat(Xtm[:], xss_dr[:, t0 : t0 + 128])
                p.dmat(lamt[:, 0:NH], lam_dr[0:NH, t0 : t0 + 128])
                p.dmat(etmt[:, 0:NH], etm_dr[0:NH, t0 : t0 + 128])
                p.dma(lambc[:, 0:NH], easum_dr[c : c + 1, :].partition_broadcast(128))
                p.dma(carow[0:1, :], caf_dr[c : c + 1, :])
                p.dma(nclrow[0:1, :], ncl_dr[c : c + 1, :])
                x3 = Xtm[:].rearrange("p (h e) -> p h e", e=HP)
                p.tt(Xd[:].rearrange("p (h e) -> p h e", e=HP), x3, dttm[:, NH * c : NH * (c + 1)].unsqueeze(2).broadcast_to([128, NH, HP]), ALU.mult)
                p.tt(XdE[:].rearrange("p (h e) -> p h e", e=HP), Xd[:].rearrange("p (h e) -> p h e", e=HP), etmt[:, 0:NH].unsqueeze(2).broadcast_to([128, NH, HP]), ALU.mult)
                p.mm(pd[:, 0:128], BM[:, t0 : t0 + 128], CM[:, t0 : t0 + 128])
                p.copy(Gt[:], pd[:, 0:128])
                p.copy(Bf32[:], BM[:, t0 : t0 + 128])
                p.transpose(pd[:, 128:256], Bf32[:], ident[:])
                p.copy(Btm[:], pd[:, 128:256])
                for hh in range(2):
                    o0 = DIN * hh
                    for s4 in range(4):
                        p.mm(pd[:, 512 * s4 : 512 * (s4 + 1)], onesr[0:1, :], carow[0:1, 2048 * hh + 512 * s4 : 2048 * hh + 512 * (s4 + 1)], start=True, stop=False)
                    for j in range(16):
                        h = 16 * hh + j
                        p.mm(pd[:, 128 * j : 128 * (j + 1)], nclrow[0:1, 128 * h : 128 * (h + 1)], onesr[0:1, :], start=False, stop=True)
                    p.tt(
                        Ws[:, o0 : o0 + 2048].rearrange("p (j t) -> p j t", t=128) if False else Ws[:, 0:2048].rearrange("p (j t) -> p j t", t=128),
                        pd[:].rearrange("p (j t) -> p j t", t=128),
                        maskDT[:].unsqueeze(1).broadcast_to([128, 16, 128]),
                        ALU.add,
                    )
                    p.act(Wp[:, 0:2048], Ws[:, 0:2048], AF.Exp)
                    p.tt(
                        Wp[:, 0:2048].rearrange("p (j t) -> p j t", t=128),
                        Wp[:, 0:2048].rearrange("p (j t) -> p j t", t=128),
                        Gt[:].unsqueeze(1).broadcast_to([128, 16, 128]),
                        ALU.mult,
                    )
                    for j in range(16):
                        h = 16 * hh + j
                        p.mm(py[:, HP * h : HP * (h + 1)], Wp[:, 128 * j : 128 * (j + 1)], Xd[:, HP * h : HP * (h + 1)], start=True, stop=True)
                p.copy(Sb[:], SST[:])
                for h in range(NH):
                    p.mm(pd[:, HP * h : HP * (h + 1)], CM[:, t0 : t0 + 128], Sb[:, HP * h : HP * (h + 1)], start=True, stop=True)
                p.tt(ybuf[:].rearrange("p (h e) -> p h e", e=HP), pd[:].rearrange("p (h e) -> p h e", e=HP), lamt[:, 0:NH].unsqueeze(2).broadcast_to([128, NH, HP]), ALU.mult)
                p.tt(ybuf[:], ybuf[:], py[:], ALU.add)
                p.tt(ybuf2[:].rearrange("p (h e) -> p h e", e=HP), x3, dskip_bc[:].unsqueeze(2).broadcast_to([128, NH, HP]), ALU.mult)
                p.tt(yb16[:], ybuf[:], ybuf2[:], ALU.add)
                p.dma(yraw_dr[t0 : t0 + 128, :], yb16[:])
                p.ts(SST[:].rearrange("p (h e) -> p h e", e=HP), SST[:].rearrange("p (h e) -> p h e", e=HP), None, None, ALU.mult) if False else None
                p.tt(SST[:].rearrange("p (h e) -> p h e", e=HP), SST[:].rearrange("p (h e) -> p h e", e=HP), lambc[:, 0:NH].unsqueeze(2).broadcast_to([128, NH, HP]), ALU.mult)
                for h in range(NH):
                    p.mm(py[:, HP * h : HP * (h + 1)], Btm[:], XdE[:, HP * h : HP * (h + 1)], start=True, stop=True)
                p.tt(SST[:], SST[:], py[:], ALU.add)
            p.dma(sfin_dr[:, :], SST[:])
        p.collective("AllGather", ALU.bypass, PAIRS, [sfin_dr.ap()], [sag_dr.ap()])

        rms_bc = p.sbuf("rmsbc", [128, DIN], F32, scope=outer)
        p.dma(rms_bc[:], rmsw_d[0:1, :].partition_broadcast(128))
        with p.scope() as sc:
            omcol = p.sbuf("omcol", [128, 1], F32, scope=sc)
            p.dma(omcol[:], omask_d[0:1, 0:1].partition_broadcast(128))
            Sin = p.sbuf("Sin", [128, DIN], F32, scope=sc)
            Sinb = p.sbuf("Sinb", [128, DIN], BF16, scope=sc)
            p.dma(Sin[:], sag_dr[0:128, :])
            p.ts(Sin[:], Sin[:], omcol[:], None, ALU.mult)
            p.copy(Sinb[:], Sin[:])
            lamtt = p.sbuf("lamtt", [128, NH], BF16, scope=sc)
            yr = p.sbuf("yr", [128, DIN], BF16, scope=sc)
            zt = p.sbuf("zt", [128, DIN], BF16, scope=sc)
            zs = p.sbuf("zs", [128, DIN], F32, scope=sc)
            yv = p.sbuf("yv", [128, DIN], F32, scope=sc)
            yo = p.sbuf("yo", [128, DIN], BF16, scope=sc)
            rcol = p.sbuf("rcol", [128, 1], F32, scope=sc)
            sqs = p.sbuf("sqs", [128, DIN], F32, scope=sc)
            pz = p.psum("pz", [128, DIN], scope=sc)
            for c in range(NCH):
                t0 = 128 * c
                p.dmat(lamtt[:, 0:NH], lamtot_dr[0:NH, t0 : t0 + 128])
                p.dma(yr[:], yraw_dr[t0 : t0 + 128, :])
                p.dma(zt[:], z_dr[t0 : t0 + 128, :], eng="gpsimd")
                for h in range(NH):
                    p.mm(pz[:, HP * h : HP * (h + 1)], CM[:, t0 : t0 + 128], Sinb[:, HP * h : HP * (h + 1)], start=True, stop=True)
                p.tt(yv[:].rearrange("p (h e) -> p h e", e=HP), pz[:].rearrange("p (h e) -> p h e", e=HP), lamtt[:, 0:NH].unsqueeze(2).broadcast_to([128, NH, HP]), ALU.mult)
                p.tt(yv[:], yv[:], yr[:], ALU.add)
                p.act(zs[:], zt[:], AF.Silu)
                p.tt(yv[:], yv[:], zs[:], ALU.mult)
                p.memset(rcol[:], 0.0)
                p.act(sqs[:], yv[:], AF.Square, accum_out=rcol[:])
                p.ts(rcol[:], rcol[:], 1.0 / DIN, None, ALU.mult)
                p.act(rcol[:], rcol[:], AF.Sqrt, bias=p.constv(1e-5))
                p.recip(rcol[:], rcol[:])
                p.ts(yv[:], yv[:], rcol[:], None, ALU.mult)
                p.tt(yo[:], yv[:], rms_bc[:], ALU.mult)
                p.dma(y4_dr[t0 : t0 + 128, :], yo[:])

        out1_dr = p.dram(f"{tag}_out1", [S, D], F32)
        with p.scope() as sc:
            yf = p.sbuf("yf", [128, 16 * 512], BF16, scope=sc)
            wot = [p.sbuf(f"wot{i}", [128, D], BF16, scope=sc) for i in range(2)]
            o1t = [p.sbuf(f"o1t{i}", [128, D], F32, scope=sc) for i in range(2)]
            pso = [p.psum(f"pso{i}", [128, 512], scope=sc) for i in range(8)]

            def ldo(k, b):
                p.dma(wot[b][:], outw_d[128 * k : 128 * (k + 1), 0:1024], eng="gpsimd")

            for g in range(4):
                g0 = 512 * g
                with p.par():
                    for k in range(16):
                        p.dmat(yf[:, 512 * k : 512 * (k + 1)], y4_dr[g0 : g0 + 512, 128 * k : 128 * (k + 1)])
                    ldo(0, 0)
                for k in range(16):
                    b = k % 2
                    with p.par():
                        if k < 15:
                            ldo(k + 1, 1 - b)
                        for tt in range(4):
                            lhsT = yf[:, 512 * k + 128 * tt : 512 * k + 128 * (tt + 1)]
                            p.mm(pso[2 * tt][:], lhsT, wot[b][:, 0:512], start=(k == 0), stop=(k == 15))
                            p.mm(pso[2 * tt + 1][:], lhsT, wot[b][:, 512:1024], start=(k == 0), stop=(k == 15))
                for tt in range(4):
                    t = 4 * g + tt
                    u = uscr[tt % 2]
                    emit_plain_ln(p, X[t][:], u[:], scr, slot=t)
                    for hc in range(2):
                        p.tt(o1t[tt % 2][:, 512 * hc : 512 * (hc + 1)], pso[2 * tt + hc][:], u[:, 512 * hc : 512 * (hc + 1)], ALU.add)
                    p.dma(out1_dr[128 * t : 128 * (t + 1), :], o1t[tt % 2][:])
                    p.tt(X[t][:], X[t][:], o1t[tt % 2][:], ALU.add)

        with p.scope() as sc:
            xn2b = [p.sbuf(f"xn2b{i}", [128, D], BF16, scope=sc) for i in range(2)]
            o1l = [p.sbuf(f"o1l{i}", [128, D], F32, scope=sc) for i in range(2)]
            for t in range(NT):
                b = t % 2
                p.dma(o1l[b][:], out1_dr[128 * t : 128 * (t + 1), :])
                emit_affine_ln(p, o1l[b][:], xn2b[b][:], scr, lnw_bc[:], lnb_bc[:], uscr[b][:], slot=t)
                p.dma(xn2_dr[128 * t : 128 * (t + 1), :], xn2b[b][:])
        with p.scope() as sc:
            xnf = p.sbuf("xnf2", [128, DCH * 512], BF16, scope=sc)
            wt = [p.sbuf(f"wt2{i}", [128, 512], BF16, scope=sc) for i in range(2)]
            b1bc = p.sbuf("b1bc", [128, DFF], F32, scope=sc)
            p.dma(b1bc[:], b1_d[0:1, :].partition_broadcast(128))
            hb = [p.sbuf(f"hb1{i}", [128, 512], F32, scope=sc) for i in range(4)]
            hbb = [p.sbuf(f"hbb1{i}", [128, 512], BF16, scope=sc) for i in range(4)]
            ps1 = [p.psum(f"ps1{i}", [128, 512], scope=sc) for i in range(4)]
            def ldw1(jc, k, b):
                p.dma(wt[b][:], w1_d[128 * k : 128 * (k + 1), 512 * jc : 512 * (jc + 1)], eng="gpsimd")

            for g in range(4):
                g0 = 512 * g
                with p.par():
                    for k in range(DCH):
                        p.dmat(xnf[:, 512 * k : 512 * (k + 1)], xn2_dr[g0 : g0 + 512, 128 * k : 128 * (k + 1)])
                    ldw1(0, 0, 0)
                it = 0
                for jc in range(8):
                    j0 = 512 * jc
                    for k in range(DCH):
                        b = it % 2
                        with p.par():
                            if not (jc == 7 and k == DCH - 1):
                                njc, nk = (jc, k + 1) if k < DCH - 1 else (jc + 1, 0)
                                ldw1(njc, nk, 1 - b)
                            for tt in range(4):
                                lhsT = xnf[:, 512 * k + 128 * tt : 512 * k + 128 * (tt + 1)]
                                p.mm(ps1[tt][:], lhsT, wt[b][:], start=(k == 0), stop=(k == DCH - 1))
                        it += 1
                    for tt in range(4):
                        p.tt(hb[tt][:], ps1[tt][:], b1bc[:, j0 : j0 + 512], ALU.add)
                        p.act(hbb[tt][:], hb[tt][:], AF.Relu, eng="scalar")
                    with p.par():
                        for tt in range(4):
                            p.dma(h1_dr[g0 + 128 * tt : g0 + 128 * (tt + 1), j0 : j0 + 512], hbb[tt][:])
        with p.scope() as sc:
            h1f = p.sbuf("h1f", [128, 32 * 512], BF16, scope=sc)
            wt = [p.sbuf(f"wt3{i}", [128, D], BF16, scope=sc) for i in range(2)]
            b2bc = p.sbuf("b2bc", [128, D], F32, scope=sc)
            p.dma(b2bc[:], b2_d[0:1, :].partition_broadcast(128))
            mo = p.sbuf("mo", [128, 512], F32, scope=sc)
            ps2 = [p.psum(f"ps2{i}", [128, 512], scope=sc) for i in range(8)]
            def ldw2(k, b):
                p.dma(wt[b][:], w2_d[128 * k : 128 * (k + 1), 0:1024], eng="gpsimd")

            for g in range(4):
                g0 = 512 * g
                with p.par():
                    for k in range(32):
                        p.dmat(h1f[:, 512 * k : 512 * (k + 1)], h1_dr[g0 : g0 + 512, 128 * k : 128 * (k + 1)])
                    ldw2(0, 0)
                for k in range(32):
                    b = k % 2
                    with p.par():
                        if k < 31:
                            ldw2(k + 1, 1 - b)
                        for tt in range(4):
                            lhsT = h1f[:, 512 * k + 128 * tt : 512 * k + 128 * (tt + 1)]
                            for hc in range(2):
                                p.mm(ps2[2 * tt + hc][:], lhsT, wt[b][:, 512 * hc : 512 * (hc + 1)], start=(k == 0), stop=(k == 31))
                for tt in range(4):
                    t = 4 * g + tt
                    for hc in range(2):
                        h0 = 512 * hc
                        p.tt(mo[:], ps2[2 * tt + hc][:], b2bc[:, h0 : h0 + 512], ALU.add)
                        p.tt(X[t][:, h0 : h0 + 512], X[t][:, h0 : h0 + 512], mo[:], ALU.add, chain_ok=True)


def _l0attn_host_inputs(x_full):
    outs = []
    for c in range(8):
        b, half = c // 2, c % 2
        if half == 1:
            halo = np.ascontiguousarray(x_full[b, 2045:2048])
            hm = 1.0
        else:
            halo = np.ascontiguousarray(x_full[b, 0:3])
            hm = 0.0
        outs.append((halo, np.array([[hm]], np.float32), np.array([[1.0 if half == 1 else 0.0]], np.float32)))
    return outs


def build_l0_attn_launch():
    p = Prog("l0attn")
    nc = p.nc
    x_d = nc.dram_tensor("x", [S, D], F32, kind="ExternalInput")
    inw_d = nc.dram_tensor("inw", [D, 4384], F32, kind="ExternalInput")
    convw_d = nc.dram_tensor("convw", [CONVD, 4], F32, kind="ExternalInput")
    convb_d = nc.dram_tensor("convb", [1, CONVD], F32, kind="ExternalInput")
    dtb_d = nc.dram_tensor("dtb", [1, NH], F32, kind="ExternalInput")
    alog_d = nc.dram_tensor("alog", [1, NH], F32, kind="ExternalInput")
    dskip_d = nc.dram_tensor("dskip", [1, NH], F32, kind="ExternalInput")
    rmsw_d = nc.dram_tensor("rmsw", [1, DIN], F32, kind="ExternalInput")
    outw_d = nc.dram_tensor("outw", [DIN, D], F32, kind="ExternalInput")
    w1_d = nc.dram_tensor("w1", [D, DFF], F32, kind="ExternalInput")
    b1_d = nc.dram_tensor("b1", [1, DFF], F32, kind="ExternalInput")
    w2_d = nc.dram_tensor("w2", [DFF, D], F32, kind="ExternalInput")
    b2_d = nc.dram_tensor("b2", [1, D], F32, kind="ExternalInput")
    lnw_d = nc.dram_tensor("lnw", [1, D], F32, kind="ExternalInput")
    lnb_d = nc.dram_tensor("lnb", [1, D], F32, kind="ExternalInput")
    xhalo_d = nc.dram_tensor("xhalo", [3, D], F32, kind="ExternalInput")
    hmask_d = nc.dram_tensor("hmask", [1, 1], F32, kind="ExternalInput")
    omask_d = nc.dram_tensor("omask", [1, 1], F32, kind="ExternalInput")
    y_d = nc.dram_tensor("y", [S, D], F32, kind="ExternalOutput")
    X = [p.sbuf(f"x{t}", [128, D], F32) for t in range(NT)]
    for t in range(NT):
        p.dma(X[t][:], x_d[128 * t : 128 * (t + 1), :])
    emit_l0_attn(
        p, X, inw_d, convw_d, convb_d, dtb_d, alog_d, dskip_d, rmsw_d, outw_d,
        w1_d, b1_d, w2_d, b2_d, lnw_d, lnb_d, xhalo_d, hmask_d, omask_d
    )
    for t in range(NT):
        p.dma(y_d[128 * t : 128 * (t + 1), :], X[t][:])
    return p.finish()



SEQ = 4096
NJC = SEQ // 128


def emit_l1_attn(p, X, wq_d, wk_d, wv_d, bq_d, bk_d, bv_d, thr_d, msel_d, tag="a1"):
    xn_dr = p.dram(f"{tag}_xn", [S, D], BF16)
    q_dr = p.dram(f"{tag}_q", [S, D], BF16)
    kv_in = p.dram(f"{tag}_kvi", [S, 2 * D], BF16)
    kv_dr = p.dram(f"{tag}_kv", [SEQ, 2 * D], BF16)
    e_dr = p.dram(f"{tag}_e", [SEQ, S], BF16)
    st_in = p.dram(f"{tag}_sti", [2 * 128, NJC], F32)
    st_out = p.dram(f"{tag}_sto", [4 * 128, NJC], F32)
    v2_dr = p.dram(f"{tag}_v2", [SEQ, D], BF16)
    PAIRS = [[0, 1], [2, 3], [4, 5], [6, 7]]

    with p.scope() as outer:
        scr = {
            "sq": p.sbuf("sq", [128, D], F32, scope=outer),
            "st": [p.sbuf(f"st{i}", [128, 8], F32, scope=outer) for i in range(4)],
        }
        with p.scope() as sc:
            xn = [p.sbuf(f"xn{i}", [128, D], F32, scope=sc) for i in range(2)]
            xnb = [p.sbuf(f"xnb{i}", [128, D], BF16, scope=sc) for i in range(2)]
            for t in range(NT):
                b = t % 2
                emit_plain_ln(p, X[t][:], xn[b][:], scr, slot=t)
                p.tt(X[t][:], X[t][:], xn[b][:], ALU.add)
                p.copy(xnb[b][:], xn[b][:])
                p.dma(xn_dr[128 * t : 128 * (t + 1), :], xnb[b][:])

        p.barrier(1)
        with p.scope() as sc:
            xnf = p.sbuf("xnf", [128, DCH * 512], BF16, scope=sc)
            wt = [p.sbuf(f"wt{i}", [128, 512], BF16, scope=sc) for i in range(2)]
            bbc = p.sbuf("bbc", [128, 3 * D], F32, scope=sc)
            qstg = p.sbuf("qstg", [128, 4 * D], BF16, scope=sc)
            kvstg = p.sbuf("kvstg", [128, 4 * 2 * D], BF16, scope=sc)
            ps = [p.psum(f"pqkv{i}", [128, 512], scope=sc) for i in range(4)]
            p.dma(bbc[:, 0:D], bq_d[0:1, :].partition_broadcast(128))
            p.dma(bbc[:, D : 2 * D], bk_d[0:1, :].partition_broadcast(128))
            p.dma(bbc[:, 2 * D : 3 * D], bv_d[0:1, :].partition_broadcast(128))
            WL = [(wi, w_d, jh) for wi, w_d in enumerate((wq_d, wk_d, wv_d)) for jh in range(2)]

            def ldqkv(i, k, b):
                wi, w_d, jh = WL[i]
                p.dma(wt[b][:], w_d[128 * k : 128 * (k + 1), 512 * jh : 512 * jh + 512], eng="gpsimd")

            for g in range(4):
                g0 = 512 * g
                with p.par():
                    for k in range(DCH):
                        p.dmat(xnf[:, 512 * k : 512 * (k + 1)], xn_dr[g0 : g0 + 512, 128 * k : 128 * (k + 1)])
                    ldqkv(0, 0, 0)
                it = 0
                for i in range(6):
                    wi, w_d, jh = WL[i]
                    j0 = 512 * jh
                    for k in range(DCH):
                        b = it % 2
                        with p.par():
                            if not (i == 5 and k == DCH - 1):
                                ni, nk = (i, k + 1) if k < DCH - 1 else (i + 1, 0)
                                ldqkv(ni, nk, 1 - b)
                            for tt in range(4):
                                lhsT = xnf[:, 512 * k + 128 * tt : 512 * k + 128 * (tt + 1)]
                                p.mm(ps[tt][:], lhsT, wt[b][:], start=(k == 0), stop=(k == DCH - 1))
                        it += 1
                    with p.par():
                        for tt in range(4):
                            dst = (
                                qstg[:, D * tt + j0 : D * tt + j0 + 512]
                                if wi == 0
                                else kvstg[:, 2 * D * tt + (wi - 1) * D + j0 : 2 * D * tt + (wi - 1) * D + j0 + 512]
                            )
                            p.tt(dst, ps[tt][:], bbc[:, wi * D + j0 : wi * D + j0 + 512], ALU.add)
                with p.par():
                    for tt in range(4):
                        r0 = g0 + 128 * tt
                        p.dma(q_dr[r0 : r0 + 128, :], qstg[:, D * tt : D * (tt + 1)])
                        p.dma(kv_in[r0 : r0 + 128, :], kvstg[:, 2 * D * tt : 2 * D * (tt + 1)])
        p.collective("AllGather", ALU.bypass, PAIRS, [kv_in.ap()], [kv_dr.ap()])

        p.barrier(2)
        iotaT = p.sbuf("iotaT", [128, S], F32, scope=outer)
        p.iota(iotaT[:], [[1, S]], base=0, channel_multiplier=-1)
        thrbc = p.sbuf("thrbc", [128, NJC], F32, scope=outer)
        p.dma(thrbc[:], thr_d[0:1, :].partition_broadcast(128))
        with p.scope() as sc:
            qf = p.sbuf("qf", [128, DCH * S], BF16, scope=sc)
            kf = p.sbuf("kf", [128, DCH * SEQ], BF16, scope=sc)
            for dc in range(DCH):
                p.dmat(qf[:, S * dc : S * (dc + 1)], q_dr[:, 128 * dc : 128 * (dc + 1)])
                p.dmat(kf[:, SEQ * dc : SEQ * (dc + 1)], kv_dr[:, 128 * dc : 128 * (dc + 1)])
            sm = p.sbuf("sm", [128, S], F32, scope=sc)
            eb = p.sbuf("eb", [128, S], BF16, scope=sc)
            maskA = p.sbuf("maskA", [128, S], F32, scope=sc)
            mcol = p.sbuf("mcol", [128, 1], F32, scope=sc)
            nmcol = p.sbuf("nmcol", [128, 1], F32, scope=sc)
            dcol = p.sbuf("dcol", [128, 1], F32, scope=sc)
            pss = [p.psum(f"pss{i}", [128, 512], scope=sc) for i in range(4)]
            for jc in range(NJC):
                for it in range(4):
                    i0 = 512 * it
                    for dc in range(DCH):
                        lhsT = kf[:, SEQ * dc + 128 * jc : SEQ * dc + 128 * (jc + 1)]
                        rhs = qf[:, S * dc + i0 : S * dc + i0 + 512]
                        p.mm(pss[it][:], lhsT, rhs, start=(dc == 0), stop=(dc == DCH - 1))
                p.ts(maskA[:], iotaT[:, 0:S], thrbc[:, jc : jc + 1], -1e30, ALU.is_lt, ALU.mult)
                for it in range(4):
                    p.tt(sm[:, 512 * it : 512 * (it + 1)], pss[it][:], maskA[:, 512 * it : 512 * (it + 1)], ALU.add)
                p.red(mcol[:], sm[:], ALU.max)
                p.ts(nmcol[:], mcol[:], -1.0, None, ALU.mult)
                p.memset(dcol[:], 0.0)
                p.act(eb[:], sm[:], AF.Exp, bias=nmcol[:], accum_out=dcol[:])
                with p.par():
                    p.dma(e_dr[128 * jc : 128 * (jc + 1), :], eb[:])
                    p.dma(st_in[0:128, jc : jc + 1], mcol[:], allow_slow_non_contiguous=True, eng="gpsimd")
                    p.dma(st_in[128:256, jc : jc + 1], dcol[:], allow_slow_non_contiguous=True, eng="gpsimd")
        p.collective("AllGather", ALU.bypass, PAIRS, [st_in.ap()], [st_out.ap()])

        p.barrier(3)
        fmat = p.sbuf("fmat", [128, NJC], F32, scope=outer)
        with p.scope() as sc:
            me = p.sbuf("cme", [128, NJC], F32, scope=sc)
            de = p.sbuf("cde", [128, NJC], F32, scope=sc)
            mo = p.sbuf("cmo", [128, NJC], F32, scope=sc)
            do_ = p.sbuf("cdo", [128, NJC], F32, scope=sc)
            mm_ = p.sbuf("cmm", [128, NJC], F32, scope=sc)
            tmp = p.sbuf("ctmp", [128, NJC], F32, scope=sc)
            eme = p.sbuf("ceme", [128, NJC], F32, scope=sc)
            emo = p.sbuf("cemo", [128, NJC], F32, scope=sc)
            mselc = p.sbuf("mselc", [128, 1], F32, scope=sc)
            p.dma(me[:], st_out[0:128, :])
            p.dma(de[:], st_out[128:256, :])
            p.dma(mo[:], st_out[256:384, :])
            p.dma(do_[:], st_out[384:512, :])
            p.dma(mselc[:], msel_d[0:1, 0:1].partition_broadcast(128))
            p.barrier(5)
            p.tt(mm_[:], me[:], mo[:], ALU.max)
            p.tt(tmp[:], me[:], mm_[:], ALU.subtract)
            p.act(eme[:], tmp[:], AF.Exp)
            p.tt(tmp[:], mo[:], mm_[:], ALU.subtract)
            p.act(emo[:], tmp[:], AF.Exp)
            p.tt(de[:], de[:], eme[:], ALU.mult)
            p.tt(do_[:], do_[:], emo[:], ALU.mult)
            p.tt(de[:], de[:], do_[:], ALU.add)
            p.ts(tmp[:], eme[:], mselc[:], None, ALU.mult)
            p.ts(eme[:], emo[:], mselc[:], None, ALU.mult)
            p.tt(emo[:], emo[:], eme[:], ALU.subtract)
            p.tt(tmp[:], tmp[:], emo[:], ALU.add)
            p.ts(de[:], de[:], 32.0, None, ALU.mult)
            p.recip(de[:], de[:])
            p.tt(fmat[:], tmp[:], de[:], ALU.mult)
            import os as _os
            if _os.environ.get("DBGA"):
                fdbg = p.dram(f"{tag}_fdbg", [128, 4 * NJC], F32)
                DBG["a1_fdbg"] = fdbg
                p.dma(fdbg[:, 0:NJC], fmat[:])
                p.dma(fdbg[:, NJC : 2 * NJC], me[:])
                p.dma(fdbg[:, 2 * NJC : 3 * NJC], de[:])
                p.dma(fdbg[:, 3 * NJC : 4 * NJC], mselc[:].broadcast_to([128, NJC]) if False else eme[:])
        p.barrier(6)
        with p.scope() as sc:
            vt = p.sbuf("vt", [128, D], BF16, scope=sc)
            for jc in range(NJC):
                p.barrier(7)
                p.dma(vt[:], kv_dr[128 * jc : 128 * (jc + 1), D : 2 * D])
                p.barrier(7)
                p.ts(vt[:], vt[:], fmat[:, jc : jc + 1], None, ALU.mult)
                p.barrier(7)
                p.dma(v2_dr[128 * jc : 128 * (jc + 1), :], vt[:])

        p.barrier(4)
        with p.scope() as sc:
            ers = [p.sbuf(f"ers{i}", [128, 512], BF16, scope=sc) for i in range(2)]
            vt = [p.sbuf(f"vt2{i}", [128, D], BF16, scope=sc) for i in range(2)]
            psr = [p.psum(f"psr{i}", [128, 512], scope=sc) for i in range(8)]

            def ldpv(ig, jc, b):
                p.dma(ers[b][:], e_dr[128 * jc : 128 * (jc + 1), 512 * ig : 512 * ig + 512])
                p.dma(vt[b][:], v2_dr[128 * jc : 128 * (jc + 1), :], eng="gpsimd")

            with p.par():
                ldpv(0, 0, 0)
            for ig in range(4):
                for jc in range(NJC):
                    b = jc % 2
                    with p.par():
                        if not (ig == 3 and jc == NJC - 1):
                            nig, njc = (ig, jc + 1) if jc < NJC - 1 else (ig + 1, 0)
                            ldpv(nig, njc, 1 - b)
                        for isub in range(4):
                            lhsT = ers[b][:, 128 * isub : 128 * (isub + 1)]
                            for vfh in range(2):
                                p.mm(
                                    psr[2 * isub + vfh][:],
                                    lhsT,
                                    vt[b][:, 512 * vfh : 512 * (vfh + 1)],
                                    start=(jc == 0),
                                    stop=(jc == NJC - 1),
                                )
                with p.par():
                    for isub in range(4):
                        t = 4 * ig + isub
                        for vfh in range(2):
                            h0 = 512 * vfh
                            p.tt(X[t][:, h0 : h0 + 512], X[t][:, h0 : h0 + 512], psr[2 * isub + vfh][:], ALU.add)


def _l1attn_host_inputs():
    outs = []
    for c in range(8):
        base = 2048 * (c % 2)
        thr = np.array([[128 * jc - base for jc in range(NJC)]], np.float32)
        msel = np.array([[1.0 if c % 2 == 0 else 0.0]], np.float32)
        outs.append((thr, msel))
    return outs


def build_l1_attn_launch():
    p = Prog("l1attn")
    nc = p.nc
    x_d = nc.dram_tensor("x", [S, D], F32, kind="ExternalInput")
    wq_d = nc.dram_tensor("wq", [D, D], F32, kind="ExternalInput")
    wk_d = nc.dram_tensor("wk", [D, D], F32, kind="ExternalInput")
    wv_d = nc.dram_tensor("wv", [D, D], F32, kind="ExternalInput")
    bq_d = nc.dram_tensor("bq", [1, D], F32, kind="ExternalInput")
    bk_d = nc.dram_tensor("bk", [1, D], F32, kind="ExternalInput")
    bv_d = nc.dram_tensor("bv", [1, D], F32, kind="ExternalInput")
    thr_d = nc.dram_tensor("thr", [1, NJC], F32, kind="ExternalInput")
    msel_d = nc.dram_tensor("msel", [1, 1], F32, kind="ExternalInput")
    y_d = nc.dram_tensor("y", [S, D], F32, kind="ExternalOutput")
    X = [p.sbuf(f"x{t}", [128, D], F32) for t in range(NT)]
    for t in range(NT):
        p.dma(X[t][:], x_d[128 * t : 128 * (t + 1), :])
    emit_l1_attn(p, X, wq_d, wk_d, wv_d, bq_d, bk_d, bv_d, thr_d, msel_d)
    for t in range(NT):
        p.dma(y_d[128 * t : 128 * (t + 1), :], X[t][:])
    return p.finish()


def _l0ffn_host_inputs(wg, wu, wd):
    outs = []
    stacked = np.concatenate([wg, wu], axis=0)
    for c in range(8):
        wisl = stacked[704 * c : 704 * (c + 1), :]
        wdsl = wd[128 * c : 128 * (c + 1), :]
        outs.append((np.ascontiguousarray(wisl), np.ascontiguousarray(wdsl)))
    return outs


def build_l0_ffn_launch():
    p = Prog("l0ffn")
    nc = p.nc
    x_d = nc.dram_tensor("x", [S, D], F32, kind="ExternalInput")
    wisl_d = nc.dram_tensor("wisl", [704, D], F32, kind="ExternalInput")
    wdsl_d = nc.dram_tensor("wdsl", [128, I_FF], F32, kind="ExternalInput")
    clipx_d = nc.dram_tensor("clipx", [1, 1], F32, kind="ExternalInput")
    clipd_d = nc.dram_tensor("clipd", [1, 1], F32, kind="ExternalInput")
    y_d = nc.dram_tensor("y", [S, D], F32, kind="ExternalOutput")
    X = [p.sbuf(f"x{t}", [128, D], F32) for t in range(NT)]
    for t in range(NT):
        p.dma(X[t][:], x_d[128 * t : 128 * (t + 1), :])
    emit_l0_ffn(p, X, wisl_d, wdsl_d, clipx_d, clipd_d)
    for t in range(NT):
        p.dma(y_d[128 * t : 128 * (t + 1), :], X[t][:])
    return p.finish()


def _l1ffn_host_inputs(wg, wu, wd):
    outs = []
    for c in range(8):
        wsl = np.concatenate(
            [wg[:, 128 * c : 128 * (c + 1)], wu[:, 128 * c : 128 * (c + 1)]], axis=0
        )
        wdsl = wd[128 * c : 128 * (c + 1), :]
        outs.append((np.ascontiguousarray(wsl), np.ascontiguousarray(wdsl)))
    return outs




def build_l1_ffn_launch(debug=False):
    p = Prog("l1ffn")
    nc = p.nc
    x_d = nc.dram_tensor("x", [S, D], F32, kind="ExternalInput")
    wsl_d = nc.dram_tensor("wsl", [2 * I_FF, 128], F32, kind="ExternalInput")
    wdsl_d = nc.dram_tensor("wdsl", [128, I_FF], F32, kind="ExternalInput")
    y_d = nc.dram_tensor("y", [S, D], F32, kind="ExternalOutput")

    X = [p.sbuf(f"x{t}", [128, D], F32) for t in range(NT)]
    for t in range(NT):
        p.dma(X[t][:], x_d[128 * t : 128 * (t + 1), :])
    emit_l1_ffn(p, X, wsl_d, wdsl_d)
    for t in range(NT):
        p.dma(y_d[128 * t : 128 * (t + 1), :], X[t][:])
    if debug:
        for nm, h in DBG.items():
            out = nc.dram_tensor("dbg_" + nm, list(h.ap().shape), h.ap().dtype, kind="ExternalOutput")
            p.dma(out.ap(), h.ap())
    return p.finish()


DBG = {}


def build_fused():
    p = Prog("mk")
    nc = p.nc
    x_d = nc.dram_tensor("x", [S, D], F32, kind="ExternalInput")
    inw_d = nc.dram_tensor("inw", [D, 4384], F32, kind="ExternalInput")
    convw_d = nc.dram_tensor("convw", [CONVD, 4], F32, kind="ExternalInput")
    convb_d = nc.dram_tensor("convb", [1, CONVD], F32, kind="ExternalInput")
    dtb_d = nc.dram_tensor("dtb", [1, NH], F32, kind="ExternalInput")
    alog_d = nc.dram_tensor("alog", [1, NH], F32, kind="ExternalInput")
    dskip_d = nc.dram_tensor("dskip", [1, NH], F32, kind="ExternalInput")
    rmsw_d = nc.dram_tensor("rmsw", [1, DIN], F32, kind="ExternalInput")
    outw_d = nc.dram_tensor("outw", [DIN, D], F32, kind="ExternalInput")
    w1_d = nc.dram_tensor("w1", [D, DFF], F32, kind="ExternalInput")
    b1_d = nc.dram_tensor("b1", [1, DFF], F32, kind="ExternalInput")
    w2_d = nc.dram_tensor("w2", [DFF, D], F32, kind="ExternalInput")
    b2_d = nc.dram_tensor("b2", [1, D], F32, kind="ExternalInput")
    lnw_d = nc.dram_tensor("lnw", [1, D], F32, kind="ExternalInput")
    lnb_d = nc.dram_tensor("lnb", [1, D], F32, kind="ExternalInput")
    xhalo_d = nc.dram_tensor("xhalo", [3, D], F32, kind="ExternalInput")
    hmask_d = nc.dram_tensor("hmask", [1, 1], F32, kind="ExternalInput")
    omask_d = nc.dram_tensor("omask", [1, 1], F32, kind="ExternalInput")
    wisl_d = nc.dram_tensor("wisl", [704, D], F32, kind="ExternalInput")
    wdsl0_d = nc.dram_tensor("wdsl0", [128, I_FF], F32, kind="ExternalInput")
    clipx_d = nc.dram_tensor("clipx", [1, 1], F32, kind="ExternalInput")
    clipd_d = nc.dram_tensor("clipd", [1, 1], F32, kind="ExternalInput")
    wq_d = nc.dram_tensor("wq", [D, D], F32, kind="ExternalInput")
    wk_d = nc.dram_tensor("wk", [D, D], F32, kind="ExternalInput")
    wv_d = nc.dram_tensor("wv", [D, D], F32, kind="ExternalInput")
    bq_d = nc.dram_tensor("bq", [1, D], F32, kind="ExternalInput")
    bk_d = nc.dram_tensor("bk", [1, D], F32, kind="ExternalInput")
    bv_d = nc.dram_tensor("bv", [1, D], F32, kind="ExternalInput")
    thr_d = nc.dram_tensor("thr", [1, NJC], F32, kind="ExternalInput")
    msel_d = nc.dram_tensor("msel", [1, 1], F32, kind="ExternalInput")
    wsl1_d = nc.dram_tensor("wsl1", [2 * I_FF, 128], F32, kind="ExternalInput")
    wdsl1_d = nc.dram_tensor("wdsl1", [128, I_FF], F32, kind="ExternalInput")
    y_d = nc.dram_tensor("y", [S, D], F32, kind="ExternalOutput")

    X = [p.sbuf(f"x{t}", [128, D], F32) for t in range(NT)]
    for t in range(NT):
        p.dma(X[t][:], x_d[128 * t : 128 * (t + 1), :])
    emit_l0_attn(
        p, X, inw_d, convw_d, convb_d, dtb_d, alog_d, dskip_d, rmsw_d, outw_d,
        w1_d, b1_d, w2_d, b2_d, lnw_d, lnb_d, xhalo_d, hmask_d, omask_d
    )
    emit_l0_ffn(p, X, wisl_d, wdsl0_d, clipx_d, clipd_d)
    emit_l1_attn(p, X, wq_d, wk_d, wv_d, bq_d, bk_d, bv_d, thr_d, msel_d)
    emit_l1_ffn(p, X, wsl1_d, wdsl1_d)
    for t in range(NT):
        p.dma(y_d[128 * t : 128 * (t + 1), :], X[t][:])
    return p.finish()


def _fused_in_maps(inputs):
    x = np.asarray(inputs["L0_attn_x"], np.float32)
    xs = _shard_x(x)
    hi0 = _l0attn_host_inputs(x)
    wsl0 = _l0ffn_host_inputs(
        np.asarray(inputs["L0_ffn_w_gate"], np.float32),
        np.asarray(inputs["L0_ffn_w_up"], np.float32),
        np.asarray(inputs["L0_ffn_w_down"], np.float32),
    )
    hi1 = _l1attn_host_inputs()
    wsl1 = _l1ffn_host_inputs(
        np.asarray(inputs["L1_ffn_w_gate"], np.float32),
        np.asarray(inputs["L1_ffn_w_up"], np.float32),
        np.asarray(inputs["L1_ffn_w_down"], np.float32),
    )
    base = {
        "inw": np.asarray(inputs["L0_attn_in_proj_w"], np.float32),
        "convw": np.asarray(inputs["L0_attn_conv_w"], np.float32),
        "convb": np.asarray(inputs["L0_attn_conv_b"], np.float32).reshape(1, -1),
        "dtb": np.asarray(inputs["L0_attn_dt_bias"], np.float32).reshape(1, -1),
        "alog": np.asarray(inputs["L0_attn_A_log"], np.float32).reshape(1, -1),
        "dskip": np.asarray(inputs["L0_attn_D_skip"], np.float32).reshape(1, -1),
        "rmsw": np.asarray(inputs["L0_attn_rms_w"], np.float32).reshape(1, -1),
        "outw": np.asarray(inputs["L0_attn_out_proj_w"], np.float32),
        "w1": np.asarray(inputs["L0_attn_mlp_w1"], np.float32),
        "b1": np.asarray(inputs["L0_attn_mlp_b1"], np.float32).reshape(1, -1),
        "w2": np.asarray(inputs["L0_attn_mlp_w2"], np.float32),
        "b2": np.asarray(inputs["L0_attn_mlp_b2"], np.float32).reshape(1, -1),
        "lnw": np.asarray(inputs["L0_attn_ln_w"], np.float32).reshape(1, -1),
        "lnb": np.asarray(inputs["L0_attn_ln_b"], np.float32).reshape(1, -1),
        "clipx": np.asarray(inputs["L0_ffn_clip_x"], np.float32).reshape(1, 1),
        "clipd": np.asarray(inputs["L0_ffn_clip_down"], np.float32).reshape(1, 1),
        "wq": np.asarray(inputs["L1_attn_Wq"], np.float32),
        "wk": np.asarray(inputs["L1_attn_Wk"], np.float32),
        "wv": np.asarray(inputs["L1_attn_Wv"], np.float32),
        "bq": np.asarray(inputs["L1_attn_bq"], np.float32).reshape(1, -1),
        "bk": np.asarray(inputs["L1_attn_bk"], np.float32).reshape(1, -1),
        "bv": np.asarray(inputs["L1_attn_bv"], np.float32).reshape(1, -1),
    }
    in_maps = []
    for c in range(8):
        m = dict(base)
        m["x"] = xs[c]
        m["xhalo"], m["hmask"], m["omask"] = hi0[c]
        m["wisl"], m["wdsl0"] = wsl0[c]
        m["thr"], m["msel"] = hi1[c]
        m["wsl1"], m["wdsl1"] = wsl1[c]
        in_maps.append(m)
    return in_maps


def kernel(**inputs):
    nc = build_fused()
    in_maps = _fused_in_maps(inputs)
    res = run_bass_kernel_spmd(nc, in_maps, core_ids=list(range(8)))
    out = np.zeros((4, 4096, 1024), np.float32)
    for c in range(8):
        out[c // 2, 2048 * (c % 2) : 2048 * (c % 2 + 1)] = np.asarray(res.results[c]["y"])
    return out


def _shard_x(x):
    return [np.ascontiguousarray(x[c // 2, 2048 * (c % 2) : 2048 * (c % 2 + 1)]) for c in range(8)]


def kernel_phase_a(**inputs):
    x = np.asarray(inputs["L0_attn_x"], np.float32)
    xs = _shard_x(x)

    nc = build_l0_attn_launch()
    hi = _l0attn_host_inputs(x)
    base = {
        "inw": np.asarray(inputs["L0_attn_in_proj_w"], np.float32),
        "convw": np.asarray(inputs["L0_attn_conv_w"], np.float32),
        "convb": np.asarray(inputs["L0_attn_conv_b"], np.float32).reshape(1, -1),
        "dtb": np.asarray(inputs["L0_attn_dt_bias"], np.float32).reshape(1, -1),
        "alog": np.asarray(inputs["L0_attn_A_log"], np.float32).reshape(1, -1),
        "dskip": np.asarray(inputs["L0_attn_D_skip"], np.float32).reshape(1, -1),
        "rmsw": np.asarray(inputs["L0_attn_rms_w"], np.float32).reshape(1, -1),
        "outw": np.asarray(inputs["L0_attn_out_proj_w"], np.float32),
        "w1": np.asarray(inputs["L0_attn_mlp_w1"], np.float32),
        "b1": np.asarray(inputs["L0_attn_mlp_b1"], np.float32).reshape(1, -1),
        "w2": np.asarray(inputs["L0_attn_mlp_w2"], np.float32),
        "b2": np.asarray(inputs["L0_attn_mlp_b2"], np.float32).reshape(1, -1),
        "lnw": np.asarray(inputs["L0_attn_ln_w"], np.float32).reshape(1, -1),
        "lnb": np.asarray(inputs["L0_attn_ln_b"], np.float32).reshape(1, -1),
    }
    in_maps = []
    for c in range(8):
        m = dict(base)
        m["x"] = xs[c]
        m["xhalo"], m["hmask"], m["omask"] = hi[c]
        in_maps.append(m)
    res = run_bass_kernel_spmd(nc, in_maps, core_ids=list(range(8)))
    xs = [np.asarray(res.results[c]["y"]) for c in range(8)]

    nc = build_l0_ffn_launch()
    wsl = _l0ffn_host_inputs(
        np.asarray(inputs["L0_ffn_w_gate"], np.float32),
        np.asarray(inputs["L0_ffn_w_up"], np.float32),
        np.asarray(inputs["L0_ffn_w_down"], np.float32),
    )
    cx = np.asarray(inputs["L0_ffn_clip_x"], np.float32).reshape(1, 1)
    cd = np.asarray(inputs["L0_ffn_clip_down"], np.float32).reshape(1, 1)
    in_maps = [
        {"x": xs[c], "wisl": wsl[c][0], "wdsl": wsl[c][1], "clipx": cx, "clipd": cd}
        for c in range(8)
    ]
    res = run_bass_kernel_spmd(nc, in_maps, core_ids=list(range(8)))
    xs = [np.asarray(res.results[c]["y"]) for c in range(8)]

    nc = build_l1_attn_launch()
    hi = _l1attn_host_inputs()
    in_maps = [
        {
            "x": xs[c],
            "wq": np.asarray(inputs["L1_attn_Wq"], np.float32),
            "wk": np.asarray(inputs["L1_attn_Wk"], np.float32),
            "wv": np.asarray(inputs["L1_attn_Wv"], np.float32),
            "bq": np.asarray(inputs["L1_attn_bq"], np.float32).reshape(1, -1),
            "bk": np.asarray(inputs["L1_attn_bk"], np.float32).reshape(1, -1),
            "bv": np.asarray(inputs["L1_attn_bv"], np.float32).reshape(1, -1),
            "thr": hi[c][0],
            "msel": hi[c][1],
        }
        for c in range(8)
    ]
    res = run_bass_kernel_spmd(nc, in_maps, core_ids=list(range(8)))
    xs = [np.asarray(res.results[c]["y"]) for c in range(8)]

    nc = build_l1_ffn_launch()
    wsl = _l1ffn_host_inputs(
        np.asarray(inputs["L1_ffn_w_gate"], np.float32),
        np.asarray(inputs["L1_ffn_w_up"], np.float32),
        np.asarray(inputs["L1_ffn_w_down"], np.float32),
    )
    in_maps = [{"x": xs[c], "wsl": wsl[c][0], "wdsl": wsl[c][1]} for c in range(8)]
    res = run_bass_kernel_spmd(nc, in_maps, core_ids=list(range(8)))
    xs = [np.asarray(res.results[c]["y"]) for c in range(8)]

    out = np.zeros((4, 4096, 1024), np.float32)
    for c in range(8):
        out[c // 2, 2048 * (c % 2) : 2048 * (c % 2 + 1)] = xs[c]
    return out
